# version 20
# baseline (speedup 1.0000x reference)
DEBUG_TAPS = False

import numpy as np
import ml_dtypes
import concourse.bass as bass
import concourse.mybir as mybir
from concourse.bass_utils import run_bass_kernel_spmd
from concourse.alu_op_type import AluOpType

F32 = mybir.dt.float32
BF16 = mybir.dt.bfloat16
I32 = mybir.dt.int32
U32 = mybir.dt.uint32
AF = mybir.ActivationFunctionType
AX = mybir.AxisListType.X
Op = AluOpType

T = 8192
D = 1024
CL = 256
HLOC = 4
HCOL = 256
II = 2816
QB = 512
NQB = 16
NKT = 64
KB = 1024
EXP_SCALE = 0.125
EXP_BIAS = -3.0
LN_EPS = 1e-5

ENGS = ('tensor', 'vector', 'scalar', 'gpsimd', 'sync')


class P:

    def __init__(self, nc, active, eng, sems):
        self.nc = nc
        self.active = active
        self.eng = eng
        self.sems = sems
        self.cnt = dict.fromkeys("tvsgdce1234567", 0)
        self.wm = {e: dict.fromkeys("tvsgdce1234567", 0) for e in ENGS}

    def w(self, tgt, key, count):
        if not count or count <= self.wm[tgt][key]:
            return
        self.wm[tgt][key] = count
        if self.active == tgt:
            self.eng.wait_ge(self.sems[key], count)

    def _do(self, key, inc, fn, act):
        self.cnt[key] += inc
        if self.active == act:
            fn().then_inc(self.sems[key], inc)
        return self.cnt[key]

    def t(self, fn):
        return self._do('t', 1, fn, 'tensor')

    def v(self, fn):
        return self._do('v', 1, fn, 'vector')

    def s(self, fn):
        return self._do('s', 1, fn, 'scalar')

    def g(self, fn):
        return self._do('g', 16, fn, 'gpsimd')

    def d(self, fn):
        return self._do('d', 16, fn, 'sync')

    def e(self, fn):
        return self._do('e', 16, fn, 'scalar')

    def c(self, fn, key='c'):
        self.cnt[key] += 1
        if self.active == 'gpsimd':
            fn().then_inc(self.sems[key])
        return self.cnt[key]


def build_nc():
    nc = bass.Bass("TRN2", target_bir_lowering=False)

    x_d = nc.dram_tensor("x", [T, D], F32, kind="ExternalInput").ap()
    xq_d = nc.dram_tensor("xquarter", [2048, D], F32, kind="ExternalInput").ap()
    wdkv_d = nc.dram_tensor("wdkv", [D, CL], F32, kind="ExternalInput").ap()
    wuq_d = nc.dram_tensor("wuq", [D, HCOL], F32, kind="ExternalInput").ap()
    wuk_d = nc.dram_tensor("wuk", [CL, HCOL], F32, kind="ExternalInput").ap()
    wuv_d = nc.dram_tensor("wuv", [CL, HCOL], F32, kind="ExternalInput").ap()
    wo_d = nc.dram_tensor("wo", [D, D], F32, kind="ExternalInput").ap()
    wgsh_d = nc.dram_tensor("wgsh", [II, 128], F32, kind="ExternalInput").ap()
    wush_d = nc.dram_tensor("wush", [II, 128], F32, kind="ExternalInput").ap()
    wdsh_d = nc.dram_tensor("wdsh", [D, 384], F32, kind="ExternalInput").ap()
    identb_d = nc.dram_tensor("identb", [128, 128], BF16, kind="ExternalInput").ap()
    masks_d = nc.dram_tensor("masks", [128, 2048], BF16, kind="ExternalInput").ap()
    cvals_d = nc.dram_tensor("cvals", [128, 4], F32, kind="ExternalInput").ap()
    gidx_d = nc.dram_tensor("gidx", [128, 64], U32, kind="ExternalInput").ap()
    y_d = nc.dram_tensor("y", [2048, D], F32, kind="ExternalOutput").ap()
    y2_d = nc.dram_tensor("y2", [2048, D], F32, kind="ExternalOutput").ap() if DEBUG_TAPS else None
    y3_d = nc.dram_tensor("y3", [2048, D], BF16, kind="ExternalOutput").ap() if DEBUG_TAPS else None
    y4_d = nc.dram_tensor("y4", [2048, II], BF16, kind="ExternalOutput").ap() if DEBUG_TAPS else None
    wgin_d = nc.dram_tensor("wgin", [128, II], BF16).ap()
    wuin_d = nc.dram_tensor("wuin", [128, II], BF16).ap()
    wdin_d = nc.dram_tensor("wdin", [384, D], BF16).ap()
    wgT_d = nc.dram_tensor("wgT", [D, II], BF16, addr_space="Shared").ap()
    wuT_d = nc.dram_tensor("wuT", [D, II], BF16, addr_space="Shared").ap()
    wdT_d = nc.dram_tensor("wdT", [3072, D], BF16, addr_space="Shared").ap()
    ccin_d = nc.dram_tensor("ccin", [4 * HCOL, 2048], BF16).ap()
    ccout_d = nc.dram_tensor("ccout", [4096, 2048], BF16).ap()
    ccout_rows = ccout_d.rearrange("r (c n) -> (r c) n", n=QB)

    def sb(name, shape, dt):
        return nc.alloc_sbuf_tensor("s_" + name, shape, dt).ap()

    xres = sb("xres", [128, 16 * KB], F32)
    xres_v = xres.rearrange("p (tt k) -> p tt k", k=KB)
    identb = sb("identb", [128, 128], BF16)
    cvals = sb("cvals", [128, 4], F32)
    nc.const_aps.register(F32, EXP_SCALE, cvals[:, 0:1])
    nc.const_aps.register(F32, EXP_BIAS, cvals[:, 1:2])
    nc.const_aps.register(F32, LN_EPS, cvals[:, 2:3])
    gidx = sb("gidx", [128, 64], U32)
    stat = sb("stat", [128, 32], F32)
    sqscr = sb("sqscr", [128, KB], F32)
    onesc = sb("onesc", [1, 64], F32)

    from concourse.bass_primitives import MemorySpace
    AB = (nc.bytes_allocated(MemorySpace.SBUF) + 511) & ~511
    _names = [0]

    def at(off, shape, dt):
        _names[0] += 1
        return nc.alloc_sbuf_tensor_at(f"a_{_names[0]}", shape, dt, offset=AB + off).ap()

    qfm = [at(16384 * u, [128, T], BF16) for u in range(2)]
    kfm = [at(32768 + 16384 * u, [128, T], BF16) for u in range(2)]
    vaug = at(65536, [128, NKT * 260], BF16)
    wdkv = at(98816, [128, 8 * CL], BF16)
    wuq = at(102912, [128, 8 * HCOL], BF16)
    wuk = at(107008, [128, 2 * HCOL], BF16)
    wuv = at(108032, [128, 2 * HCOL], BF16)
    ebuf = at(109056, [128, 4 * KB], BF16)
    masks = at(98816, [128, 2048], BF16)
    rl4 = at(117248, [1, 4 * QB], F32)
    rbuf4 = at(125440, [64, 4 * QB], F32)
    xtile = at(121344, [128, 2 * KB], F32)
    stage = at(121344, [128, KB], F32)
    stageW = at(121344, [128, 2 * KB], F32)
    wcast = at(129536, [128, KB], BF16)
    wcastW = at(129536, [128, 2 * KB], BF16)
    xntm = at(129536, [128, 2 * KB], BF16)
    ttmp = at(133632, [128, KB], BF16)
    ttmpW = at(133632, [128, 2 * KB], BF16)
    xnfm = at(133632, [128, 8 * 256], BF16)
    ckv = at(137728, [128, 2 * 256], BF16)
    ctxn = at(138752, [128, 2 * QB], BF16)
    ARENA_ATTN_END = 140800

    wo_s = at(0, [128, 8 * KB], BF16)
    ctxg = [at(16384 + 8192 * i, [128, 8 * QB], BF16) for i in range(2)]

    hbuf = at(0, [128, 4 * KB], F32)
    wdnR = [at(16384 + 2048 * i, [128, KB], BF16) for i in range(6)]
    hqtm = at(36864, [128, II], BF16)
    wgu = [at(43008 + 8192 * i, [128, 8 * QB], BF16) for i in range(4)]
    hqfm = at(75776, [128, 22 * 512], BF16)
    hqfm_v = hqfm.rearrange("p (c n) -> p c n", n=512)
    xq32 = at(98304, [128, 2 * KB], F32)
    xqtm = at(106496, [128, 2 * KB], BF16)
    xqfm = at(110592, [128, 4 * KB], BF16)
    qsP = tuple([at(4096 * i, [128, 1024], F32) for i in range(4)]
                + [at(12288, [128, 1024], I32), at(16384, [128, 192], F32),
                   at(17152, [128, 64], F32)])
    qsF = tuple([at(118784 + 4096 * i, [128, 1024], F32) for i in range(4)]
                + [at(131072, [128, 1024], I32), at(135168, [128, 192], F32),
                   at(135936, [128, 64], F32)])
    silu_s = at(136192, [128, 2 * QB], F32)
    assert AB + ARENA_ATTN_END <= 229300, (AB, ARENA_ATTN_END)
    assert AB + 140800 <= 229376, AB

    PSA = nc.alloc_psum_tensor("PSA", [128, 2048], F32).ap()
    PSB = nc.alloc_psum_tensor("PSB", [128, 1024], F32).ap()
    PSC = nc.alloc_psum_tensor("PSC", [128, 1024], F32).ap()
    PSB_b = PSB.bitcast(BF16)
    PSC_b = PSC.bitcast(BF16)

    sems = {}
    for k in "tvsgdce1234567":
        sems[k] = nc.alloc_semaphore(name="sem_" + k)

    def program(p: P, sec):
        V, S, Tn, G, Dq = p.v, p.s, p.t, p.g, p.d
        m = {}

        Dq(lambda: sec.dma_start(out=identb[:], in_=identb_d))
        Dq(lambda: sec.dma_start(out=cvals[:], in_=cvals_d))
        d_gidx = Dq(lambda: sec.dma_start(out=gidx[:], in_=gidx_d))

        def load_weight_cast(dst, src_d, rows, cols):
            last = 0
            for ch in range(rows // 128):
                p.w('sync', 'v', m.get('stage_free', 0))
                dd = Dq(lambda ch=ch: sec.dma_start(out=stage[:, 0:cols],
                                                    in_=src_d[128 * ch:128 * ch + 128, :]))
                p.w('vector', 'd', dd)
                last = V(lambda ch=ch: nc.vector.tensor_copy(
                    dst[:, cols * ch:cols * ch + cols], stage[:, 0:cols]))
                m['stage_free'] = last
            return last

        def quant(dst, src, width, bufs, twait=0):
            q0f, q1f, q2f, q3f, qsif, qsmf, qsm2f = bufs
            nb = width // 16
            p.w('vector', 'v', p.cnt['v'])
            p.w('vector', 't', twait)
            src3 = src.rearrange("p (b k) -> p b k", k=16)
            dst3 = dst.rearrange("p (b k) -> p b k", k=16)

            def vs(fn):
                r = V(fn)
                p.w('vector', 'v', r)
                return r

            q0, q1, q2, q3 = (q[:, 0:width] for q in (q0f, q1f, q2f, q3f))
            qsi_ = qsif[:, 0:width]
            q03 = q0.rearrange("p (b k) -> p b k", k=16)
            q13 = q1.rearrange("p (b k) -> p b k", k=16)
            vs(lambda: nc.vector.tensor_reduce(qsmf[:, 0:nb], src3, axis=AX, op=Op.max,
                                               apply_absolute_value=True))
            vs(lambda: nc.vector.tensor_scalar(qsmf[:, 64:64 + nb], qsmf[:, 0:nb],
                                               1.0 / 3.0, 1e-12, op0=Op.mult, op1=Op.max))
            vs(lambda: nc.vector.tensor_scalar(qsmf[:, 128:128 + nb], qsmf[:, 0:nb],
                                               1.0 / 12.0, None, op0=Op.mult))
            vs(lambda: nc.vector.reciprocal(qsm2f[:, 0:nb], qsmf[:, 64:64 + nb]))
            binvh = qsm2f[:, 0:nb].broadcast_to([128, nb, 16])
            bsc12 = qsmf[:, 128:128 + nb].broadcast_to([128, nb, 16])
            v0 = vs(lambda: nc.vector.tensor_tensor(q03, src3, binvh, op=Op.mult))
            p.w('scalar', 'v', v0)
            sq = S(lambda: nc.scalar.activation(q1, q0, AF.Square))
            p.w('vector', 's', sq)
            vs(lambda: nc.vector.tensor_scalar(q2, q1, 1.0, 1.0, op0=Op.is_lt, op1=Op.add))
            vs(lambda: nc.vector.tensor_scalar(q3, q1, 4.0, 1.0, op0=Op.is_lt, op1=Op.add))
            vs(lambda: nc.vector.tensor_tensor(q2, q2, q3, op=Op.mult))
            vs(lambda: nc.vector.tensor_tensor(qsi_, q0, q2, op=Op.mult))
            vs(lambda: nc.vector.tensor_copy(q0, qsi_))
            vs(lambda: nc.vector.tensor_scalar(q2, q1, 1.0, 1.0, op0=Op.is_ge, op1=Op.add))
            vs(lambda: nc.vector.tensor_scalar(q3, q1, 4.0, 1.0, op0=Op.is_ge, op1=Op.add))
            vs(lambda: nc.vector.tensor_tensor(q2, q2, q3, op=Op.mult))
            vs(lambda: nc.vector.tensor_tensor(q1, q0, q2, op=Op.mult))
            return vs(lambda: nc.vector.tensor_tensor(dst3, q13, bsc12, op=Op.mult))



        def prep_col(src_d, in_d, rows):
            for r0 in range(0, rows, 1024):
                rlen = min(1024, rows - r0)
                nch = rlen // 128
                p.w('sync', 'v', m.get('stage_free', 0))
                dd = Dq(lambda src_d=src_d, r0=r0, rlen=rlen: sec.dma_start(
                    out=stageW[:, 0:rlen].rearrange("p (a c) -> p a c", c=128),
                    in_=src_d[r0:r0 + rlen, :].rearrange("(a p) c -> p a c", p=128)))
                p.w('vector', 'd', dd)
                cv = quant(wcastW[:, 0:rlen], stageW[:, 0:rlen], rlen, qsP,
                           twait=m.get('wprep_t', 0))
                m['stage_free'] = cv
                p.w('tensor', 'v', cv)
                p.w('tensor', 'v', m.get('psbT_free', 0))
                p.w('tensor', 'd', 16)
                tl = 0
                for j in range(nch):
                    tl = Tn(lambda j=j: nc.tensor.transpose(
                        PSB_b[:, 128 * j:128 * j + 128],
                        wcastW[:, 128 * j:128 * j + 128], identb[:]))
                m['wprep_t'] = tl
                p.w('vector', 't', tl)
                p.w('vector', 'd', m.get('ttmp_free', 0))
                ev = V(lambda rlen=rlen: nc.vector.tensor_copy(
                    ttmpW[:, 0:rlen], PSB_b[:, 0:rlen]))
                m['psbT_free'] = ev
                p.w('sync', 'v', ev)
                m['ttmp_free'] = Dq(lambda in_d=in_d, r0=r0, rlen=rlen: sec.dma_start(
                    out=in_d[:, r0:r0 + rlen], in_=ttmpW[:, 0:rlen]))
            return m['ttmp_free']

        def prep_wd():
            for g in range(2):
                r0 = 512 * g
                p.w('sync', 'v', m.get('stage_free', 0))
                dd = Dq(lambda r0=r0: sec.dma_start(
                    out=stageW[:, 0:1536].rearrange("p (a c) -> p a c", c=384),
                    in_=wdsh_d[r0:r0 + 512, :].rearrange("(a p) c -> p a c", p=128)))
                p.w('vector', 'd', dd)
                cv = 0
                for off in range(0, 1536, 768):
                    cv = quant(wcastW[:, off:off + 768], stageW[:, off:off + 768],
                               768, qsP, twait=m.get('wprep_t', 0))
                m['stage_free'] = cv
                p.w('tensor', 'v', cv)
                p.w('tensor', 'v', m.get('psbT_free', 0))
                p.w('tensor', 'd', 16)
                tl = 0
                for cb in range(3):
                    for a in range(4):
                        tl = Tn(lambda cb=cb, a=a: nc.tensor.transpose(
                            PSB_b[:, 512 * cb + 128 * a:512 * cb + 128 * a + 128],
                            wcastW[:, 384 * a + 128 * cb:384 * a + 128 * cb + 128],
                            identb[:]))
                m['wprep_t'] = tl
                p.w('vector', 't', tl)
                p.w('vector', 'd', m.get('ttmp_free', 0))
                ev = V(lambda: nc.vector.tensor_copy(ttmpW[:, 0:1536], PSB_b[:, 0:1536]))
                m['psbT_free'] = ev
                p.w('sync', 'v', ev)
                for cb in range(3):
                    m['ttmp_free'] = Dq(lambda cb=cb, r0=r0: sec.dma_start(
                        out=wdin_d[128 * cb:128 * cb + 128, r0:r0 + 512],
                        in_=ttmpW[:, 512 * cb:512 * cb + 512]))
            return m['ttmp_free']

        G8 = [[0, 1, 2, 3, 4, 5, 6, 7]]
        d_wg = prep_col(wgsh_d, wgin_d, II)
        p.w('gpsimd', 'd', d_wg)
        p.c(lambda: nc.gpsimd.collective_compute(
            "AllGather", mybir.AluOpType.bypass, replica_groups=G8,
            ins=[wgin_d.opt()], outs=[wgT_d.opt()]), key='1')
        d_wu = prep_col(wush_d, wuin_d, II)
        p.w('gpsimd', 'd', d_wu)
        p.c(lambda: nc.gpsimd.collective_compute(
            "AllGather", mybir.AluOpType.bypass, replica_groups=G8,
            ins=[wuin_d.opt()], outs=[wuT_d.opt()]), key='2')
        d_wd = prep_wd()
        p.w('gpsimd', 'd', d_wd)
        p.c(lambda: nc.gpsimd.collective_compute(
            "AllGather", mybir.AluOpType.bypass, replica_groups=G8,
            ins=[wdin_d.opt()], outs=[wdT_d.opt()]), key='3')
        load_weight_cast(wdkv, wdkv_d, D, CL)
        load_weight_cast(wuq, wuq_d, D, HCOL)
        load_weight_cast(wuk, wuk_d, CL, HCOL)
        w_attn_v = load_weight_cast(wuv, wuv_d, CL, HCOL)
        V(lambda: nc.vector.memset(onesc[:], 1.0))
        v_init = V(lambda: nc.vector.memset(vaug[:, 64::65], 1.0))

        m['psb_free_l1'] = m.get('psbT_free', 0)

        m['xntm_free'] = p.cnt['t']
        m['xnfm_guard_d'] = m.get('ttmp_free', 0)
        p.w('tensor', 'v', w_attn_v)
        p.w('tensor', 'v', v_init)
        NB = 256
        for batch in range(32):
            tok0 = batch * NB
            dl = []
            for tt in range(2):
                p.w('sync', 'v', m.get(f'xtile_free{tt}', 0))
                p.w('sync', 's', m.get(f'xtile_free_s{tt}', 0))
                p.w('sync', 'v', m.get('stage_free', 0))
                dl.append(Dq(lambda tt=tt, tok0=tok0: sec.dma_start(
                    out=xtile[:, KB * tt:KB * tt + KB],
                    in_=x_d[tok0 + 128 * tt:tok0 + 128 * tt + 128, :])))
            p.w('vector', 'v', m.get('stat_v_free', 0))
            p.w('scalar', 'v', m.get('stat_v_free', 0))
            s_last = 0
            for tt in range(2):
                p.w('vector', 'd', dl[tt])
                p.w('scalar', 'd', dl[tt])
                V(lambda tt=tt: nc.vector.tensor_reduce(
                    stat[:, tt:tt + 1], xtile[:, KB * tt:KB * tt + KB], axis=AX, op=Op.add))
                s_last = S(lambda tt=tt: nc.scalar.activation(
                    sqscr[:, :], xtile[:, KB * tt:KB * tt + KB], AF.Square,
                    accum_out=stat[:, 8 + tt:8 + tt + 1]))
                m[f'xtile_free_s{tt}'] = s_last
            p.w('vector', 's', s_last)
            p.w('vector', 'v', p.cnt['v'])
            V(lambda: nc.vector.tensor_scalar(stat[:, 0:2], stat[:, 0:2], 1.0 / D, None, op0=Op.mult))
            p.w('vector', 'v', p.cnt['v'])
            V(lambda: nc.vector.tensor_scalar(stat[:, 8:10], stat[:, 8:10], 1.0 / D, None, op0=Op.mult))
            p.w('vector', 'v', p.cnt['v'])
            V(lambda: nc.vector.tensor_tensor(stat[:, 16:18], stat[:, 0:2], stat[:, 0:2], op=Op.mult))
            p.w('vector', 'v', p.cnt['v'])
            vvar = V(lambda: nc.vector.tensor_tensor(stat[:, 16:18], stat[:, 8:10], stat[:, 16:18], op=Op.subtract))
            p.w('scalar', 'v', vvar)
            svar = S(lambda: nc.scalar.activation(stat[:, 20:22], stat[:, 16:18], AF.Sqrt, bias=LN_EPS))
            p.w('vector', 's', svar)
            vrs = V(lambda: nc.vector.reciprocal(stat[:, 24:26], stat[:, 20:22]))
            p.w('vector', 'v', vrs)
            p.w('vector', 't', m.get('xntm_free', 0))
            for tt in range(2):
                va = V(lambda tt=tt: nc.vector.tensor_scalar(
                    xntm[:, KB * tt:KB * tt + KB], xtile[:, KB * tt:KB * tt + KB],
                    stat[:, tt:tt + 1], stat[:, 24 + tt:24 + tt + 1],
                    op0=Op.subtract, op1=Op.mult))
                m[f'xtile_free{tt}'] = va
            m['stat_v_free'] = p.cnt['v']
            v_apply = p.cnt['v']
            p.w('tensor', 'v', v_apply)
            for ch in range(8):
                p.w('tensor', 's', m.get(f'pscT_free{ch % 2}', 0))
                if ch % 2 == 1:
                    p.w('tensor', 'v', m.get('pscV_free', 0))
                tl = 0
                for tt in range(2):
                    tl = Tn(lambda ch=ch, tt=tt: nc.tensor.transpose(
                        PSC_b[:, 1024 * (ch % 2) + 128 * tt:1024 * (ch % 2) + 128 * tt + 128],
                        xntm[:, KB * tt + 128 * ch:KB * tt + 128 * ch + 128], identb[:]))
                if ch == 7:
                    m['xntm_free'] = tl
                p.w('scalar', 't', tl)
                p.w('scalar', 't', m.get('xnfm_used', 0))
                p.w('scalar', 'd', m.get('xnfm_guard_d', 0))
                ev = S(lambda ch=ch: nc.scalar.copy(
                    xnfm[:, NB * ch:NB * ch + NB],
                    PSC_b[:, 1024 * (ch % 2):1024 * (ch % 2) + NB]))
                m[f'pscT_free{ch % 2}'] = ev
            s_xnfm = p.cnt['s']
            p.w('tensor', 's', s_xnfm)
            p.w('tensor', 'v', m.get('psa_free_l1', 0))
            for mt in range(2):
                for ch in range(8):
                    Tn(lambda mt=mt, ch=ch: nc.tensor.matmul(
                        PSA[:, 512 * mt:512 * mt + NB],
                        wdkv[:, CL * ch + 128 * mt:CL * ch + 128 * mt + 128],
                        xnfm[:, NB * ch:NB * ch + NB], start=(ch == 0), stop=(ch == 7)))
            t_ckv = p.cnt['t']
            for mt in range(2):
                for ch in range(8):
                    Tn(lambda mt=mt, ch=ch: nc.tensor.matmul(
                        PSA[:, 1024 + 512 * mt:1024 + 512 * mt + NB],
                        wuq[:, HCOL * ch + 128 * mt:HCOL * ch + 128 * mt + 128],
                        xnfm[:, NB * ch:NB * ch + NB], start=(ch == 0), stop=(ch == 7)))
            m['xnfm_used'] = p.cnt['t']
            t_q = p.cnt['t']
            p.w('vector', 't', t_ckv)
            p.w('vector', 't', m.get('ckv_used', 0))
            ev_ckv = 0
            for mt in range(2):
                ev_ckv = V(lambda mt=mt: nc.vector.tensor_copy(
                    ckv[:, NB * mt:NB * mt + NB], PSA[:, 512 * mt:512 * mt + NB]))
            p.w('scalar', 't', t_q)
            for mt in range(2):
                S(lambda mt=mt, tok0=tok0: nc.scalar.copy(
                    qfm[mt][:, tok0:tok0 + NB], PSA[:, 1024 + 512 * mt:1024 + 512 * mt + NB]))
            m['psa_free_l1'] = p.cnt['v']
            m['psa_free_l1_s'] = p.cnt['s']
            p.w('tensor', 'v', ev_ckv)
            p.w('tensor', 'v', m.get('psb_free_l1', 0))
            p.w('tensor', 's', m.get('psb_free_l1_s', 0))
            for mt in range(2):
                for ch in range(2):
                    Tn(lambda mt=mt, ch=ch: nc.tensor.matmul(
                        PSB[:, 512 * mt:512 * mt + NB],
                        wuk[:, HCOL * ch + 128 * mt:HCOL * ch + 128 * mt + 128],
                        ckv[:, NB * ch:NB * ch + NB], start=(ch == 0), stop=(ch == 1)))
            t_k = p.cnt['t']
            for ktl in range(2):
                kt = 2 * batch + ktl
                p.w('tensor', 'v', m.get('pscV_free', 0))
                p.w('tensor', 'v', m.get('pscT_free1', 0))
                for ch in range(2):
                    Tn(lambda ktl=ktl, ch=ch: nc.tensor.matmul(
                        PSC[:, 512:512 + 256],
                        ckv[:, NB * ch + 128 * ktl:NB * ch + 128 * ktl + 128],
                        wuv[:, HCOL * ch:HCOL * ch + HCOL], start=(ch == 0), stop=(ch == 1)))
                t_v = p.cnt['t']
                p.w('vector', 't', t_v)
                vv2 = V(lambda kt=kt: nc.vector.tensor_copy(
                    vaug[:, 260 * kt:260 * kt + 260].rearrange("p (h j) -> p h j", j=65)[:, :, 0:64],
                    PSC[:, 512:512 + 256].rearrange("p (h j) -> p h j", j=64)))
                m['pscV_free'] = vv2
            m['ckv_used'] = p.cnt['t']
            p.w('scalar', 't', t_k)
            for mt in range(2):
                S(lambda mt=mt, tok0=tok0: nc.scalar.copy(
                    kfm[mt][:, tok0:tok0 + NB], PSB[:, 512 * mt:512 * mt + NB]))
            m['psb_free_l1_s'] = p.cnt['s']
        l1_v = p.cnt['v']
        l1_s = p.cnt['s']
        p.w('sync', 't', p.cnt['t'])
        d_masks = Dq(lambda: sec.dma_start(out=masks[:], in_=masks_d))

        p.w('tensor', 'v', l1_v)
        p.w('tensor', 's', l1_s)
        for tt in range(16):
            Dq(lambda tt=tt: sec.dma_start(out=xres[:, KB * tt:KB * tt + KB],
                                           in_=xq_d[128 * tt:128 * tt + 128, :]))
        d_xres = p.cnt['d']
        ccin_d_count = 0
        SC = [PSB[:, 0:512], PSB[:, 512:1024], PSC[:, 0:512], PSC[:, 512:1024]]
        for qb in range(NQB):
            nkt = 4 * (qb + 1)
            for kt in range(nkt):
                diag_r = kt - 4 * qb
                for h in range(HLOC):
                    u, rb0 = h // 2, 64 * (h % 2)
                    p.w('tensor', 's', m.get(f'exp_h{h}', 0))
                    if kt == 0:
                        p.w('tensor', 'v', m.get('norm_v_done', 0))
                    Tn(lambda u=u, rb0=rb0, kt=kt, qb=qb, h=h: nc.tensor.matmul(
                        SC[h],
                        kfm[u][rb0:rb0 + 64, 128 * kt:128 * kt + 128],
                        qfm[u][rb0:rb0 + 64, QB * qb:QB * qb + QB],
                        start=True, stop=True))
                    m[f'score_h{h}'] = p.cnt['t']
                eb0 = 2048 * (kt % 2)
                for h in range(HLOC):
                    p.w('scalar', 't', m[f'score_h{h}'])
                    p.w('scalar', 't', m.get(f'pv_p{kt % 2}', 0))
                    se = S(lambda h=h, eb0=eb0: nc.scalar.activation(
                        ebuf[:, eb0 + 512 * h:eb0 + 512 * h + 512], SC[h], AF.Exp,
                        bias=EXP_BIAS, scale=EXP_SCALE))
                    m[f'exp_h{h}'] = se
                    if diag_r >= 0:
                        p.w('vector', 's', se)
                        p.w('vector', 'd', d_masks)
                        mv = V(lambda h=h, diag_r=diag_r, eb0=eb0: nc.vector.tensor_tensor(
                            ebuf[:, eb0 + 512 * h:eb0 + 512 * h + 512],
                            ebuf[:, eb0 + 512 * h:eb0 + 512 * h + 512],
                            masks[:, 512 * diag_r:512 * diag_r + 512],
                            op=Op.mult))
                        m[f'mask_h{h}'] = mv
                for h in range(HLOC):
                    if diag_r >= 0:
                        p.w('tensor', 'v', m[f'mask_h{h}'])
                    else:
                        p.w('tensor', 's', m[f'exp_h{h}'])
                    if kt == 0:
                        p.w('tensor', 'v', m.get('norm_v_done', 0))
                    Tn(lambda h=h, kt=kt, nkt=nkt, eb0=eb0: nc.tensor.matmul(
                        PSA[0:65, 512 * h:512 * h + 512],
                        vaug[:, 260 * kt + 65 * h:260 * kt + 65 * h + 65],
                        ebuf[:, eb0 + 512 * h:eb0 + 512 * h + 512],
                        start=(kt == 0), stop=(kt == nkt - 1)))
                    m[f'pv_h{h}'] = p.cnt['t']
                m[f'pv_p{kt % 2}'] = p.cnt['t']
            t_pv = p.cnt['t']
            p.w('vector', 'd', m.get('ctxn_free', 0))
            p.w('vector', 't', t_pv)
            p.w('vector', 'v', p.cnt['v'])
            vr2 = V(lambda: nc.vector.reciprocal(rl4[:, :], PSA[64:65, 0:2048]))
            p.w('tensor', 'v', vr2)
            tr = 0
            for h in range(HLOC):
                tr = Tn(lambda h=h: nc.tensor.matmul(
                    SC[h][0:64, :], onesc[:], rl4[:, 512 * h:512 * h + 512],
                    start=True, stop=True))
            p.w('vector', 't', tr)
            V(lambda: nc.vector.tensor_copy(rbuf4[:, 0:1024], PSB[0:64, 0:1024]))
            vR = V(lambda: nc.vector.tensor_copy(rbuf4[:, 1024:2048], PSC[0:64, 0:1024]))
            p.w('vector', 'v', vR)
            for rp in range(2):
                V(lambda rp=rp: nc.vector.tensor_tensor(
                    ctxn[64 * rp:64 * rp + 64, 0:1024].rearrange("p (a n) -> p a n", n=512),
                    PSA[0:64, 0:2048].rearrange("p (a b n) -> p a b n", b=2, n=512)[:, :, rp, :],
                    rbuf4[:, 0:2048].rearrange("p (a b n) -> p a b n", b=2, n=512)[:, :, rp, :],
                    op=Op.mult))
            v_ctxn = p.cnt['v']
            m['norm_v_done'] = v_ctxn
            m['psb_guard_v'] = v_ctxn
            p.w('sync', 'v', v_ctxn)
            for u in range(2):
                ccin_d_count = Dq(lambda u=u, qb=qb: sec.dma_start(
                    out=ccin_d[256 * (qb // 4) + 128 * u:256 * (qb // 4) + 128 * u + 128,
                               QB * (qb % 4):QB * (qb % 4) + QB],
                    in_=ctxn[:, QB * u:QB * u + QB]))
            m['ctxn_free'] = ccin_d_count
            if qb % 4 == 3:
                cq = qb // 4
                p.w('gpsimd', 'd', ccin_d_count)
                p.c(lambda cq=cq: nc.gpsimd.collective_compute(
                    "AllGather", mybir.AluOpType.bypass,
                    replica_groups=[[0, 1, 2, 3], [4, 5, 6, 7]],
                    ins=[ccin_d[256 * cq:256 * cq + 256, :].opt()],
                    outs=[ccout_d[1024 * cq:1024 * cq + 1024, :].opt()]), key=str(4 + cq))

        m['psa_free_wo'] = p.cnt['v']
        m['exp_done1'] = p.cnt['s']
        t_quad_end = p.cnt['t']

        p.w('sync', 't', t_quad_end)
        p.w('sync', 's', m.get('exp_done1', 0))
        p.w('sync', 'v', m.get('norm_v_done', 0))
        wo_v = load_weight_cast(wo_s, wo_d, D, KB)
        for k in '4567':
            p.w('gpsimd', k, 1)
        p.w('gpsimd', 'd', d_gidx)

        for tb in range(4):
            cur = tb % 2
            p.w('gpsimd', 't', t_quad_end)
            p.w('gpsimd', 't', m.get(f'ctxg_used{cur}', 0))
            gl = 0
            for ch in range(8):
                gl = G(lambda ch=ch, tb=tb, cur=cur: nc.gpsimd.indirect_dma_start(
                    out=ctxg[cur][:, QB * ch:QB * ch + QB], out_offset=None,
                    in_=ccout_rows,
                    in_offset=bass.IndirectOffsetOnAxis(ap=gidx[:, 4 * ch + tb:4 * ch + tb + 1], axis=0)))
            p.w('tensor', 'g', gl)
            p.w('tensor', 'v', wo_v)
            p.w('tensor', 'v', m.get('psa_free_wo', 0))
            for sub in range(4):
                slot = sub % 2
                p.w('tensor', 'v', m.get(f'wo_res{slot}', 0))
                for ch in range(8):
                    for hf in range(2):
                        Tn(lambda ch=ch, sub=sub, slot=slot, cur=cur, hf=hf: nc.tensor.matmul(
                            PSA[:, 1024 * slot + 512 * hf:1024 * slot + 512 * hf + 512],
                            ctxg[cur][:, QB * ch + 128 * sub:QB * ch + 128 * sub + 128],
                            wo_s[:, KB * ch + 512 * hf:KB * ch + 512 * hf + 512],
                            start=(ch == 0), stop=(ch == 7)))
                t_wo = p.cnt['t']
                m[f'ctxg_used{cur}'] = t_wo
                p.w('vector', 't', t_wo)
                p.w('vector', 'd', d_xres)
                ti = 4 * tb + sub
                va = V(lambda ti=ti, slot=slot: nc.vector.tensor_tensor(
                    xres[:, KB * ti:KB * ti + KB], xres[:, KB * ti:KB * ti + KB],
                    PSA[:, 1024 * slot:1024 * slot + 1024], op=Op.add))
                m[f'wo_res{slot}'] = va
            m[f'wo_tb_done{tb}'] = p.cnt['v']
        wo_res_v = p.cnt['v']
        t_wo_end = p.cnt['t']
        m['hqtm_used'] = t_wo_end
        m['h_guard_t'] = t_wo_end

        y_store = 0
        m['psa_slot0'] = wo_res_v
        m['psa_slot1'] = wo_res_v
        pendB = [None]

        def emit_passB_res(t_dnB, tiles):
            for j, ti in enumerate(tiles):
                p.w('vector', 't', t_dnB)
                va = V(lambda ti=ti, j=j: nc.vector.tensor_tensor(
                    xres[:, KB * ti:KB * ti + KB], xres[:, KB * ti:KB * ti + KB],
                    PSA[:, 1024 * j:1024 * j + 1024], op=Op.add))
                m[f'psa_slot{j}'] = va
                p.w('sync', 'v', va)
                m['y_store'] = Dq(lambda ti=ti: sec.dma_start(
                    out=y_d[128 * ti:128 * ti + 128, :],
                    in_=xres[:, KB * ti:KB * ti + KB]))

        for g in range(4):
            for pr in range(2):
                p.w('vector', 'v', m[f'wo_tb_done{g}'])
                p.w('scalar', 'v', m[f'wo_tb_done{g}'])
                p.w('vector', 'v', m.get('stat_v_free', 0))
                p.w('scalar', 'v', m.get('stat_v_free', 0))
                s_last = 0
                for t2 in range(2):
                    xsl = xres[:, KB * (4 * g + 2 * pr + t2):KB * (4 * g + 2 * pr + t2) + KB]
                    V(lambda xsl=xsl, t2=t2: nc.vector.tensor_reduce(
                        stat[:, t2:t2 + 1], xsl, axis=AX, op=Op.add))
                    s_last = S(lambda xsl=xsl, t2=t2: nc.scalar.activation(
                        sqscr[:, :], xsl, AF.Square, accum_out=stat[:, 8 + t2:8 + t2 + 1]))
                p.w('vector', 's', s_last)
                p.w('vector', 'v', p.cnt['v'])
                V(lambda: nc.vector.tensor_scalar(stat[:, 0:2], stat[:, 0:2], 1.0 / D, None, op0=Op.mult))
                p.w('vector', 'v', p.cnt['v'])
                V(lambda: nc.vector.tensor_scalar(stat[:, 8:10], stat[:, 8:10], 1.0 / D, None, op0=Op.mult))
                p.w('vector', 'v', p.cnt['v'])
                V(lambda: nc.vector.tensor_tensor(stat[:, 16:18], stat[:, 0:2], stat[:, 0:2], op=Op.mult))
                p.w('vector', 'v', p.cnt['v'])
                vv3 = V(lambda: nc.vector.tensor_tensor(stat[:, 16:18], stat[:, 8:10], stat[:, 16:18], op=Op.subtract))
                p.w('scalar', 'v', vv3)
                sv3 = S(lambda: nc.scalar.activation(stat[:, 20:22], stat[:, 16:18], AF.Sqrt, bias=LN_EPS))
                p.w('vector', 's', sv3)
                vr3 = V(lambda: nc.vector.reciprocal(stat[:, 24:26], stat[:, 20:22]))
                p.w('vector', 'v', vr3)
                for t2 in range(2):
                    xsl = xres[:, KB * (4 * g + 2 * pr + t2):KB * (4 * g + 2 * pr + t2) + KB]
                    V(lambda xsl=xsl, t2=t2: nc.vector.tensor_scalar(
                        xq32[:, KB * t2:KB * t2 + KB], xsl,
                        stat[:, t2:t2 + 1], stat[:, 24 + t2:24 + t2 + 1],
                        op0=Op.subtract, op1=Op.mult))
                m['stat_v_free'] = p.cnt['v']
                if pr == 0 and pendB[0] is not None:
                    emit_passB_res(*pendB[0])
                    pendB[0] = None
                for t2 in range(2):
                    tl = 2 * pr + t2
                    qv = quant(xqtm[:, KB * t2:KB * t2 + KB], xq32[:, KB * t2:KB * t2 + KB],
                               KB, qsF, twait=m.get('xqtm_used', 0))
                    par = m.get('pscpar', 0)
                    p.w('tensor', 'v', qv)
                    p.w('tensor', 'v', m.get(f'pscT{par}', 0))
                    tl_t = 0
                    for ch in range(8):
                        tl_t = Tn(lambda t2=t2, ch=ch, par=par: nc.tensor.transpose(
                            PSC_b[:, 1024 * par + 128 * ch:1024 * par + 128 * ch + 128],
                            xqtm[:, KB * t2 + 128 * ch:KB * t2 + 128 * ch + 128], identb[:]))
                    m['xqtm_used'] = tl_t
                    p.w('vector', 't', tl_t)
                    p.w('vector', 't', m.get('xqfm_used', 0))
                    vxf = V(lambda tl=tl, par=par: nc.vector.tensor_copy(
                        xqfm[:, KB * tl:KB * tl + KB], PSC_b[:, 1024 * par:1024 * par + KB]))
                    m[f'pscT{par}'] = vxf
                    m[f'xqfm_rdy{tl}'] = vxf
                    m['pscpar'] = par ^ 1
            for si in range(6):
                wslice = min(512, II - 512 * si)
                cur = si % 2
                pcw = si // 2
                p.w('sync', '1', 1)
                p.w('sync', 't', m.get(f'wgu_used{cur}', 0))
                p.w('gpsimd', '2', 1)
                p.w('gpsimd', 't', m.get(f'wgu_used{cur}', 0))
                dgu = 0
                ggu = 0
                for wch in range(8):
                    srcg = wgT_d[:, 512 * si:512 * si + wslice]
                    dgu = Dq(lambda srcg=srcg, wch=wch, cur=cur, wslice=wslice: sec.dma_start(
                        out=wgu[2 * cur][:, QB * wch:QB * wch + wslice],
                        in_=srcg[128 * wch:128 * wch + 128, :]))
                    srcu = wuT_d[:, 512 * si:512 * si + wslice]
                    ggu = G(lambda srcu=srcu, wch=wch, cur=cur, wslice=wslice: sec.dma_start(
                        out=wgu[2 * cur + 1][:, QB * wch:QB * wch + wslice],
                        in_=srcu[128 * wch:128 * wch + 128, :]))
                p.w('tensor', 'd', dgu)
                p.w('tensor', 'g', ggu)
                p.w('tensor', 'v', m.get('psb_guard_v', 0))
                for tl in range(4):
                    pp = tl % 2
                    gb = PSB[:, 0:wslice] if pp == 0 else PSA[:, 0:wslice]
                    ub = PSB[:, 512:512 + wslice] if pp == 0 else PSA[:, 512:512 + wslice]
                    p.w('tensor', 'v', m[f'xqfm_rdy{tl}'])
                    if pp == 1:
                        p.w('tensor', 'v', m.get('psa_slot0', 0))
                    p.w('tensor', 'v', m.get(f'h_done{pp}', 0))
                    p.w('tensor', 's', m.get(f'silu_done{pp}', 0))
                    for ch in range(8):
                        Tn(lambda tl=tl, ch=ch, cur=cur, wslice=wslice, gb=gb: nc.tensor.matmul(
                            gb,
                            xqfm[:, 1024 * tl + 128 * ch:1024 * tl + 128 * ch + 128],
                            wgu[2 * cur][:, QB * ch:QB * ch + wslice],
                            start=(ch == 0), stop=(ch == 7)))
                    t_g = p.cnt['t']
                    for ch in range(8):
                        Tn(lambda tl=tl, ch=ch, cur=cur, wslice=wslice, ub=ub: nc.tensor.matmul(
                            ub,
                            xqfm[:, 1024 * tl + 128 * ch:1024 * tl + 128 * ch + 128],
                            wgu[2 * cur + 1][:, QB * ch:QB * ch + wslice],
                            start=(ch == 0), stop=(ch == 7)))
                    t_u = p.cnt['t']
                    m[f'wgu_used{cur}'] = t_u
                    m['xqfm_used'] = t_u
                    p.w('scalar', 't', t_g)
                    p.w('scalar', 'v', m.get(f'h_done{pp}', 0))
                    ssl = S(lambda wslice=wslice, pp=pp, gb=gb: nc.scalar.activation(
                        silu_s[:, 512 * pp:512 * pp + wslice], gb, AF.Silu))
                    m[f'silu_done{pp}'] = ssl
                    p.w('vector', 't', t_u)
                    p.w('vector', 't', m.get('h_guard_t', 0))
                    p.w('vector', 's', ssl)
                    hv = V(lambda tl=tl, si=si, wslice=wslice, pp=pp, ub=ub:
                           nc.vector.tensor_tensor(
                        hbuf[:, 1024 * tl + 512 * (si % 2):
                             1024 * tl + 512 * (si % 2) + wslice],
                        ub, silu_s[:, 512 * pp:512 * pp + wslice], op=Op.mult))
                    m[f'h_done{pp}'] = hv
                if si % 2 == 1 or si == 5:
                    pc = si // 2
                    w = 1024 if pc < 2 else 768
                    nch = w // 128
                    c0 = 1024 * pc
                    for tl in range(4):
                        qv = quant(hqtm[:, c0:c0 + w],
                                   hbuf[:, 1024 * tl:1024 * tl + w],
                                   w, qsF, twait=m.get('hqtm_used', 0))
                        par = m.get('pscpar', 0)
                        p.w('tensor', 'v', qv)
                        p.w('tensor', 'v', m.get(f'pscT{par}', 0))
                        tl_t = 0
                        for j in range(nch):
                            tl_t = Tn(lambda c0=c0, j=j, par=par: nc.tensor.transpose(
                                PSC_b[:, 1024 * par + 128 * j:1024 * par + 128 * j + 128],
                                hqtm[:, c0 + 128 * j:c0 + 128 * j + 128], identb[:]))
                        m['hqtm_used'] = tl_t
                        p.w('vector', 't', tl_t)
                        p.w('vector', 't', m.get('hqfm_used', 0))
                        hv2 = V(lambda tl=tl, pc=pc, nch=nch, par=par: nc.vector.tensor_copy(
                            hqfm_v[:, 8 * pc:8 * pc + nch, 128 * tl:128 * tl + 128],
                            PSC_b[:, 1024 * par:1024 * par + 128 * nch].rearrange("p (c i) -> p c i", i=128)))
                        m[f'pscT{par}'] = hv2
                        m[f'hqfm_rdy_{tl}_{pc}'] = hv2
                        m['pscpar'] = par ^ 1
            for half in range(2):
                for ch in range(22):
                    pc = min(ch // 8, 2)
                    slot = m.get('wdnslot', 0)
                    m['wdnslot'] = (slot + 1) % 6
                    qi = slot % 3
                    qk = ('d', 'g', 'e')[qi]
                    qeng = ('sync', 'gpsimd', 'scalar')[qi]
                    p.w(qeng, '3', 1)
                    p.w(qeng, 't', t_wo_end)
                    p.w(qeng, 't', m.get(f'wdn_used{slot}', 0))
                    fn = (lambda ch=ch, slot=slot: sec.dma_start(
                        out=wdnR[slot][:, :], in_=wdT_d[128 * ch:128 * ch + 128, :]))
                    dwd = {'d': Dq, 'g': G, 'e': p.e}[qk](fn)
                    p.w('tensor', qk, dwd)
                    for j in range(2):
                        tl = 2 * half + j
                        p.w('tensor', 'v', m[f'hqfm_rdy_{tl}_{pc}'])
                        if ch == 0:
                            p.w('tensor', 'v', m[f'psa_slot{j}'])
                            p.w('tensor', 'v', m.get('h_done1', 0))
                            p.w('tensor', 's', m.get('silu_done1', 0))
                        for hf in range(2):
                            Tn(lambda ch=ch, j=j, tl=tl, slot=slot, hf=hf: nc.tensor.matmul(
                                PSA[:, 1024 * j + 512 * hf:1024 * j + 512 * hf + 512],
                                hqfm_v[:, ch, 128 * tl:128 * tl + 128],
                                wdnR[slot][:, 512 * hf:512 * hf + 512],
                                start=(ch == 0), stop=(ch == 21)))
                    m[f'wdn_used{slot}'] = p.cnt['t']
                t_dn = p.cnt['t']
                if half == 0:
                    for j in range(2):
                        ti = 4 * g + j
                        p.w('vector', 't', t_dn)
                        va = V(lambda ti=ti, j=j: nc.vector.tensor_tensor(
                            xres[:, KB * ti:KB * ti + KB], xres[:, KB * ti:KB * ti + KB],
                            PSA[:, 1024 * j:1024 * j + 1024], op=Op.add))
                        m[f'psa_slot{j}'] = va
                        p.w('sync', 'v', va)
                        m['y_store'] = Dq(lambda ti=ti: sec.dma_start(
                            out=y_d[128 * ti:128 * ti + 128, :],
                            in_=xres[:, KB * ti:KB * ti + KB]))
            m['hqfm_used'] = p.cnt['t']
            pendB[0] = (p.cnt['t'], [4 * g + 2, 4 * g + 3])
        emit_passB_res(*pendB[0])
        pendB[0] = None
        y_store = m['y_store']

        p.w('sync', 'd', y_store)

    with nc.Block() as block:
        @block.sync
        def _(sec):
            program(P(nc, 'sync', sec, sems), sec)

        @block.vector
        def _(sec):
            program(P(nc, 'vector', sec, sems), sec)

        @block.scalar
        def _(sec):
            program(P(nc, 'scalar', sec, sems), sec)

        @block.tensor
        def _(sec):
            program(P(nc, 'tensor', sec, sems), sec)

        @block.gpsimd
        def _(sec):
            program(P(nc, 'gpsimd', sec, sems), sec)

    return nc


_NC_CACHE = None


def _consts():
    mm = np.zeros((128, 4, 512), np.float32)
    kk = np.arange(128)[:, None]
    qq = np.arange(512)[None, :]
    for r in range(4):
        mm[:, r, :] = (qq >= kk + 128 * r)
    masks = np.ascontiguousarray(mm.reshape(128, 2048)).astype(ml_dtypes.bfloat16)
    identb = np.eye(128, dtype=ml_dtypes.bfloat16)
    cvals = np.tile(np.array([[EXP_SCALE, EXP_BIAS, LN_EPS, 0.0]], np.float32), (128, 1))
    return identb, masks, cvals


def kernel(L0_attn_x, L0_attn_W_DKV, L0_attn_W_UK, L0_attn_W_UV, L0_attn_W_UQ,
           L0_attn_W_O, L0_ffn_w_gate, L0_ffn_w_up, L0_ffn_w_down):
    global _NC_CACHE
    x = np.ascontiguousarray(np.asarray(L0_attn_x, np.float32))
    W_DKV = np.ascontiguousarray(np.asarray(L0_attn_W_DKV, np.float32))
    W_UK = np.ascontiguousarray(np.asarray(L0_attn_W_UK, np.float32))
    W_UV = np.ascontiguousarray(np.asarray(L0_attn_W_UV, np.float32))
    W_UQ = np.ascontiguousarray(np.asarray(L0_attn_W_UQ, np.float32))
    W_O = np.ascontiguousarray(np.asarray(L0_attn_W_O, np.float32))
    wg = np.ascontiguousarray(np.asarray(L0_ffn_w_gate, np.float32))
    wu = np.ascontiguousarray(np.asarray(L0_ffn_w_up, np.float32))
    wd = np.ascontiguousarray(np.asarray(L0_ffn_w_down, np.float32))

    identb, masks, cvals = _consts()
    if _NC_CACHE is None:
        _NC_CACHE = build_nc()
    nc = _NC_CACHE

    in_maps = []
    pidx = np.arange(128, dtype=np.uint32)
    for c in range(8):
        b, hq = c // 4, c % 4
        hs = slice(HCOL * hq, HCOL * hq + HCOL)
        gidx = np.zeros((128, 64), np.uint32)
        for ch in range(8):
            for tb in range(4):
                gidx[:, 4 * ch + tb] = 4096 * hq + (128 * ch + pidx) * 4 + tb
        wdsh = np.zeros((1024, 384), np.float32)
        lo = 384 * c
        hi = min(lo + 384, II)
        wdsh[:, 0:hi - lo] = wd[:, lo:hi]
        in_maps.append({
            "x": x[b],
            "xquarter": np.ascontiguousarray(x[b, 2048 * hq:2048 * hq + 2048]),
            "wdkv": W_DKV,
            "wuq": np.ascontiguousarray(W_UQ[:, hs]),
            "wuk": np.ascontiguousarray(W_UK[:, hs]),
            "wuv": np.ascontiguousarray(W_UV[:, hs]),
            "wo": W_O,
            "wgsh": np.ascontiguousarray(wg[:, 128 * c:128 * c + 128]),
            "wush": np.ascontiguousarray(wu[:, 128 * c:128 * c + 128]),
            "wdsh": wdsh,
            "identb": identb, "masks": masks, "cvals": cvals,
            "gidx": gidx,
        })

    res = run_bass_kernel_spmd(nc, in_maps, core_ids=list(range(8)))
    out = np.empty((2, T, D), np.float32)
    for c in range(8):
        b, hq = c // 4, c % 4
        out[b, 2048 * hq:2048 * hq + 2048] = np.asarray(res.results[c]["y"])
    return out


# version 21
# speedup vs baseline: 1.0005x; 1.0005x over previous
DEBUG_TAPS = False

import numpy as np
import ml_dtypes
import concourse.bass as bass
import concourse.mybir as mybir
from concourse.bass_utils import run_bass_kernel_spmd
from concourse.alu_op_type import AluOpType

F32 = mybir.dt.float32
BF16 = mybir.dt.bfloat16
I32 = mybir.dt.int32
U32 = mybir.dt.uint32
AF = mybir.ActivationFunctionType
AX = mybir.AxisListType.X
Op = AluOpType

T = 8192
D = 1024
CL = 256
HLOC = 4
HCOL = 256
II = 2816
QB = 512
NQB = 16
NKT = 64
KB = 1024
EXP_SCALE = 0.125
EXP_BIAS = -3.0
LN_EPS = 1e-5

ENGS = ('tensor', 'vector', 'scalar', 'gpsimd', 'sync')


class P:

    def __init__(self, nc, active, eng, sems):
        self.nc = nc
        self.active = active
        self.eng = eng
        self.sems = sems
        self.cnt = dict.fromkeys("tvsgdce1234567", 0)
        self.wm = {e: dict.fromkeys("tvsgdce1234567", 0) for e in ENGS}

    def w(self, tgt, key, count):
        if not count or count <= self.wm[tgt][key]:
            return
        self.wm[tgt][key] = count
        if self.active == tgt:
            self.eng.wait_ge(self.sems[key], count)

    def _do(self, key, inc, fn, act):
        self.cnt[key] += inc
        if self.active == act:
            fn().then_inc(self.sems[key], inc)
        return self.cnt[key]

    def t(self, fn):
        return self._do('t', 1, fn, 'tensor')

    def v(self, fn):
        return self._do('v', 1, fn, 'vector')

    def s(self, fn):
        return self._do('s', 1, fn, 'scalar')

    def g(self, fn):
        return self._do('g', 16, fn, 'gpsimd')

    def d(self, fn):
        return self._do('d', 16, fn, 'sync')

    def e(self, fn):
        return self._do('e', 16, fn, 'scalar')

    def c(self, fn, key='c'):
        self.cnt[key] += 1
        if self.active == 'gpsimd':
            fn().then_inc(self.sems[key])
        return self.cnt[key]


def build_nc():
    nc = bass.Bass("TRN2", target_bir_lowering=False)

    x_d = nc.dram_tensor("x", [T, D], F32, kind="ExternalInput").ap()
    xq_d = nc.dram_tensor("xquarter", [2048, D], F32, kind="ExternalInput").ap()
    wdkv_d = nc.dram_tensor("wdkv", [D, CL], F32, kind="ExternalInput").ap()
    wuq_d = nc.dram_tensor("wuq", [D, HCOL], F32, kind="ExternalInput").ap()
    wuk_d = nc.dram_tensor("wuk", [CL, HCOL], F32, kind="ExternalInput").ap()
    wuv_d = nc.dram_tensor("wuv", [CL, HCOL], F32, kind="ExternalInput").ap()
    wo_d = nc.dram_tensor("wo", [D, D], F32, kind="ExternalInput").ap()
    wgsh_d = nc.dram_tensor("wgsh", [II, 128], F32, kind="ExternalInput").ap()
    wush_d = nc.dram_tensor("wush", [II, 128], F32, kind="ExternalInput").ap()
    wdsh_d = nc.dram_tensor("wdsh", [D, 384], F32, kind="ExternalInput").ap()
    identb_d = nc.dram_tensor("identb", [128, 128], BF16, kind="ExternalInput").ap()
    masks_d = nc.dram_tensor("masks", [128, 2048], BF16, kind="ExternalInput").ap()
    cvals_d = nc.dram_tensor("cvals", [128, 4], F32, kind="ExternalInput").ap()
    gidx_d = nc.dram_tensor("gidx", [128, 64], U32, kind="ExternalInput").ap()
    y_d = nc.dram_tensor("y", [2048, D], F32, kind="ExternalOutput").ap()
    y2_d = nc.dram_tensor("y2", [2048, D], F32, kind="ExternalOutput").ap() if DEBUG_TAPS else None
    y3_d = nc.dram_tensor("y3", [2048, D], BF16, kind="ExternalOutput").ap() if DEBUG_TAPS else None
    y4_d = nc.dram_tensor("y4", [2048, II], BF16, kind="ExternalOutput").ap() if DEBUG_TAPS else None
    wgin_d = nc.dram_tensor("wgin", [352, KB], BF16).ap()
    wuin_d = nc.dram_tensor("wuin", [352, KB], BF16).ap()
    wdin_d = nc.dram_tensor("wdin", [384, D], BF16).ap()
    wgT_d = nc.dram_tensor("wgT", [2816, KB], BF16, addr_space="Shared").ap()
    wuT_d = nc.dram_tensor("wuT", [2816, KB], BF16, addr_space="Shared").ap()
    wdT_d = nc.dram_tensor("wdT", [3072, D], BF16, addr_space="Shared").ap()
    ccin_d = nc.dram_tensor("ccin", [4 * HCOL, 2048], BF16).ap()
    ccout_d = nc.dram_tensor("ccout", [4096, 2048], BF16).ap()
    ccout_rows = ccout_d.rearrange("r (c n) -> (r c) n", n=QB)

    def sb(name, shape, dt):
        return nc.alloc_sbuf_tensor("s_" + name, shape, dt).ap()

    xres = sb("xres", [128, 16 * KB], F32)
    xres_v = xres.rearrange("p (tt k) -> p tt k", k=KB)
    identb = sb("identb", [128, 128], BF16)
    cvals = sb("cvals", [128, 4], F32)
    nc.const_aps.register(F32, EXP_SCALE, cvals[:, 0:1])
    nc.const_aps.register(F32, EXP_BIAS, cvals[:, 1:2])
    nc.const_aps.register(F32, LN_EPS, cvals[:, 2:3])
    gidx = sb("gidx", [128, 64], U32)
    stat = sb("stat", [128, 32], F32)
    sqscr = sb("sqscr", [128, KB], F32)
    onesc = sb("onesc", [1, 64], F32)

    from concourse.bass_primitives import MemorySpace
    AB = (nc.bytes_allocated(MemorySpace.SBUF) + 511) & ~511
    _names = [0]

    def at(off, shape, dt):
        _names[0] += 1
        return nc.alloc_sbuf_tensor_at(f"a_{_names[0]}", shape, dt, offset=AB + off).ap()

    qfm = [at(16384 * u, [128, T], BF16) for u in range(2)]
    kfm = [at(32768 + 16384 * u, [128, T], BF16) for u in range(2)]
    vaug = at(65536, [128, NKT * 260], BF16)
    wdkv = at(98816, [128, 8 * CL], BF16)
    wuq = at(102912, [128, 8 * HCOL], BF16)
    wuk = at(107008, [128, 2 * HCOL], BF16)
    wuv = at(108032, [128, 2 * HCOL], BF16)
    ebuf = at(109056, [128, 2 * KB], BF16)
    masks = at(113152, [128, 2048], BF16)
    rl4 = at(117248, [1, 4 * QB], F32)
    rbuf4 = at(125440, [64, 4 * QB], F32)
    xtile = at(121344, [128, 2 * KB], F32)
    stage = at(121344, [128, KB], F32)
    stageW = at(121344, [128, 2 * KB], F32)
    wcast = at(129536, [128, KB], BF16)
    wcastW = at(129536, [128, 2 * KB], BF16)
    xntm = at(129536, [128, 2 * KB], BF16)
    ttmp = at(133632, [128, KB], BF16)
    ttmpW = at(133632, [128, 2 * KB], BF16)
    xnfm = at(133632, [128, 8 * 256], BF16)
    ckv = at(137728, [128, 2 * 256], BF16)
    ctxn = at(138752, [128, 2 * QB], BF16)
    ARENA_ATTN_END = 140800

    wo_s = at(0, [128, 8 * KB], BF16)
    ctxg = [at(16384 + 8192 * i, [128, 8 * QB], BF16) for i in range(2)]

    hbuf = at(0, [128, 4 * KB], F32)
    wdnR = [at(16384 + 2048 * i, [128, KB], BF16) for i in range(6)]
    hqtm = at(36864, [128, II], BF16)
    wgu = [at(43008 + 8192 * i, [128, 8 * QB], BF16) for i in range(4)]
    hqfm = at(75776, [128, 22 * 512], BF16)
    hqfm_v = hqfm.rearrange("p (c n) -> p c n", n=512)
    xq32 = at(98304, [128, 2 * KB], F32)
    xqtm = at(106496, [128, 2 * KB], BF16)
    xqfm = at(110592, [128, 4 * KB], BF16)
    qsP = tuple([at(4096 * i, [128, 1024], F32) for i in range(4)]
                + [at(12288, [128, 1024], I32), at(16384, [128, 192], F32),
                   at(17152, [128, 64], F32)])
    qsF = tuple([at(118784 + 4096 * i, [128, 1024], F32) for i in range(4)]
                + [at(131072, [128, 1024], I32), at(135168, [128, 192], F32),
                   at(135936, [128, 64], F32)])
    silu_s = at(136192, [128, 2 * QB], F32)
    assert AB + ARENA_ATTN_END <= 229300, (AB, ARENA_ATTN_END)
    assert AB + 140800 <= 229376, AB

    PSA = nc.alloc_psum_tensor("PSA", [128, 2048], F32).ap()
    PSB = nc.alloc_psum_tensor("PSB", [128, 1024], F32).ap()
    PSC = nc.alloc_psum_tensor("PSC", [128, 1024], F32).ap()
    PSB_b = PSB.bitcast(BF16)
    PSC_b = PSC.bitcast(BF16)

    sems = {}
    for k in "tvsgdce1234567":
        sems[k] = nc.alloc_semaphore(name="sem_" + k)

    def program(p: P, sec):
        V, S, Tn, G, Dq = p.v, p.s, p.t, p.g, p.d
        m = {}

        Dq(lambda: sec.dma_start(out=identb[:], in_=identb_d))
        Dq(lambda: sec.dma_start(out=masks[:], in_=masks_d))
        Dq(lambda: sec.dma_start(out=cvals[:], in_=cvals_d))
        d_gidx = Dq(lambda: sec.dma_start(out=gidx[:], in_=gidx_d))

        def load_weight_cast(dst, src_d, rows, cols):
            last = 0
            for ch in range(rows // 128):
                p.w('sync', 'v', m.get('stage_free', 0))
                dd = Dq(lambda ch=ch: sec.dma_start(out=stage[:, 0:cols],
                                                    in_=src_d[128 * ch:128 * ch + 128, :]))
                p.w('vector', 'd', dd)
                last = V(lambda ch=ch: nc.vector.tensor_copy(
                    dst[:, cols * ch:cols * ch + cols], stage[:, 0:cols]))
                m['stage_free'] = last
            return last

        def quant(dst, src, width, bufs, twait=0):
            q0f, q1f, q2f, q3f, qsif, qsmf, qsm2f = bufs
            nb = width // 16
            p.w('vector', 'v', p.cnt['v'])
            p.w('vector', 't', twait)
            src3 = src.rearrange("p (b k) -> p b k", k=16)
            dst3 = dst.rearrange("p (b k) -> p b k", k=16)

            def vs(fn):
                r = V(fn)
                p.w('vector', 'v', r)
                return r

            q0, q1, q2, q3 = (q[:, 0:width] for q in (q0f, q1f, q2f, q3f))
            qsi_ = qsif[:, 0:width]
            q03 = q0.rearrange("p (b k) -> p b k", k=16)
            q13 = q1.rearrange("p (b k) -> p b k", k=16)
            vs(lambda: nc.vector.tensor_reduce(qsmf[:, 0:nb], src3, axis=AX, op=Op.max,
                                               apply_absolute_value=True))
            vs(lambda: nc.vector.tensor_scalar(qsmf[:, 64:64 + nb], qsmf[:, 0:nb],
                                               1.0 / 3.0, 1e-12, op0=Op.mult, op1=Op.max))
            vs(lambda: nc.vector.tensor_scalar(qsmf[:, 128:128 + nb], qsmf[:, 0:nb],
                                               1.0 / 12.0, None, op0=Op.mult))
            vs(lambda: nc.vector.reciprocal(qsm2f[:, 0:nb], qsmf[:, 64:64 + nb]))
            binvh = qsm2f[:, 0:nb].broadcast_to([128, nb, 16])
            bsc12 = qsmf[:, 128:128 + nb].broadcast_to([128, nb, 16])
            v0 = vs(lambda: nc.vector.tensor_tensor(q03, src3, binvh, op=Op.mult))
            p.w('scalar', 'v', v0)
            sq = S(lambda: nc.scalar.activation(q1, q0, AF.Square))
            p.w('vector', 's', sq)
            vs(lambda: nc.vector.tensor_scalar(q2, q1, 1.0, 1.0, op0=Op.is_lt, op1=Op.add))
            vs(lambda: nc.vector.tensor_scalar(q3, q1, 4.0, 1.0, op0=Op.is_lt, op1=Op.add))
            vs(lambda: nc.vector.tensor_tensor(q2, q2, q3, op=Op.mult))
            vs(lambda: nc.vector.tensor_tensor(qsi_, q0, q2, op=Op.mult))
            vs(lambda: nc.vector.tensor_copy(q0, qsi_))
            vs(lambda: nc.vector.tensor_scalar(q2, q1, 1.0, 1.0, op0=Op.is_ge, op1=Op.add))
            vs(lambda: nc.vector.tensor_scalar(q3, q1, 4.0, 1.0, op0=Op.is_ge, op1=Op.add))
            vs(lambda: nc.vector.tensor_tensor(q2, q2, q3, op=Op.mult))
            vs(lambda: nc.vector.tensor_tensor(q1, q0, q2, op=Op.mult))
            return vs(lambda: nc.vector.tensor_tensor(dst3, q13, bsc12, op=Op.mult))



        def prep_col(src_d, in_d, rows):
            for r0 in range(0, rows, 1024):
                rlen = min(1024, rows - r0)
                nch = rlen // 128
                p.w('sync', 'v', m.get('stage_free', 0))
                dd = Dq(lambda src_d=src_d, r0=r0, rlen=rlen: sec.dma_start(
                    out=stageW[:, 0:rlen].rearrange("p (a c) -> p a c", c=128),
                    in_=src_d[r0:r0 + rlen, :].rearrange("(a p) c -> p a c", p=128)))
                p.w('vector', 'd', dd)
                cv = quant(wcastW[:, 0:rlen], stageW[:, 0:rlen], rlen, qsP,
                           twait=m.get('wprep_t', 0))
                m['stage_free'] = cv
                p.w('tensor', 'v', cv)
                p.w('tensor', 'v', m.get('psbT_free', 0))
                p.w('tensor', 'd', 16)
                tl = 0
                for j in range(nch):
                    tl = Tn(lambda j=j: nc.tensor.transpose(
                        PSB_b[:, 128 * j:128 * j + 128],
                        wcastW[:, 128 * j:128 * j + 128], identb[:]))
                m['wprep_t'] = tl
                p.w('vector', 't', tl)
                p.w('vector', 'd', m.get('ttmp_free', 0))
                ev = V(lambda rlen=rlen: nc.vector.tensor_copy(
                    ttmpW[:, 0:rlen], PSB_b[:, 0:rlen]))
                m['psbT_free'] = ev
                p.w('sync', 'v', ev)
                for j in range((rlen + 511) // 512):
                    w0 = min(512, rlen - 512 * j)
                    t0 = (r0 + 512 * j) // 512
                    m['ttmp_free'] = Dq(lambda in_d=in_d, j=j, w0=w0, t0=t0: sec.dma_start(
                        out=in_d[64 * t0:64 * t0 + w0 // 8, :].rearrange(
                            "r (a c) -> (r a) c", c=w0),
                        in_=ttmpW[:, 512 * j:512 * j + w0]))
            return m['ttmp_free']

        def prep_wd():
            for g in range(2):
                r0 = 512 * g
                p.w('sync', 'v', m.get('stage_free', 0))
                dd = Dq(lambda r0=r0: sec.dma_start(
                    out=stageW[:, 0:1536].rearrange("p (a c) -> p a c", c=384),
                    in_=wdsh_d[r0:r0 + 512, :].rearrange("(a p) c -> p a c", p=128)))
                p.w('vector', 'd', dd)
                cv = 0
                for off in range(0, 1536, 768):
                    cv = quant(wcastW[:, off:off + 768], stageW[:, off:off + 768],
                               768, qsP, twait=m.get('wprep_t', 0))
                m['stage_free'] = cv
                p.w('tensor', 'v', cv)
                p.w('tensor', 'v', m.get('psbT_free', 0))
                p.w('tensor', 'd', 16)
                tl = 0
                for cb in range(3):
                    for a in range(4):
                        tl = Tn(lambda cb=cb, a=a: nc.tensor.transpose(
                            PSB_b[:, 512 * cb + 128 * a:512 * cb + 128 * a + 128],
                            wcastW[:, 384 * a + 128 * cb:384 * a + 128 * cb + 128],
                            identb[:]))
                m['wprep_t'] = tl
                p.w('vector', 't', tl)
                p.w('vector', 'd', m.get('ttmp_free', 0))
                ev = V(lambda: nc.vector.tensor_copy(ttmpW[:, 0:1536], PSB_b[:, 0:1536]))
                m['psbT_free'] = ev
                p.w('sync', 'v', ev)
                for cb in range(3):
                    m['ttmp_free'] = Dq(lambda cb=cb, r0=r0: sec.dma_start(
                        out=wdin_d[128 * cb:128 * cb + 128, r0:r0 + 512],
                        in_=ttmpW[:, 512 * cb:512 * cb + 512]))
            return m['ttmp_free']

        G8 = [[0, 1, 2, 3, 4, 5, 6, 7]]
        d_wg = prep_col(wgsh_d, wgin_d, II)
        p.w('gpsimd', 'd', d_wg)
        p.c(lambda: nc.gpsimd.collective_compute(
            "AllGather", mybir.AluOpType.bypass, replica_groups=G8,
            ins=[wgin_d.opt()], outs=[wgT_d.opt()]), key='1')
        d_wu = prep_col(wush_d, wuin_d, II)
        p.w('gpsimd', 'd', d_wu)
        p.c(lambda: nc.gpsimd.collective_compute(
            "AllGather", mybir.AluOpType.bypass, replica_groups=G8,
            ins=[wuin_d.opt()], outs=[wuT_d.opt()]), key='2')
        d_wd = prep_wd()
        p.w('gpsimd', 'd', d_wd)
        p.c(lambda: nc.gpsimd.collective_compute(
            "AllGather", mybir.AluOpType.bypass, replica_groups=G8,
            ins=[wdin_d.opt()], outs=[wdT_d.opt()]), key='3')
        load_weight_cast(wdkv, wdkv_d, D, CL)
        load_weight_cast(wuq, wuq_d, D, HCOL)
        load_weight_cast(wuk, wuk_d, CL, HCOL)
        w_attn_v = load_weight_cast(wuv, wuv_d, CL, HCOL)
        V(lambda: nc.vector.memset(onesc[:], 1.0))
        v_init = V(lambda: nc.vector.memset(vaug[:, 64::65], 1.0))

        m['psb_free_l1'] = m.get('psbT_free', 0)

        m['xntm_free'] = p.cnt['t']
        m['xnfm_guard_d'] = m.get('ttmp_free', 0)
        p.w('tensor', 'v', w_attn_v)
        p.w('tensor', 'v', v_init)
        NB = 256
        for batch in range(32):
            tok0 = batch * NB
            dl = []
            for tt in range(2):
                p.w('sync', 'v', m.get(f'xtile_free{tt}', 0))
                p.w('sync', 's', m.get(f'xtile_free_s{tt}', 0))
                p.w('sync', 'v', m.get('stage_free', 0))
                dl.append(Dq(lambda tt=tt, tok0=tok0: sec.dma_start(
                    out=xtile[:, KB * tt:KB * tt + KB],
                    in_=x_d[tok0 + 128 * tt:tok0 + 128 * tt + 128, :])))
            p.w('vector', 'v', m.get('stat_v_free', 0))
            p.w('scalar', 'v', m.get('stat_v_free', 0))
            s_last = 0
            for tt in range(2):
                p.w('vector', 'd', dl[tt])
                p.w('scalar', 'd', dl[tt])
                V(lambda tt=tt: nc.vector.tensor_reduce(
                    stat[:, tt:tt + 1], xtile[:, KB * tt:KB * tt + KB], axis=AX, op=Op.add))
                s_last = S(lambda tt=tt: nc.scalar.activation(
                    sqscr[:, :], xtile[:, KB * tt:KB * tt + KB], AF.Square,
                    accum_out=stat[:, 8 + tt:8 + tt + 1]))
                m[f'xtile_free_s{tt}'] = s_last
            p.w('vector', 's', s_last)
            p.w('vector', 'v', p.cnt['v'])
            V(lambda: nc.vector.tensor_scalar(stat[:, 0:2], stat[:, 0:2], 1.0 / D, None, op0=Op.mult))
            p.w('vector', 'v', p.cnt['v'])
            V(lambda: nc.vector.tensor_scalar(stat[:, 8:10], stat[:, 8:10], 1.0 / D, None, op0=Op.mult))
            p.w('vector', 'v', p.cnt['v'])
            V(lambda: nc.vector.tensor_tensor(stat[:, 16:18], stat[:, 0:2], stat[:, 0:2], op=Op.mult))
            p.w('vector', 'v', p.cnt['v'])
            vvar = V(lambda: nc.vector.tensor_tensor(stat[:, 16:18], stat[:, 8:10], stat[:, 16:18], op=Op.subtract))
            p.w('scalar', 'v', vvar)
            svar = S(lambda: nc.scalar.activation(stat[:, 20:22], stat[:, 16:18], AF.Sqrt, bias=LN_EPS))
            p.w('vector', 's', svar)
            vrs = V(lambda: nc.vector.reciprocal(stat[:, 24:26], stat[:, 20:22]))
            p.w('vector', 'v', vrs)
            p.w('vector', 't', m.get('xntm_free', 0))
            for tt in range(2):
                va = V(lambda tt=tt: nc.vector.tensor_scalar(
                    xntm[:, KB * tt:KB * tt + KB], xtile[:, KB * tt:KB * tt + KB],
                    stat[:, tt:tt + 1], stat[:, 24 + tt:24 + tt + 1],
                    op0=Op.subtract, op1=Op.mult))
                m[f'xtile_free{tt}'] = va
            m['stat_v_free'] = p.cnt['v']
            v_apply = p.cnt['v']
            p.w('tensor', 'v', v_apply)
            for ch in range(8):
                p.w('tensor', 's', m.get(f'pscT_free{ch % 2}', 0))
                if ch % 2 == 1:
                    p.w('tensor', 'v', m.get('pscV_free', 0))
                tl = 0
                for tt in range(2):
                    tl = Tn(lambda ch=ch, tt=tt: nc.tensor.transpose(
                        PSC_b[:, 1024 * (ch % 2) + 128 * tt:1024 * (ch % 2) + 128 * tt + 128],
                        xntm[:, KB * tt + 128 * ch:KB * tt + 128 * ch + 128], identb[:]))
                if ch == 7:
                    m['xntm_free'] = tl
                p.w('scalar', 't', tl)
                p.w('scalar', 't', m.get('xnfm_used', 0))
                p.w('scalar', 'd', m.get('xnfm_guard_d', 0))
                ev = S(lambda ch=ch: nc.scalar.copy(
                    xnfm[:, NB * ch:NB * ch + NB],
                    PSC_b[:, 1024 * (ch % 2):1024 * (ch % 2) + NB]))
                m[f'pscT_free{ch % 2}'] = ev
            s_xnfm = p.cnt['s']
            p.w('tensor', 's', s_xnfm)
            p.w('tensor', 'v', m.get('psa_free_l1', 0))
            for mt in range(2):
                for ch in range(8):
                    Tn(lambda mt=mt, ch=ch: nc.tensor.matmul(
                        PSA[:, 512 * mt:512 * mt + NB],
                        wdkv[:, CL * ch + 128 * mt:CL * ch + 128 * mt + 128],
                        xnfm[:, NB * ch:NB * ch + NB], start=(ch == 0), stop=(ch == 7)))
            t_ckv = p.cnt['t']
            for mt in range(2):
                for ch in range(8):
                    Tn(lambda mt=mt, ch=ch: nc.tensor.matmul(
                        PSA[:, 1024 + 512 * mt:1024 + 512 * mt + NB],
                        wuq[:, HCOL * ch + 128 * mt:HCOL * ch + 128 * mt + 128],
                        xnfm[:, NB * ch:NB * ch + NB], start=(ch == 0), stop=(ch == 7)))
            m['xnfm_used'] = p.cnt['t']
            t_q = p.cnt['t']
            p.w('vector', 't', t_ckv)
            p.w('vector', 't', m.get('ckv_used', 0))
            ev_ckv = 0
            for mt in range(2):
                ev_ckv = V(lambda mt=mt: nc.vector.tensor_copy(
                    ckv[:, NB * mt:NB * mt + NB], PSA[:, 512 * mt:512 * mt + NB]))
            p.w('scalar', 't', t_q)
            for mt in range(2):
                S(lambda mt=mt, tok0=tok0: nc.scalar.copy(
                    qfm[mt][:, tok0:tok0 + NB], PSA[:, 1024 + 512 * mt:1024 + 512 * mt + NB]))
            m['psa_free_l1'] = p.cnt['v']
            m['psa_free_l1_s'] = p.cnt['s']
            p.w('tensor', 'v', ev_ckv)
            p.w('tensor', 'v', m.get('psb_free_l1', 0))
            p.w('tensor', 's', m.get('psb_free_l1_s', 0))
            for mt in range(2):
                for ch in range(2):
                    Tn(lambda mt=mt, ch=ch: nc.tensor.matmul(
                        PSB[:, 512 * mt:512 * mt + NB],
                        wuk[:, HCOL * ch + 128 * mt:HCOL * ch + 128 * mt + 128],
                        ckv[:, NB * ch:NB * ch + NB], start=(ch == 0), stop=(ch == 1)))
            t_k = p.cnt['t']
            for ktl in range(2):
                kt = 2 * batch + ktl
                p.w('tensor', 'v', m.get('pscV_free', 0))
                p.w('tensor', 'v', m.get('pscT_free1', 0))
                for ch in range(2):
                    Tn(lambda ktl=ktl, ch=ch: nc.tensor.matmul(
                        PSC[:, 512:512 + 256],
                        ckv[:, NB * ch + 128 * ktl:NB * ch + 128 * ktl + 128],
                        wuv[:, HCOL * ch:HCOL * ch + HCOL], start=(ch == 0), stop=(ch == 1)))
                t_v = p.cnt['t']
                p.w('vector', 't', t_v)
                vv2 = V(lambda kt=kt: nc.vector.tensor_copy(
                    vaug[:, 260 * kt:260 * kt + 260].rearrange("p (h j) -> p h j", j=65)[:, :, 0:64],
                    PSC[:, 512:512 + 256].rearrange("p (h j) -> p h j", j=64)))
                m['pscV_free'] = vv2
            m['ckv_used'] = p.cnt['t']
            p.w('scalar', 't', t_k)
            for mt in range(2):
                S(lambda mt=mt, tok0=tok0: nc.scalar.copy(
                    kfm[mt][:, tok0:tok0 + NB], PSB[:, 512 * mt:512 * mt + NB]))
            m['psb_free_l1_s'] = p.cnt['s']
        l1_v = p.cnt['v']
        l1_s = p.cnt['s']

        p.w('tensor', 'v', l1_v)
        p.w('tensor', 's', l1_s)
        for tt in range(16):
            Dq(lambda tt=tt: sec.dma_start(out=xres[:, KB * tt:KB * tt + KB],
                                           in_=xq_d[128 * tt:128 * tt + 128, :]))
        d_xres = p.cnt['d']
        ccin_d_count = 0
        SC = [PSB[:, 0:512], PSB[:, 512:1024], PSC[:, 0:512], PSC[:, 512:1024]]
        for qb in range(NQB):
            nkt = 4 * (qb + 1)
            for kt in range(nkt):
                diag_r = kt - 4 * qb
                for h in range(HLOC):
                    u, rb0 = h // 2, 64 * (h % 2)
                    p.w('tensor', 's', m.get(f'exp_h{h}', 0))
                    if kt == 0:
                        p.w('tensor', 'v', m.get('norm_v_done', 0))
                    Tn(lambda u=u, rb0=rb0, kt=kt, qb=qb, h=h: nc.tensor.matmul(
                        SC[h],
                        kfm[u][rb0:rb0 + 64, 128 * kt:128 * kt + 128],
                        qfm[u][rb0:rb0 + 64, QB * qb:QB * qb + QB],
                        start=True, stop=True))
                    m[f'score_h{h}'] = p.cnt['t']
                for h in range(HLOC):
                    p.w('scalar', 't', m[f'score_h{h}'])
                    p.w('scalar', 't', m.get(f'pv_h{h}', 0))
                    se = S(lambda h=h: nc.scalar.activation(
                        ebuf[:, 512 * h:512 * h + 512], SC[h], AF.Exp,
                        bias=EXP_BIAS, scale=EXP_SCALE))
                    m[f'exp_h{h}'] = se
                    if diag_r >= 0:
                        p.w('vector', 's', se)
                        mv = V(lambda h=h, diag_r=diag_r: nc.vector.tensor_tensor(
                            ebuf[:, 512 * h:512 * h + 512],
                            ebuf[:, 512 * h:512 * h + 512],
                            masks[:, 512 * diag_r:512 * diag_r + 512],
                            op=Op.mult))
                        m[f'mask_h{h}'] = mv
                for h in range(HLOC):
                    if diag_r >= 0:
                        p.w('tensor', 'v', m[f'mask_h{h}'])
                    else:
                        p.w('tensor', 's', m[f'exp_h{h}'])
                    if kt == 0:
                        p.w('tensor', 'v', m.get('norm_v_done', 0))
                    Tn(lambda h=h, kt=kt, nkt=nkt: nc.tensor.matmul(
                        PSA[0:65, 512 * h:512 * h + 512],
                        vaug[:, 260 * kt + 65 * h:260 * kt + 65 * h + 65],
                        ebuf[:, 512 * h:512 * h + 512],
                        start=(kt == 0), stop=(kt == nkt - 1)))
                    m[f'pv_h{h}'] = p.cnt['t']
            t_pv = p.cnt['t']
            p.w('vector', 'd', m.get('ctxn_free', 0))
            p.w('vector', 't', t_pv)
            p.w('vector', 'v', p.cnt['v'])
            vr2 = V(lambda: nc.vector.reciprocal(rl4[:, :], PSA[64:65, 0:2048]))
            p.w('tensor', 'v', vr2)
            tr = 0
            for h in range(HLOC):
                tr = Tn(lambda h=h: nc.tensor.matmul(
                    SC[h][0:64, :], onesc[:], rl4[:, 512 * h:512 * h + 512],
                    start=True, stop=True))
            p.w('vector', 't', tr)
            V(lambda: nc.vector.tensor_copy(rbuf4[:, 0:1024], PSB[0:64, 0:1024]))
            vR = V(lambda: nc.vector.tensor_copy(rbuf4[:, 1024:2048], PSC[0:64, 0:1024]))
            p.w('vector', 'v', vR)
            for rp in range(2):
                V(lambda rp=rp: nc.vector.tensor_tensor(
                    ctxn[64 * rp:64 * rp + 64, 0:1024].rearrange("p (a n) -> p a n", n=512),
                    PSA[0:64, 0:2048].rearrange("p (a b n) -> p a b n", b=2, n=512)[:, :, rp, :],
                    rbuf4[:, 0:2048].rearrange("p (a b n) -> p a b n", b=2, n=512)[:, :, rp, :],
                    op=Op.mult))
            v_ctxn = p.cnt['v']
            m['norm_v_done'] = v_ctxn
            m['psb_guard_v'] = v_ctxn
            p.w('sync', 'v', v_ctxn)
            for u in range(2):
                ccin_d_count = Dq(lambda u=u, qb=qb: sec.dma_start(
                    out=ccin_d[256 * (qb // 4) + 128 * u:256 * (qb // 4) + 128 * u + 128,
                               QB * (qb % 4):QB * (qb % 4) + QB],
                    in_=ctxn[:, QB * u:QB * u + QB]))
            m['ctxn_free'] = ccin_d_count
            if qb % 4 == 3:
                cq = qb // 4
                p.w('gpsimd', 'd', ccin_d_count)
                p.c(lambda cq=cq: nc.gpsimd.collective_compute(
                    "AllGather", mybir.AluOpType.bypass,
                    replica_groups=[[0, 1, 2, 3], [4, 5, 6, 7]],
                    ins=[ccin_d[256 * cq:256 * cq + 256, :].opt()],
                    outs=[ccout_d[1024 * cq:1024 * cq + 1024, :].opt()]), key=str(4 + cq))

        m['psa_free_wo'] = p.cnt['v']
        m['exp_done1'] = p.cnt['s']
        t_quad_end = p.cnt['t']

        p.w('sync', 't', t_quad_end)
        p.w('sync', 's', m.get('exp_done1', 0))
        p.w('sync', 'v', m.get('norm_v_done', 0))
        wo_v = load_weight_cast(wo_s, wo_d, D, KB)
        for k in '4567':
            p.w('gpsimd', k, 1)
        p.w('gpsimd', 'd', d_gidx)

        for tb in range(4):
            cur = tb % 2
            p.w('gpsimd', 't', t_quad_end)
            p.w('gpsimd', 't', m.get(f'ctxg_used{cur}', 0))
            gl = 0
            for ch in range(8):
                gl = G(lambda ch=ch, tb=tb, cur=cur: nc.gpsimd.indirect_dma_start(
                    out=ctxg[cur][:, QB * ch:QB * ch + QB], out_offset=None,
                    in_=ccout_rows,
                    in_offset=bass.IndirectOffsetOnAxis(ap=gidx[:, 4 * ch + tb:4 * ch + tb + 1], axis=0)))
            p.w('tensor', 'g', gl)
            p.w('tensor', 'v', wo_v)
            p.w('tensor', 'v', m.get('psa_free_wo', 0))
            for sub in range(4):
                slot = sub % 2
                p.w('tensor', 'v', m.get(f'wo_res{slot}', 0))
                for ch in range(8):
                    for hf in range(2):
                        Tn(lambda ch=ch, sub=sub, slot=slot, cur=cur, hf=hf: nc.tensor.matmul(
                            PSA[:, 1024 * slot + 512 * hf:1024 * slot + 512 * hf + 512],
                            ctxg[cur][:, QB * ch + 128 * sub:QB * ch + 128 * sub + 128],
                            wo_s[:, KB * ch + 512 * hf:KB * ch + 512 * hf + 512],
                            start=(ch == 0), stop=(ch == 7)))
                t_wo = p.cnt['t']
                m[f'ctxg_used{cur}'] = t_wo
                p.w('vector', 't', t_wo)
                p.w('vector', 'd', d_xres)
                ti = 4 * tb + sub
                va = V(lambda ti=ti, slot=slot: nc.vector.tensor_tensor(
                    xres[:, KB * ti:KB * ti + KB], xres[:, KB * ti:KB * ti + KB],
                    PSA[:, 1024 * slot:1024 * slot + 1024], op=Op.add))
                m[f'wo_res{slot}'] = va
            m[f'wo_tb_done{tb}'] = p.cnt['v']
        wo_res_v = p.cnt['v']
        t_wo_end = p.cnt['t']
        m['hqtm_used'] = t_wo_end
        m['h_guard_t'] = t_wo_end

        y_store = 0
        m['psa_slot0'] = wo_res_v
        m['psa_slot1'] = wo_res_v
        pendB = [None]

        def emit_passB_res(t_dnB, tiles):
            for j, ti in enumerate(tiles):
                p.w('vector', 't', t_dnB)
                va = V(lambda ti=ti, j=j: nc.vector.tensor_tensor(
                    xres[:, KB * ti:KB * ti + KB], xres[:, KB * ti:KB * ti + KB],
                    PSA[:, 1024 * j:1024 * j + 1024], op=Op.add))
                m[f'psa_slot{j}'] = va
                p.w('sync', 'v', va)
                m['y_store'] = Dq(lambda ti=ti: sec.dma_start(
                    out=y_d[128 * ti:128 * ti + 128, :],
                    in_=xres[:, KB * ti:KB * ti + KB]))

        for g in range(4):
            for pr in range(2):
                p.w('vector', 'v', m[f'wo_tb_done{g}'])
                p.w('scalar', 'v', m[f'wo_tb_done{g}'])
                p.w('vector', 'v', m.get('stat_v_free', 0))
                p.w('scalar', 'v', m.get('stat_v_free', 0))
                s_last = 0
                for t2 in range(2):
                    xsl = xres[:, KB * (4 * g + 2 * pr + t2):KB * (4 * g + 2 * pr + t2) + KB]
                    V(lambda xsl=xsl, t2=t2: nc.vector.tensor_reduce(
                        stat[:, t2:t2 + 1], xsl, axis=AX, op=Op.add))
                    s_last = S(lambda xsl=xsl, t2=t2: nc.scalar.activation(
                        sqscr[:, :], xsl, AF.Square, accum_out=stat[:, 8 + t2:8 + t2 + 1]))
                p.w('vector', 's', s_last)
                p.w('vector', 'v', p.cnt['v'])
                V(lambda: nc.vector.tensor_scalar(stat[:, 0:2], stat[:, 0:2], 1.0 / D, None, op0=Op.mult))
                p.w('vector', 'v', p.cnt['v'])
                V(lambda: nc.vector.tensor_scalar(stat[:, 8:10], stat[:, 8:10], 1.0 / D, None, op0=Op.mult))
                p.w('vector', 'v', p.cnt['v'])
                V(lambda: nc.vector.tensor_tensor(stat[:, 16:18], stat[:, 0:2], stat[:, 0:2], op=Op.mult))
                p.w('vector', 'v', p.cnt['v'])
                vv3 = V(lambda: nc.vector.tensor_tensor(stat[:, 16:18], stat[:, 8:10], stat[:, 16:18], op=Op.subtract))
                p.w('scalar', 'v', vv3)
                sv3 = S(lambda: nc.scalar.activation(stat[:, 20:22], stat[:, 16:18], AF.Sqrt, bias=LN_EPS))
                p.w('vector', 's', sv3)
                vr3 = V(lambda: nc.vector.reciprocal(stat[:, 24:26], stat[:, 20:22]))
                p.w('vector', 'v', vr3)
                for t2 in range(2):
                    xsl = xres[:, KB * (4 * g + 2 * pr + t2):KB * (4 * g + 2 * pr + t2) + KB]
                    V(lambda xsl=xsl, t2=t2: nc.vector.tensor_scalar(
                        xq32[:, KB * t2:KB * t2 + KB], xsl,
                        stat[:, t2:t2 + 1], stat[:, 24 + t2:24 + t2 + 1],
                        op0=Op.subtract, op1=Op.mult))
                m['stat_v_free'] = p.cnt['v']
                if pr == 0 and pendB[0] is not None:
                    emit_passB_res(*pendB[0])
                    pendB[0] = None
                for t2 in range(2):
                    tl = 2 * pr + t2
                    qv = quant(xqtm[:, KB * t2:KB * t2 + KB], xq32[:, KB * t2:KB * t2 + KB],
                               KB, qsF, twait=m.get('xqtm_used', 0))
                    par = m.get('pscpar', 0)
                    p.w('tensor', 'v', qv)
                    p.w('tensor', 'v', m.get(f'pscT{par}', 0))
                    tl_t = 0
                    for ch in range(8):
                        tl_t = Tn(lambda t2=t2, ch=ch, par=par: nc.tensor.transpose(
                            PSC_b[:, 1024 * par + 128 * ch:1024 * par + 128 * ch + 128],
                            xqtm[:, KB * t2 + 128 * ch:KB * t2 + 128 * ch + 128], identb[:]))
                    m['xqtm_used'] = tl_t
                    p.w('vector', 't', tl_t)
                    p.w('vector', 't', m.get('xqfm_used', 0))
                    vxf = V(lambda tl=tl, par=par: nc.vector.tensor_copy(
                        xqfm[:, KB * tl:KB * tl + KB], PSC_b[:, 1024 * par:1024 * par + KB]))
                    m[f'pscT{par}'] = vxf
                    m[f'xqfm_rdy{tl}'] = vxf
                    m['pscpar'] = par ^ 1
            for si in range(6):
                wslice = min(512, II - 512 * si)
                cur = si % 2
                pcw = si // 2
                p.w('sync', '1', 1)
                p.w('sync', 't', m.get(f'wgu_used{cur}', 0))
                p.w('gpsimd', '2', 1)
                p.w('gpsimd', 't', m.get(f'wgu_used{cur}', 0))
                dgu = 0
                ggu = 0
                for wch in range(8):
                    r0w = 352 * wch + 64 * si
                    nrw = wslice // 8
                    dgu = Dq(lambda r0w=r0w, nrw=nrw, wch=wch, cur=cur, wslice=wslice: sec.dma_start(
                        out=wgu[2 * cur][:, QB * wch:QB * wch + wslice],
                        in_=wgT_d[r0w:r0w + nrw, :].rearrange("r (a c) -> (r a) c", c=wslice)))
                    ggu = G(lambda r0w=r0w, nrw=nrw, wch=wch, cur=cur, wslice=wslice: sec.dma_start(
                        out=wgu[2 * cur + 1][:, QB * wch:QB * wch + wslice],
                        in_=wuT_d[r0w:r0w + nrw, :].rearrange("r (a c) -> (r a) c", c=wslice)))
                p.w('tensor', 'd', dgu)
                p.w('tensor', 'g', ggu)
                p.w('tensor', 'v', m.get('psb_guard_v', 0))
                for tl in range(4):
                    pp = tl % 2
                    gb = PSB[:, 0:wslice] if pp == 0 else PSA[:, 0:wslice]
                    ub = PSB[:, 512:512 + wslice] if pp == 0 else PSA[:, 512:512 + wslice]
                    p.w('tensor', 'v', m[f'xqfm_rdy{tl}'])
                    if pp == 1:
                        p.w('tensor', 'v', m.get('psa_slot0', 0))
                    p.w('tensor', 'v', m.get(f'h_done{pp}', 0))
                    p.w('tensor', 's', m.get(f'silu_done{pp}', 0))
                    for ch in range(8):
                        Tn(lambda tl=tl, ch=ch, cur=cur, wslice=wslice, gb=gb: nc.tensor.matmul(
                            gb,
                            xqfm[:, 1024 * tl + 128 * ch:1024 * tl + 128 * ch + 128],
                            wgu[2 * cur][:, QB * ch:QB * ch + wslice],
                            start=(ch == 0), stop=(ch == 7)))
                    t_g = p.cnt['t']
                    for ch in range(8):
                        Tn(lambda tl=tl, ch=ch, cur=cur, wslice=wslice, ub=ub: nc.tensor.matmul(
                            ub,
                            xqfm[:, 1024 * tl + 128 * ch:1024 * tl + 128 * ch + 128],
                            wgu[2 * cur + 1][:, QB * ch:QB * ch + wslice],
                            start=(ch == 0), stop=(ch == 7)))
                    t_u = p.cnt['t']
                    m[f'wgu_used{cur}'] = t_u
                    m['xqfm_used'] = t_u
                    p.w('scalar', 't', t_g)
                    p.w('scalar', 'v', m.get(f'h_done{pp}', 0))
                    ssl = S(lambda wslice=wslice, pp=pp, gb=gb: nc.scalar.activation(
                        silu_s[:, 512 * pp:512 * pp + wslice], gb, AF.Silu))
                    m[f'silu_done{pp}'] = ssl
                    p.w('vector', 't', t_u)
                    p.w('vector', 't', m.get('h_guard_t', 0))
                    p.w('vector', 's', ssl)
                    hv = V(lambda tl=tl, si=si, wslice=wslice, pp=pp, ub=ub:
                           nc.vector.tensor_tensor(
                        hbuf[:, 1024 * tl + 512 * (si % 2):
                             1024 * tl + 512 * (si % 2) + wslice],
                        ub, silu_s[:, 512 * pp:512 * pp + wslice], op=Op.mult))
                    m[f'h_done{pp}'] = hv
                if si % 2 == 1 or si == 5:
                    pc = si // 2
                    w = 1024 if pc < 2 else 768
                    nch = w // 128
                    c0 = 1024 * pc
                    for tl in range(4):
                        qv = quant(hqtm[:, c0:c0 + w],
                                   hbuf[:, 1024 * tl:1024 * tl + w],
                                   w, qsF, twait=m.get('hqtm_used', 0))
                        par = m.get('pscpar', 0)
                        p.w('tensor', 'v', qv)
                        p.w('tensor', 'v', m.get(f'pscT{par}', 0))
                        tl_t = 0
                        for j in range(nch):
                            tl_t = Tn(lambda c0=c0, j=j, par=par: nc.tensor.transpose(
                                PSC_b[:, 1024 * par + 128 * j:1024 * par + 128 * j + 128],
                                hqtm[:, c0 + 128 * j:c0 + 128 * j + 128], identb[:]))
                        m['hqtm_used'] = tl_t
                        p.w('vector', 't', tl_t)
                        p.w('vector', 't', m.get('hqfm_used', 0))
                        hv2 = V(lambda tl=tl, pc=pc, nch=nch, par=par: nc.vector.tensor_copy(
                            hqfm_v[:, 8 * pc:8 * pc + nch, 128 * tl:128 * tl + 128],
                            PSC_b[:, 1024 * par:1024 * par + 128 * nch].rearrange("p (c i) -> p c i", i=128)))
                        m[f'pscT{par}'] = hv2
                        m[f'hqfm_rdy_{tl}_{pc}'] = hv2
                        m['pscpar'] = par ^ 1
            for half in range(2):
                for ch in range(22):
                    pc = min(ch // 8, 2)
                    slot = m.get('wdnslot', 0)
                    m['wdnslot'] = (slot + 1) % 6
                    qi = slot % 3
                    qk = ('d', 'g', 'e')[qi]
                    qeng = ('sync', 'gpsimd', 'scalar')[qi]
                    p.w(qeng, '3', 1)
                    p.w(qeng, 't', t_wo_end)
                    p.w(qeng, 't', m.get(f'wdn_used{slot}', 0))
                    fn = (lambda ch=ch, slot=slot: sec.dma_start(
                        out=wdnR[slot][:, :], in_=wdT_d[128 * ch:128 * ch + 128, :]))
                    dwd = {'d': Dq, 'g': G, 'e': p.e}[qk](fn)
                    p.w('tensor', qk, dwd)
                    for j in range(2):
                        tl = 2 * half + j
                        p.w('tensor', 'v', m[f'hqfm_rdy_{tl}_{pc}'])
                        if ch == 0:
                            p.w('tensor', 'v', m[f'psa_slot{j}'])
                            p.w('tensor', 'v', m.get('h_done1', 0))
                            p.w('tensor', 's', m.get('silu_done1', 0))
                        for hf in range(2):
                            Tn(lambda ch=ch, j=j, tl=tl, slot=slot, hf=hf: nc.tensor.matmul(
                                PSA[:, 1024 * j + 512 * hf:1024 * j + 512 * hf + 512],
                                hqfm_v[:, ch, 128 * tl:128 * tl + 128],
                                wdnR[slot][:, 512 * hf:512 * hf + 512],
                                start=(ch == 0), stop=(ch == 21)))
                    m[f'wdn_used{slot}'] = p.cnt['t']
                t_dn = p.cnt['t']
                if half == 0:
                    for j in range(2):
                        ti = 4 * g + j
                        p.w('vector', 't', t_dn)
                        va = V(lambda ti=ti, j=j: nc.vector.tensor_tensor(
                            xres[:, KB * ti:KB * ti + KB], xres[:, KB * ti:KB * ti + KB],
                            PSA[:, 1024 * j:1024 * j + 1024], op=Op.add))
                        m[f'psa_slot{j}'] = va
                        p.w('sync', 'v', va)
                        m['y_store'] = Dq(lambda ti=ti: sec.dma_start(
                            out=y_d[128 * ti:128 * ti + 128, :],
                            in_=xres[:, KB * ti:KB * ti + KB]))
            m['hqfm_used'] = p.cnt['t']
            pendB[0] = (p.cnt['t'], [4 * g + 2, 4 * g + 3])
        emit_passB_res(*pendB[0])
        pendB[0] = None
        y_store = m['y_store']

        p.w('sync', 'd', y_store)

    with nc.Block() as block:
        @block.sync
        def _(sec):
            program(P(nc, 'sync', sec, sems), sec)

        @block.vector
        def _(sec):
            program(P(nc, 'vector', sec, sems), sec)

        @block.scalar
        def _(sec):
            program(P(nc, 'scalar', sec, sems), sec)

        @block.tensor
        def _(sec):
            program(P(nc, 'tensor', sec, sems), sec)

        @block.gpsimd
        def _(sec):
            program(P(nc, 'gpsimd', sec, sems), sec)

    return nc


_NC_CACHE = None


def _consts():
    mm = np.zeros((128, 4, 512), np.float32)
    kk = np.arange(128)[:, None]
    qq = np.arange(512)[None, :]
    for r in range(4):
        mm[:, r, :] = (qq >= kk + 128 * r)
    masks = np.ascontiguousarray(mm.reshape(128, 2048)).astype(ml_dtypes.bfloat16)
    identb = np.eye(128, dtype=ml_dtypes.bfloat16)
    cvals = np.tile(np.array([[EXP_SCALE, EXP_BIAS, LN_EPS, 0.0]], np.float32), (128, 1))
    return identb, masks, cvals


def kernel(L0_attn_x, L0_attn_W_DKV, L0_attn_W_UK, L0_attn_W_UV, L0_attn_W_UQ,
           L0_attn_W_O, L0_ffn_w_gate, L0_ffn_w_up, L0_ffn_w_down):
    global _NC_CACHE
    x = np.ascontiguousarray(np.asarray(L0_attn_x, np.float32))
    W_DKV = np.ascontiguousarray(np.asarray(L0_attn_W_DKV, np.float32))
    W_UK = np.ascontiguousarray(np.asarray(L0_attn_W_UK, np.float32))
    W_UV = np.ascontiguousarray(np.asarray(L0_attn_W_UV, np.float32))
    W_UQ = np.ascontiguousarray(np.asarray(L0_attn_W_UQ, np.float32))
    W_O = np.ascontiguousarray(np.asarray(L0_attn_W_O, np.float32))
    wg = np.ascontiguousarray(np.asarray(L0_ffn_w_gate, np.float32))
    wu = np.ascontiguousarray(np.asarray(L0_ffn_w_up, np.float32))
    wd = np.ascontiguousarray(np.asarray(L0_ffn_w_down, np.float32))

    identb, masks, cvals = _consts()
    if _NC_CACHE is None:
        _NC_CACHE = build_nc()
    nc = _NC_CACHE

    in_maps = []
    pidx = np.arange(128, dtype=np.uint32)
    for c in range(8):
        b, hq = c // 4, c % 4
        hs = slice(HCOL * hq, HCOL * hq + HCOL)
        gidx = np.zeros((128, 64), np.uint32)
        for ch in range(8):
            for tb in range(4):
                gidx[:, 4 * ch + tb] = 4096 * hq + (128 * ch + pidx) * 4 + tb
        wdsh = np.zeros((1024, 384), np.float32)
        lo = 384 * c
        hi = min(lo + 384, II)
        wdsh[:, 0:hi - lo] = wd[:, lo:hi]
        in_maps.append({
            "x": x[b],
            "xquarter": np.ascontiguousarray(x[b, 2048 * hq:2048 * hq + 2048]),
            "wdkv": W_DKV,
            "wuq": np.ascontiguousarray(W_UQ[:, hs]),
            "wuk": np.ascontiguousarray(W_UK[:, hs]),
            "wuv": np.ascontiguousarray(W_UV[:, hs]),
            "wo": W_O,
            "wgsh": np.ascontiguousarray(wg[:, 128 * c:128 * c + 128]),
            "wush": np.ascontiguousarray(wu[:, 128 * c:128 * c + 128]),
            "wdsh": wdsh,
            "identb": identb, "masks": masks, "cvals": cvals,
            "gidx": gidx,
        })

    res = run_bass_kernel_spmd(nc, in_maps, core_ids=list(range(8)))
    out = np.empty((2, T, D), np.float32)
    for c in range(8):
        b, hq = c // 4, c % 4
        out[b, 2048 * hq:2048 * hq + 2048] = np.asarray(res.results[c]["y"])
    return out


# version 22
# speedup vs baseline: 1.1398x; 1.1392x over previous
DEBUG_TAPS = False

import numpy as np
import ml_dtypes
import concourse.bass as bass
import concourse.mybir as mybir
from concourse.bass_utils import run_bass_kernel_spmd
from concourse.alu_op_type import AluOpType

F32 = mybir.dt.float32
BF16 = mybir.dt.bfloat16
I32 = mybir.dt.int32
U32 = mybir.dt.uint32
AF = mybir.ActivationFunctionType
AX = mybir.AxisListType.X
Op = AluOpType

T = 8192
D = 1024
CL = 256
HLOC = 4
HCOL = 256
II = 2816
QB = 512
NQB = 16
NKT = 64
KB = 1024
EXP_SCALE = 0.125
EXP_BIAS = -3.0
LN_EPS = 1e-5

ENGS = ('tensor', 'vector', 'scalar', 'gpsimd', 'sync')


class P:

    def __init__(self, nc, active, eng, sems):
        self.nc = nc
        self.active = active
        self.eng = eng
        self.sems = sems
        self.cnt = dict.fromkeys("tvsgdce1234567", 0)
        self.wm = {e: dict.fromkeys("tvsgdce1234567", 0) for e in ENGS}

    def w(self, tgt, key, count):
        if not count or count <= self.wm[tgt][key]:
            return
        self.wm[tgt][key] = count
        if self.active == tgt:
            self.eng.wait_ge(self.sems[key], count)

    def _do(self, key, inc, fn, act):
        self.cnt[key] += inc
        if self.active == act:
            fn().then_inc(self.sems[key], inc)
        return self.cnt[key]

    def t(self, fn):
        return self._do('t', 1, fn, 'tensor')

    def v(self, fn):
        return self._do('v', 1, fn, 'vector')

    def s(self, fn):
        return self._do('s', 1, fn, 'scalar')

    def g(self, fn):
        return self._do('g', 16, fn, 'gpsimd')

    def d(self, fn):
        return self._do('d', 16, fn, 'sync')

    def e(self, fn):
        return self._do('e', 16, fn, 'scalar')

    def c(self, fn, key='c'):
        self.cnt[key] += 1
        if self.active == 'gpsimd':
            fn().then_inc(self.sems[key])
        return self.cnt[key]


def build_nc():
    nc = bass.Bass("TRN2", target_bir_lowering=False)

    x_d = nc.dram_tensor("x", [T, D], F32, kind="ExternalInput").ap()
    xq_d = nc.dram_tensor("xquarter", [2048, D], F32, kind="ExternalInput").ap()
    wdkv_d = nc.dram_tensor("wdkv", [D, CL], F32, kind="ExternalInput").ap()
    wuq_d = nc.dram_tensor("wuq", [D, HCOL], F32, kind="ExternalInput").ap()
    wuk_d = nc.dram_tensor("wuk", [CL, HCOL], F32, kind="ExternalInput").ap()
    wuv_d = nc.dram_tensor("wuv", [CL, HCOL], F32, kind="ExternalInput").ap()
    wo_d = nc.dram_tensor("wo", [D, D], F32, kind="ExternalInput").ap()
    wgsh_d = nc.dram_tensor("wgsh", [II, 128], F32, kind="ExternalInput").ap()
    wush_d = nc.dram_tensor("wush", [II, 128], F32, kind="ExternalInput").ap()
    wdsh_d = nc.dram_tensor("wdsh", [D, 384], F32, kind="ExternalInput").ap()
    identb_d = nc.dram_tensor("identb", [128, 128], BF16, kind="ExternalInput").ap()
    masks_d = nc.dram_tensor("masks", [128, 2048], BF16, kind="ExternalInput").ap()
    cvals_d = nc.dram_tensor("cvals", [128, 4], F32, kind="ExternalInput").ap()
    gidx_d = nc.dram_tensor("gidx", [128, 64], U32, kind="ExternalInput").ap()
    y_d = nc.dram_tensor("y", [2048, D], F32, kind="ExternalOutput").ap()
    y2_d = nc.dram_tensor("y2", [2048, D], F32, kind="ExternalOutput").ap() if DEBUG_TAPS else None
    y3_d = nc.dram_tensor("y3", [2048, D], BF16, kind="ExternalOutput").ap() if DEBUG_TAPS else None
    y4_d = nc.dram_tensor("y4", [2048, II], BF16, kind="ExternalOutput").ap() if DEBUG_TAPS else None
    wgin_d = nc.dram_tensor("wgin", [128, II], BF16).ap()
    wuin_d = nc.dram_tensor("wuin", [128, II], BF16).ap()
    wdin_d = nc.dram_tensor("wdin", [384, D], BF16).ap()
    wgT_d = nc.dram_tensor("wgT", [D, II], BF16, addr_space="Shared").ap()
    wuT_d = nc.dram_tensor("wuT", [D, II], BF16, addr_space="Shared").ap()
    wdT_d = nc.dram_tensor("wdT", [3072, D], BF16, addr_space="Shared").ap()
    ccin_d = nc.dram_tensor("ccin", [4 * HCOL, 2048], BF16).ap()
    ccout_d = nc.dram_tensor("ccout", [4096, 2048], BF16).ap()
    ccout_rows = ccout_d.rearrange("r (c n) -> (r c) n", n=QB)

    def sb(name, shape, dt):
        return nc.alloc_sbuf_tensor("s_" + name, shape, dt).ap()

    xres = sb("xres", [128, 16 * KB], F32)
    xres_v = xres.rearrange("p (tt k) -> p tt k", k=KB)
    identb = sb("identb", [128, 128], BF16)
    cvals = sb("cvals", [128, 4], F32)
    nc.const_aps.register(F32, EXP_SCALE, cvals[:, 0:1])
    nc.const_aps.register(F32, EXP_BIAS, cvals[:, 1:2])
    nc.const_aps.register(F32, LN_EPS, cvals[:, 2:3])
    gidx = sb("gidx", [128, 64], U32)
    stat = sb("stat", [128, 32], F32)
    sqscr = sb("sqscr", [128, KB], F32)
    onesc = sb("onesc", [1, 64], F32)

    from concourse.bass_primitives import MemorySpace
    AB = (nc.bytes_allocated(MemorySpace.SBUF) + 511) & ~511
    _names = [0]

    def at(off, shape, dt):
        _names[0] += 1
        return nc.alloc_sbuf_tensor_at(f"a_{_names[0]}", shape, dt, offset=AB + off).ap()

    qfm = [at(16384 * u, [128, T], BF16) for u in range(2)]
    kfm = [at(32768 + 16384 * u, [128, T], BF16) for u in range(2)]
    vaug = at(65536, [128, NKT * 260], BF16)
    wdkv = at(98816, [128, 8 * CL], BF16)
    wuq = at(102912, [128, 8 * HCOL], BF16)
    wuk = at(107008, [128, 2 * HCOL], BF16)
    wuv = at(108032, [128, 2 * HCOL], BF16)
    ebuf = at(109056, [128, 4 * KB], BF16)
    masks = at(98816, [128, 2048], BF16)
    rl4 = at(117248, [1, 4 * QB], F32)
    rbuf4 = at(125440, [64, 4 * QB], F32)
    xtile = at(121344, [128, 2 * KB], F32)
    stage = at(121344, [128, KB], F32)
    stageW = at(121344, [128, 2 * KB], F32)
    wcast = at(129536, [128, KB], BF16)
    wcastW = at(129536, [128, 2 * KB], BF16)
    xntm = at(129536, [128, 2 * KB], BF16)
    ttmp = at(133632, [128, KB], BF16)
    ttmpW = at(133632, [128, 2 * KB], BF16)
    xnfm = at(133632, [128, 8 * 256], BF16)
    ckv = at(137728, [128, 2 * 256], BF16)
    ctxn = at(138752, [128, 2 * QB], BF16)
    ARENA_ATTN_END = 140800

    wo_s = at(0, [128, 8 * KB], BF16)
    ctxg = [at(16384 + 8192 * i, [128, 8 * QB], BF16) for i in range(2)]

    hbuf = at(0, [128, 4 * KB], F32)
    wdnR = [at(16384 + 2048 * i, [128, KB], BF16) for i in range(6)]
    hqtm = at(36864, [128, II], BF16)
    wgu = [at(43008 + 8192 * i, [128, 8 * QB], BF16) for i in range(4)]
    hqfm = at(75776, [128, 22 * 512], BF16)
    hqfm_v = hqfm.rearrange("p (c n) -> p c n", n=512)
    xq32 = at(98304, [128, 2 * KB], F32)
    xqtm = at(106496, [128, 2 * KB], BF16)
    xqfm = at(110592, [128, 4 * KB], BF16)
    qsP = tuple([at(4096 * i, [128, 1024], F32) for i in range(4)]
                + [at(12288, [128, 1024], I32), at(16384, [128, 192], F32),
                   at(17152, [128, 64], F32)])
    qsF = tuple([at(118784 + 4096 * i, [128, 1024], F32) for i in range(4)]
                + [at(131072, [128, 1024], I32), at(135168, [128, 192], F32),
                   at(135936, [128, 64], F32)])
    silu_s = at(136192, [128, 2 * QB], F32)
    assert AB + ARENA_ATTN_END <= 229300, (AB, ARENA_ATTN_END)
    assert AB + 140800 <= 229376, AB

    PSA = nc.alloc_psum_tensor("PSA", [128, 2048], F32).ap()
    PSB = nc.alloc_psum_tensor("PSB", [128, 1024], F32).ap()
    PSC = nc.alloc_psum_tensor("PSC", [128, 1024], F32).ap()
    PSB_b = PSB.bitcast(BF16)
    PSC_b = PSC.bitcast(BF16)

    sems = {}
    for k in "tvsgdce1234567":
        sems[k] = nc.alloc_semaphore(name="sem_" + k)

    def program(p: P, sec):
        V, S, Tn, G, Dq = p.v, p.s, p.t, p.g, p.d
        m = {}

        Dq(lambda: sec.dma_start(out=identb[:], in_=identb_d))
        Dq(lambda: sec.dma_start(out=cvals[:], in_=cvals_d))
        d_gidx = Dq(lambda: sec.dma_start(out=gidx[:], in_=gidx_d))

        def load_weight_cast(dst, src_d, rows, cols):
            last = 0
            for ch in range(rows // 128):
                p.w('sync', 'v', m.get('stage_free', 0))
                dd = Dq(lambda ch=ch: sec.dma_start(out=stage[:, 0:cols],
                                                    in_=src_d[128 * ch:128 * ch + 128, :]))
                p.w('vector', 'd', dd)
                last = V(lambda ch=ch: nc.vector.tensor_copy(
                    dst[:, cols * ch:cols * ch + cols], stage[:, 0:cols]))
                m['stage_free'] = last
            return last

        def quant(dst, src, width, bufs, twait=0):
            q0f, q1f, q2f, q3f, qsif, qsmf, qsm2f = bufs
            nb = width // 16
            p.w('vector', 'v', p.cnt['v'])
            p.w('vector', 't', twait)
            src3 = src.rearrange("p (b k) -> p b k", k=16)
            dst3 = dst.rearrange("p (b k) -> p b k", k=16)

            def vs(fn):
                r = V(fn)
                p.w('vector', 'v', r)
                return r

            q0, q1, q2, q3 = (q[:, 0:width] for q in (q0f, q1f, q2f, q3f))
            qsi_ = qsif[:, 0:width]
            q03 = q0.rearrange("p (b k) -> p b k", k=16)
            q13 = q1.rearrange("p (b k) -> p b k", k=16)
            vs(lambda: nc.vector.tensor_reduce(qsmf[:, 0:nb], src3, axis=AX, op=Op.max,
                                               apply_absolute_value=True))
            vs(lambda: nc.vector.tensor_scalar(qsmf[:, 64:64 + nb], qsmf[:, 0:nb],
                                               1.0 / 3.0, 1e-12, op0=Op.mult, op1=Op.max))
            vs(lambda: nc.vector.tensor_scalar(qsmf[:, 128:128 + nb], qsmf[:, 0:nb],
                                               1.0 / 12.0, None, op0=Op.mult))
            vs(lambda: nc.vector.reciprocal(qsm2f[:, 0:nb], qsmf[:, 64:64 + nb]))
            binvh = qsm2f[:, 0:nb].broadcast_to([128, nb, 16])
            bsc12 = qsmf[:, 128:128 + nb].broadcast_to([128, nb, 16])
            v0 = vs(lambda: nc.vector.tensor_tensor(q03, src3, binvh, op=Op.mult))
            p.w('scalar', 'v', v0)
            sq = S(lambda: nc.scalar.activation(q1, q0, AF.Square))
            p.w('vector', 's', sq)
            vs(lambda: nc.vector.tensor_scalar(q2, q1, 1.0, 1.0, op0=Op.is_lt, op1=Op.add))
            vs(lambda: nc.vector.tensor_scalar(q3, q1, 4.0, 1.0, op0=Op.is_lt, op1=Op.add))
            vs(lambda: nc.vector.tensor_tensor(q2, q2, q3, op=Op.mult))
            vs(lambda: nc.vector.tensor_tensor(qsi_, q0, q2, op=Op.mult))
            vs(lambda: nc.vector.tensor_copy(q0, qsi_))
            vs(lambda: nc.vector.tensor_scalar(q2, q1, 1.0, 1.0, op0=Op.is_ge, op1=Op.add))
            vs(lambda: nc.vector.tensor_scalar(q3, q1, 4.0, 1.0, op0=Op.is_ge, op1=Op.add))
            vs(lambda: nc.vector.tensor_tensor(q2, q2, q3, op=Op.mult))
            vs(lambda: nc.vector.tensor_tensor(q1, q0, q2, op=Op.mult))
            return vs(lambda: nc.vector.tensor_tensor(dst3, q13, bsc12, op=Op.mult))



        def prep_col(src_d, in_d, rows):
            for r0 in range(0, rows, 1024):
                rlen = min(1024, rows - r0)
                nch = rlen // 128
                p.w('sync', 'v', m.get('stage_free', 0))
                dd = Dq(lambda src_d=src_d, r0=r0, rlen=rlen: sec.dma_start(
                    out=stageW[:, 0:rlen].rearrange("p (a c) -> p a c", c=128),
                    in_=src_d[r0:r0 + rlen, :].rearrange("(a p) c -> p a c", p=128)))
                p.w('vector', 'd', dd)
                cv = quant(wcastW[:, 0:rlen], stageW[:, 0:rlen], rlen, qsP,
                           twait=m.get('wprep_t', 0))
                m['stage_free'] = cv
                p.w('tensor', 'v', cv)
                p.w('tensor', 'v', m.get('psbT_free', 0))
                p.w('tensor', 'd', 16)
                tl = 0
                for j in range(nch):
                    tl = Tn(lambda j=j: nc.tensor.transpose(
                        PSB_b[:, 128 * j:128 * j + 128],
                        wcastW[:, 128 * j:128 * j + 128], identb[:]))
                m['wprep_t'] = tl
                p.w('vector', 't', tl)
                p.w('vector', 'd', m.get('ttmp_free', 0))
                ev = V(lambda rlen=rlen: nc.vector.tensor_copy(
                    ttmpW[:, 0:rlen], PSB_b[:, 0:rlen]))
                m['psbT_free'] = ev
                p.w('sync', 'v', ev)
                m['ttmp_free'] = Dq(lambda in_d=in_d, r0=r0, rlen=rlen: sec.dma_start(
                    out=in_d[:, r0:r0 + rlen], in_=ttmpW[:, 0:rlen]))
            return m['ttmp_free']

        def prep_wd():
            for g in range(2):
                r0 = 512 * g
                p.w('sync', 'v', m.get('stage_free', 0))
                dd = Dq(lambda r0=r0: sec.dma_start(
                    out=stageW[:, 0:1536].rearrange("p (a c) -> p a c", c=384),
                    in_=wdsh_d[r0:r0 + 512, :].rearrange("(a p) c -> p a c", p=128)))
                p.w('vector', 'd', dd)
                cv = 0
                for off in range(0, 1536, 768):
                    cv = quant(wcastW[:, off:off + 768], stageW[:, off:off + 768],
                               768, qsP, twait=m.get('wprep_t', 0))
                m['stage_free'] = cv
                p.w('tensor', 'v', cv)
                p.w('tensor', 'v', m.get('psbT_free', 0))
                p.w('tensor', 'd', 16)
                tl = 0
                for cb in range(3):
                    for a in range(4):
                        tl = Tn(lambda cb=cb, a=a: nc.tensor.transpose(
                            PSB_b[:, 512 * cb + 128 * a:512 * cb + 128 * a + 128],
                            wcastW[:, 384 * a + 128 * cb:384 * a + 128 * cb + 128],
                            identb[:]))
                m['wprep_t'] = tl
                p.w('vector', 't', tl)
                p.w('vector', 'd', m.get('ttmp_free', 0))
                ev = V(lambda: nc.vector.tensor_copy(ttmpW[:, 0:1536], PSB_b[:, 0:1536]))
                m['psbT_free'] = ev
                p.w('sync', 'v', ev)
                for cb in range(3):
                    m['ttmp_free'] = Dq(lambda cb=cb, r0=r0: sec.dma_start(
                        out=wdin_d[128 * cb:128 * cb + 128, r0:r0 + 512],
                        in_=ttmpW[:, 512 * cb:512 * cb + 512]))
            return m['ttmp_free']

        G8 = [[0, 1, 2, 3, 4, 5, 6, 7]]
        d_wg = prep_col(wgsh_d, wgin_d, II)
        p.w('gpsimd', 'd', d_wg)
        p.c(lambda: nc.gpsimd.collective_compute(
            "AllGather", mybir.AluOpType.bypass, replica_groups=G8,
            ins=[wgin_d.opt()], outs=[wgT_d.opt()]), key='1')
        d_wu = prep_col(wush_d, wuin_d, II)
        p.w('gpsimd', 'd', d_wu)
        p.c(lambda: nc.gpsimd.collective_compute(
            "AllGather", mybir.AluOpType.bypass, replica_groups=G8,
            ins=[wuin_d.opt()], outs=[wuT_d.opt()]), key='2')
        d_wd = prep_wd()
        p.w('gpsimd', 'd', d_wd)
        p.c(lambda: nc.gpsimd.collective_compute(
            "AllGather", mybir.AluOpType.bypass, replica_groups=G8,
            ins=[wdin_d.opt()], outs=[wdT_d.opt()]), key='3')
        load_weight_cast(wdkv, wdkv_d, D, CL)
        load_weight_cast(wuq, wuq_d, D, HCOL)
        load_weight_cast(wuk, wuk_d, CL, HCOL)
        w_attn_v = load_weight_cast(wuv, wuv_d, CL, HCOL)
        V(lambda: nc.vector.memset(onesc[:], 1.0))
        v_init = V(lambda: nc.vector.memset(vaug[:, 64::65], 1.0))

        m['psb_free_l1'] = m.get('psbT_free', 0)

        m['xntm_free'] = p.cnt['t']
        m['xnfm_guard_d'] = m.get('ttmp_free', 0)
        p.w('tensor', 'v', w_attn_v)
        p.w('tensor', 'v', v_init)
        NB = 256
        for batch in range(32):
            tok0 = batch * NB
            dl = []
            for tt in range(2):
                p.w('sync', 'v', m.get(f'xtile_free{tt}', 0))
                p.w('sync', 's', m.get(f'xtile_free_s{tt}', 0))
                p.w('sync', 'v', m.get('stage_free', 0))
                dl.append(Dq(lambda tt=tt, tok0=tok0: sec.dma_start(
                    out=xtile[:, KB * tt:KB * tt + KB],
                    in_=x_d[tok0 + 128 * tt:tok0 + 128 * tt + 128, :])))
            p.w('vector', 'v', m.get('stat_v_free', 0))
            p.w('scalar', 'v', m.get('stat_v_free', 0))
            s_last = 0
            for tt in range(2):
                p.w('vector', 'd', dl[tt])
                p.w('scalar', 'd', dl[tt])
                V(lambda tt=tt: nc.vector.tensor_reduce(
                    stat[:, tt:tt + 1], xtile[:, KB * tt:KB * tt + KB], axis=AX, op=Op.add))
                s_last = S(lambda tt=tt: nc.scalar.activation(
                    sqscr[:, :], xtile[:, KB * tt:KB * tt + KB], AF.Square,
                    accum_out=stat[:, 8 + tt:8 + tt + 1]))
                m[f'xtile_free_s{tt}'] = s_last
            p.w('vector', 's', s_last)
            p.w('vector', 'v', p.cnt['v'])
            V(lambda: nc.vector.tensor_scalar(stat[:, 0:2], stat[:, 0:2], 1.0 / D, None, op0=Op.mult))
            p.w('vector', 'v', p.cnt['v'])
            V(lambda: nc.vector.tensor_scalar(stat[:, 8:10], stat[:, 8:10], 1.0 / D, None, op0=Op.mult))
            p.w('vector', 'v', p.cnt['v'])
            V(lambda: nc.vector.tensor_tensor(stat[:, 16:18], stat[:, 0:2], stat[:, 0:2], op=Op.mult))
            p.w('vector', 'v', p.cnt['v'])
            vvar = V(lambda: nc.vector.tensor_tensor(stat[:, 16:18], stat[:, 8:10], stat[:, 16:18], op=Op.subtract))
            p.w('scalar', 'v', vvar)
            svar = S(lambda: nc.scalar.activation(stat[:, 20:22], stat[:, 16:18], AF.Sqrt, bias=LN_EPS))
            p.w('vector', 's', svar)
            vrs = V(lambda: nc.vector.reciprocal(stat[:, 24:26], stat[:, 20:22]))
            p.w('vector', 'v', vrs)
            p.w('vector', 't', m.get('xntm_free', 0))
            for tt in range(2):
                va = V(lambda tt=tt: nc.vector.tensor_scalar(
                    xntm[:, KB * tt:KB * tt + KB], xtile[:, KB * tt:KB * tt + KB],
                    stat[:, tt:tt + 1], stat[:, 24 + tt:24 + tt + 1],
                    op0=Op.subtract, op1=Op.mult))
                m[f'xtile_free{tt}'] = va
            m['stat_v_free'] = p.cnt['v']
            v_apply = p.cnt['v']
            p.w('tensor', 'v', v_apply)
            for ch in range(8):
                p.w('tensor', 's', m.get(f'pscT_free{ch % 2}', 0))
                if ch % 2 == 1:
                    p.w('tensor', 'v', m.get('pscV_free', 0))
                tl = 0
                for tt in range(2):
                    tl = Tn(lambda ch=ch, tt=tt: nc.tensor.transpose(
                        PSC_b[:, 1024 * (ch % 2) + 128 * tt:1024 * (ch % 2) + 128 * tt + 128],
                        xntm[:, KB * tt + 128 * ch:KB * tt + 128 * ch + 128], identb[:]))
                if ch == 7:
                    m['xntm_free'] = tl
                p.w('scalar', 't', tl)
                p.w('scalar', 't', m.get('xnfm_used', 0))
                p.w('scalar', 'd', m.get('xnfm_guard_d', 0))
                ev = S(lambda ch=ch: nc.scalar.copy(
                    xnfm[:, NB * ch:NB * ch + NB],
                    PSC_b[:, 1024 * (ch % 2):1024 * (ch % 2) + NB]))
                m[f'pscT_free{ch % 2}'] = ev
            s_xnfm = p.cnt['s']
            p.w('tensor', 's', s_xnfm)
            p.w('tensor', 'v', m.get('psa_free_l1', 0))
            for mt in range(2):
                for ch in range(8):
                    Tn(lambda mt=mt, ch=ch: nc.tensor.matmul(
                        PSA[:, 512 * mt:512 * mt + NB],
                        wdkv[:, CL * ch + 128 * mt:CL * ch + 128 * mt + 128],
                        xnfm[:, NB * ch:NB * ch + NB], start=(ch == 0), stop=(ch == 7)))
            t_ckv = p.cnt['t']
            for mt in range(2):
                for ch in range(8):
                    Tn(lambda mt=mt, ch=ch: nc.tensor.matmul(
                        PSA[:, 1024 + 512 * mt:1024 + 512 * mt + NB],
                        wuq[:, HCOL * ch + 128 * mt:HCOL * ch + 128 * mt + 128],
                        xnfm[:, NB * ch:NB * ch + NB], start=(ch == 0), stop=(ch == 7)))
            m['xnfm_used'] = p.cnt['t']
            t_q = p.cnt['t']
            p.w('vector', 't', t_ckv)
            p.w('vector', 't', m.get('ckv_used', 0))
            ev_ckv = 0
            for mt in range(2):
                ev_ckv = V(lambda mt=mt: nc.vector.tensor_copy(
                    ckv[:, NB * mt:NB * mt + NB], PSA[:, 512 * mt:512 * mt + NB]))
            p.w('scalar', 't', t_q)
            for mt in range(2):
                S(lambda mt=mt, tok0=tok0: nc.scalar.copy(
                    qfm[mt][:, tok0:tok0 + NB], PSA[:, 1024 + 512 * mt:1024 + 512 * mt + NB]))
            m['psa_free_l1'] = p.cnt['v']
            m['psa_free_l1_s'] = p.cnt['s']
            p.w('tensor', 'v', ev_ckv)
            p.w('tensor', 'v', m.get('psb_free_l1', 0))
            p.w('tensor', 's', m.get('psb_free_l1_s', 0))
            for mt in range(2):
                for ch in range(2):
                    Tn(lambda mt=mt, ch=ch: nc.tensor.matmul(
                        PSB[:, 512 * mt:512 * mt + NB],
                        wuk[:, HCOL * ch + 128 * mt:HCOL * ch + 128 * mt + 128],
                        ckv[:, NB * ch:NB * ch + NB], start=(ch == 0), stop=(ch == 1)))
            t_k = p.cnt['t']
            for ktl in range(2):
                kt = 2 * batch + ktl
                p.w('tensor', 'v', m.get('pscV_free', 0))
                p.w('tensor', 'v', m.get('pscT_free1', 0))
                for ch in range(2):
                    Tn(lambda ktl=ktl, ch=ch: nc.tensor.matmul(
                        PSC[:, 512:512 + 256],
                        ckv[:, NB * ch + 128 * ktl:NB * ch + 128 * ktl + 128],
                        wuv[:, HCOL * ch:HCOL * ch + HCOL], start=(ch == 0), stop=(ch == 1)))
                t_v = p.cnt['t']
                p.w('vector', 't', t_v)
                vv2 = V(lambda kt=kt: nc.vector.tensor_copy(
                    vaug[:, 260 * kt:260 * kt + 260].rearrange("p (h j) -> p h j", j=65)[:, :, 0:64],
                    PSC[:, 512:512 + 256].rearrange("p (h j) -> p h j", j=64)))
                m['pscV_free'] = vv2
            m['ckv_used'] = p.cnt['t']
            p.w('scalar', 't', t_k)
            for mt in range(2):
                S(lambda mt=mt, tok0=tok0: nc.scalar.copy(
                    kfm[mt][:, tok0:tok0 + NB], PSB[:, 512 * mt:512 * mt + NB]))
            m['psb_free_l1_s'] = p.cnt['s']
        l1_v = p.cnt['v']
        l1_s = p.cnt['s']
        p.w('sync', 't', p.cnt['t'])
        d_masks = Dq(lambda: sec.dma_start(out=masks[:], in_=masks_d))

        p.w('tensor', 'v', l1_v)
        p.w('tensor', 's', l1_s)
        for tt in range(16):
            Dq(lambda tt=tt: sec.dma_start(out=xres[:, KB * tt:KB * tt + KB],
                                           in_=xq_d[128 * tt:128 * tt + 128, :]))
        d_xres = p.cnt['d']
        ccin_d_count = 0
        SC = [PSB[:, 0:512], PSB[:, 512:1024], PSC[:, 0:512], PSC[:, 512:1024]]

        def emit_pv(kt, nkt):
            eb0 = 2048 * (kt % 2)
            diag_r = kt - 4 * (nkt // 4 - 1)
            for h in range(HLOC):
                if diag_r >= 0:
                    p.w('tensor', 'v', m[f'mask_h{h}'])
                else:
                    p.w('tensor', 's', m[f'exp_h{h}'])
                if kt == 0:
                    p.w('tensor', 'v', m.get('norm_v_done', 0))
                Tn(lambda h=h, kt=kt, nkt=nkt, eb0=eb0: nc.tensor.matmul(
                    PSA[0:65, 512 * h:512 * h + 512],
                    vaug[:, 260 * kt + 65 * h:260 * kt + 65 * h + 65],
                    ebuf[:, eb0 + 512 * h:eb0 + 512 * h + 512],
                    start=(kt == 0), stop=(kt == nkt - 1)))
                m[f'pv_h{h}'] = p.cnt['t']
            m[f'pv_p{kt % 2}'] = p.cnt['t']

        for qb in range(NQB):
            nkt = 4 * (qb + 1)
            for kt in range(nkt):
                diag_r = kt - 4 * qb
                for h in range(HLOC):
                    u, rb0 = h // 2, 64 * (h % 2)
                    p.w('tensor', 's', m.get(f'exp_h{h}', 0))
                    if kt == 0:
                        p.w('tensor', 'v', m.get('norm_v_done', 0))
                    Tn(lambda u=u, rb0=rb0, kt=kt, qb=qb, h=h: nc.tensor.matmul(
                        SC[h],
                        kfm[u][rb0:rb0 + 64, 128 * kt:128 * kt + 128],
                        qfm[u][rb0:rb0 + 64, QB * qb:QB * qb + QB],
                        start=True, stop=True))
                    m[f'score_h{h}'] = p.cnt['t']
                eb0 = 2048 * (kt % 2)
                if kt > 0:
                    emit_pv(kt - 1, nkt)
                for h in range(HLOC):
                    p.w('scalar', 't', m[f'score_h{h}'])
                    p.w('scalar', 't', m.get(f'pv_p{kt % 2}', 0))
                    se = S(lambda h=h, eb0=eb0: nc.scalar.activation(
                        ebuf[:, eb0 + 512 * h:eb0 + 512 * h + 512], SC[h], AF.Exp,
                        bias=EXP_BIAS, scale=EXP_SCALE))
                    m[f'exp_h{h}'] = se
                    if diag_r >= 0:
                        p.w('vector', 's', se)
                        p.w('vector', 'd', d_masks)
                        mv = V(lambda h=h, diag_r=diag_r, eb0=eb0: nc.vector.tensor_tensor(
                            ebuf[:, eb0 + 512 * h:eb0 + 512 * h + 512],
                            ebuf[:, eb0 + 512 * h:eb0 + 512 * h + 512],
                            masks[:, 512 * diag_r:512 * diag_r + 512],
                            op=Op.mult))
                        m[f'mask_h{h}'] = mv
            emit_pv(nkt - 1, nkt)
            t_pv = p.cnt['t']
            p.w('vector', 'd', m.get('ctxn_free', 0))
            p.w('vector', 't', t_pv)
            p.w('vector', 'v', p.cnt['v'])
            vr2 = V(lambda: nc.vector.reciprocal(rl4[:, :], PSA[64:65, 0:2048]))
            p.w('tensor', 'v', vr2)
            tr = 0
            for h in range(HLOC):
                tr = Tn(lambda h=h: nc.tensor.matmul(
                    SC[h][0:64, :], onesc[:], rl4[:, 512 * h:512 * h + 512],
                    start=True, stop=True))
            p.w('vector', 't', tr)
            V(lambda: nc.vector.tensor_copy(rbuf4[:, 0:1024], PSB[0:64, 0:1024]))
            vR = V(lambda: nc.vector.tensor_copy(rbuf4[:, 1024:2048], PSC[0:64, 0:1024]))
            p.w('vector', 'v', vR)
            for rp in range(2):
                V(lambda rp=rp: nc.vector.tensor_tensor(
                    ctxn[64 * rp:64 * rp + 64, 0:1024].rearrange("p (a n) -> p a n", n=512),
                    PSA[0:64, 0:2048].rearrange("p (a b n) -> p a b n", b=2, n=512)[:, :, rp, :],
                    rbuf4[:, 0:2048].rearrange("p (a b n) -> p a b n", b=2, n=512)[:, :, rp, :],
                    op=Op.mult))
            v_ctxn = p.cnt['v']
            m['norm_v_done'] = v_ctxn
            m['psb_guard_v'] = v_ctxn
            p.w('sync', 'v', v_ctxn)
            for u in range(2):
                ccin_d_count = Dq(lambda u=u, qb=qb: sec.dma_start(
                    out=ccin_d[256 * (qb // 4) + 128 * u:256 * (qb // 4) + 128 * u + 128,
                               QB * (qb % 4):QB * (qb % 4) + QB],
                    in_=ctxn[:, QB * u:QB * u + QB]))
            m['ctxn_free'] = ccin_d_count
            if qb % 4 == 3:
                cq = qb // 4
                p.w('gpsimd', 'd', ccin_d_count)
                p.c(lambda cq=cq: nc.gpsimd.collective_compute(
                    "AllGather", mybir.AluOpType.bypass,
                    replica_groups=[[0, 1, 2, 3], [4, 5, 6, 7]],
                    ins=[ccin_d[256 * cq:256 * cq + 256, :].opt()],
                    outs=[ccout_d[1024 * cq:1024 * cq + 1024, :].opt()]), key=str(4 + cq))

        m['psa_free_wo'] = p.cnt['v']
        m['exp_done1'] = p.cnt['s']
        t_quad_end = p.cnt['t']

        p.w('sync', 't', t_quad_end)
        p.w('sync', 's', m.get('exp_done1', 0))
        p.w('sync', 'v', m.get('norm_v_done', 0))
        wo_v = load_weight_cast(wo_s, wo_d, D, KB)
        for k in '4567':
            p.w('gpsimd', k, 1)
        p.w('gpsimd', 'd', d_gidx)

        for tb in range(4):
            cur = tb % 2
            p.w('gpsimd', 't', t_quad_end)
            p.w('gpsimd', 't', m.get(f'ctxg_used{cur}', 0))
            gl = 0
            for ch in range(8):
                gl = G(lambda ch=ch, tb=tb, cur=cur: nc.gpsimd.indirect_dma_start(
                    out=ctxg[cur][:, QB * ch:QB * ch + QB], out_offset=None,
                    in_=ccout_rows,
                    in_offset=bass.IndirectOffsetOnAxis(ap=gidx[:, 4 * ch + tb:4 * ch + tb + 1], axis=0)))
            p.w('tensor', 'g', gl)
            p.w('tensor', 'v', wo_v)
            p.w('tensor', 'v', m.get('psa_free_wo', 0))
            for sub in range(4):
                slot = sub % 2
                p.w('tensor', 'v', m.get(f'wo_res{slot}', 0))
                for ch in range(8):
                    for hf in range(2):
                        Tn(lambda ch=ch, sub=sub, slot=slot, cur=cur, hf=hf: nc.tensor.matmul(
                            PSA[:, 1024 * slot + 512 * hf:1024 * slot + 512 * hf + 512],
                            ctxg[cur][:, QB * ch + 128 * sub:QB * ch + 128 * sub + 128],
                            wo_s[:, KB * ch + 512 * hf:KB * ch + 512 * hf + 512],
                            start=(ch == 0), stop=(ch == 7)))
                t_wo = p.cnt['t']
                m[f'ctxg_used{cur}'] = t_wo
                p.w('vector', 't', t_wo)
                p.w('vector', 'd', d_xres)
                ti = 4 * tb + sub
                va = V(lambda ti=ti, slot=slot: nc.vector.tensor_tensor(
                    xres[:, KB * ti:KB * ti + KB], xres[:, KB * ti:KB * ti + KB],
                    PSA[:, 1024 * slot:1024 * slot + 1024], op=Op.add))
                m[f'wo_res{slot}'] = va
            m[f'wo_tb_done{tb}'] = p.cnt['v']
        wo_res_v = p.cnt['v']
        t_wo_end = p.cnt['t']
        m['hqtm_used'] = t_wo_end
        m['h_guard_t'] = t_wo_end

        y_store = 0
        m['psa_slot0'] = wo_res_v
        m['psa_slot1'] = wo_res_v
        pendB = [None]

        def emit_passB_res(t_dnB, tiles):
            for j, ti in enumerate(tiles):
                p.w('vector', 't', t_dnB)
                va = V(lambda ti=ti, j=j: nc.vector.tensor_tensor(
                    xres[:, KB * ti:KB * ti + KB], xres[:, KB * ti:KB * ti + KB],
                    PSA[:, 1024 * j:1024 * j + 1024], op=Op.add))
                m[f'psa_slot{j}'] = va
                p.w('sync', 'v', va)
                m['y_store'] = Dq(lambda ti=ti: sec.dma_start(
                    out=y_d[128 * ti:128 * ti + 128, :],
                    in_=xres[:, KB * ti:KB * ti + KB]))

        for g in range(4):
            for pr in range(2):
                p.w('vector', 'v', m[f'wo_tb_done{g}'])
                p.w('scalar', 'v', m[f'wo_tb_done{g}'])
                p.w('vector', 'v', m.get('stat_v_free', 0))
                p.w('scalar', 'v', m.get('stat_v_free', 0))
                s_last = 0
                for t2 in range(2):
                    xsl = xres[:, KB * (4 * g + 2 * pr + t2):KB * (4 * g + 2 * pr + t2) + KB]
                    V(lambda xsl=xsl, t2=t2: nc.vector.tensor_reduce(
                        stat[:, t2:t2 + 1], xsl, axis=AX, op=Op.add))
                    s_last = S(lambda xsl=xsl, t2=t2: nc.scalar.activation(
                        sqscr[:, :], xsl, AF.Square, accum_out=stat[:, 8 + t2:8 + t2 + 1]))
                p.w('vector', 's', s_last)
                p.w('vector', 'v', p.cnt['v'])
                V(lambda: nc.vector.tensor_scalar(stat[:, 0:2], stat[:, 0:2], 1.0 / D, None, op0=Op.mult))
                p.w('vector', 'v', p.cnt['v'])
                V(lambda: nc.vector.tensor_scalar(stat[:, 8:10], stat[:, 8:10], 1.0 / D, None, op0=Op.mult))
                p.w('vector', 'v', p.cnt['v'])
                V(lambda: nc.vector.tensor_tensor(stat[:, 16:18], stat[:, 0:2], stat[:, 0:2], op=Op.mult))
                p.w('vector', 'v', p.cnt['v'])
                vv3 = V(lambda: nc.vector.tensor_tensor(stat[:, 16:18], stat[:, 8:10], stat[:, 16:18], op=Op.subtract))
                p.w('scalar', 'v', vv3)
                sv3 = S(lambda: nc.scalar.activation(stat[:, 20:22], stat[:, 16:18], AF.Sqrt, bias=LN_EPS))
                p.w('vector', 's', sv3)
                vr3 = V(lambda: nc.vector.reciprocal(stat[:, 24:26], stat[:, 20:22]))
                p.w('vector', 'v', vr3)
                for t2 in range(2):
                    xsl = xres[:, KB * (4 * g + 2 * pr + t2):KB * (4 * g + 2 * pr + t2) + KB]
                    V(lambda xsl=xsl, t2=t2: nc.vector.tensor_scalar(
                        xq32[:, KB * t2:KB * t2 + KB], xsl,
                        stat[:, t2:t2 + 1], stat[:, 24 + t2:24 + t2 + 1],
                        op0=Op.subtract, op1=Op.mult))
                m['stat_v_free'] = p.cnt['v']
                if pr == 0 and pendB[0] is not None:
                    emit_passB_res(*pendB[0])
                    pendB[0] = None
                for t2 in range(2):
                    tl = 2 * pr + t2
                    qv = quant(xqtm[:, KB * t2:KB * t2 + KB], xq32[:, KB * t2:KB * t2 + KB],
                               KB, qsF, twait=m.get('xqtm_used', 0))
                    par = m.get('pscpar', 0)
                    p.w('tensor', 'v', qv)
                    p.w('tensor', 'v', m.get(f'pscT{par}', 0))
                    tl_t = 0
                    for ch in range(8):
                        tl_t = Tn(lambda t2=t2, ch=ch, par=par: nc.tensor.transpose(
                            PSC_b[:, 1024 * par + 128 * ch:1024 * par + 128 * ch + 128],
                            xqtm[:, KB * t2 + 128 * ch:KB * t2 + 128 * ch + 128], identb[:]))
                    m['xqtm_used'] = tl_t
                    p.w('vector', 't', tl_t)
                    p.w('vector', 't', m.get('xqfm_used', 0))
                    vxf = V(lambda tl=tl, par=par: nc.vector.tensor_copy(
                        xqfm[:, KB * tl:KB * tl + KB], PSC_b[:, 1024 * par:1024 * par + KB]))
                    m[f'pscT{par}'] = vxf
                    m[f'xqfm_rdy{tl}'] = vxf
                    m['pscpar'] = par ^ 1
            for si in range(6):
                wslice = min(512, II - 512 * si)
                cur = si % 2
                pcw = si // 2
                p.w('sync', '1', 1)
                p.w('sync', 't', m.get(f'wgu_used{cur}', 0))
                p.w('gpsimd', '2', 1)
                p.w('gpsimd', 't', m.get(f'wgu_used{cur}', 0))
                dgu = 0
                ggu = 0
                for wch in range(8):
                    srcg = wgT_d[:, 512 * si:512 * si + wslice]
                    dgu = Dq(lambda srcg=srcg, wch=wch, cur=cur, wslice=wslice: sec.dma_start(
                        out=wgu[2 * cur][:, QB * wch:QB * wch + wslice],
                        in_=srcg[128 * wch:128 * wch + 128, :]))
                    srcu = wuT_d[:, 512 * si:512 * si + wslice]
                    ggu = G(lambda srcu=srcu, wch=wch, cur=cur, wslice=wslice: sec.dma_start(
                        out=wgu[2 * cur + 1][:, QB * wch:QB * wch + wslice],
                        in_=srcu[128 * wch:128 * wch + 128, :]))
                p.w('tensor', 'd', dgu)
                p.w('tensor', 'g', ggu)
                p.w('tensor', 'v', m.get('psb_guard_v', 0))
                for tl in range(4):
                    pp = tl % 2
                    gb = PSB[:, 0:wslice] if pp == 0 else PSA[:, 0:wslice]
                    ub = PSB[:, 512:512 + wslice] if pp == 0 else PSA[:, 512:512 + wslice]
                    p.w('tensor', 'v', m[f'xqfm_rdy{tl}'])
                    if pp == 1:
                        p.w('tensor', 'v', m.get('psa_slot0', 0))
                    p.w('tensor', 'v', m.get(f'h_done{pp}', 0))
                    p.w('tensor', 's', m.get(f'silu_done{pp}', 0))
                    for ch in range(8):
                        Tn(lambda tl=tl, ch=ch, cur=cur, wslice=wslice, gb=gb: nc.tensor.matmul(
                            gb,
                            xqfm[:, 1024 * tl + 128 * ch:1024 * tl + 128 * ch + 128],
                            wgu[2 * cur][:, QB * ch:QB * ch + wslice],
                            start=(ch == 0), stop=(ch == 7)))
                    t_g = p.cnt['t']
                    for ch in range(8):
                        Tn(lambda tl=tl, ch=ch, cur=cur, wslice=wslice, ub=ub: nc.tensor.matmul(
                            ub,
                            xqfm[:, 1024 * tl + 128 * ch:1024 * tl + 128 * ch + 128],
                            wgu[2 * cur + 1][:, QB * ch:QB * ch + wslice],
                            start=(ch == 0), stop=(ch == 7)))
                    t_u = p.cnt['t']
                    m[f'wgu_used{cur}'] = t_u
                    m['xqfm_used'] = t_u
                    p.w('scalar', 't', t_g)
                    p.w('scalar', 'v', m.get(f'h_done{pp}', 0))
                    ssl = S(lambda wslice=wslice, pp=pp, gb=gb: nc.scalar.activation(
                        silu_s[:, 512 * pp:512 * pp + wslice], gb, AF.Silu))
                    m[f'silu_done{pp}'] = ssl
                    p.w('vector', 't', t_u)
                    p.w('vector', 't', m.get('h_guard_t', 0))
                    p.w('vector', 's', ssl)
                    hv = V(lambda tl=tl, si=si, wslice=wslice, pp=pp, ub=ub:
                           nc.vector.tensor_tensor(
                        hbuf[:, 1024 * tl + 512 * (si % 2):
                             1024 * tl + 512 * (si % 2) + wslice],
                        ub, silu_s[:, 512 * pp:512 * pp + wslice], op=Op.mult))
                    m[f'h_done{pp}'] = hv
                if si % 2 == 1 or si == 5:
                    pc = si // 2
                    w = 1024 if pc < 2 else 768
                    nch = w // 128
                    c0 = 1024 * pc
                    for tl in range(4):
                        qv = quant(hqtm[:, c0:c0 + w],
                                   hbuf[:, 1024 * tl:1024 * tl + w],
                                   w, qsF, twait=m.get('hqtm_used', 0))
                        par = m.get('pscpar', 0)
                        p.w('tensor', 'v', qv)
                        p.w('tensor', 'v', m.get(f'pscT{par}', 0))
                        tl_t = 0
                        for j in range(nch):
                            tl_t = Tn(lambda c0=c0, j=j, par=par: nc.tensor.transpose(
                                PSC_b[:, 1024 * par + 128 * j:1024 * par + 128 * j + 128],
                                hqtm[:, c0 + 128 * j:c0 + 128 * j + 128], identb[:]))
                        m['hqtm_used'] = tl_t
                        p.w('vector', 't', tl_t)
                        p.w('vector', 't', m.get('hqfm_used', 0))
                        hv2 = V(lambda tl=tl, pc=pc, nch=nch, par=par: nc.vector.tensor_copy(
                            hqfm_v[:, 8 * pc:8 * pc + nch, 128 * tl:128 * tl + 128],
                            PSC_b[:, 1024 * par:1024 * par + 128 * nch].rearrange("p (c i) -> p c i", i=128)))
                        m[f'pscT{par}'] = hv2
                        m[f'hqfm_rdy_{tl}_{pc}'] = hv2
                        m['pscpar'] = par ^ 1
            for half in range(2):
                for ch in range(22):
                    pc = min(ch // 8, 2)
                    slot = m.get('wdnslot', 0)
                    m['wdnslot'] = (slot + 1) % 6
                    qi = slot % 3
                    qk = ('d', 'g', 'e')[qi]
                    qeng = ('sync', 'gpsimd', 'scalar')[qi]
                    p.w(qeng, '3', 1)
                    p.w(qeng, 't', t_wo_end)
                    p.w(qeng, 't', m.get(f'wdn_used{slot}', 0))
                    fn = (lambda ch=ch, slot=slot: sec.dma_start(
                        out=wdnR[slot][:, :], in_=wdT_d[128 * ch:128 * ch + 128, :]))
                    dwd = {'d': Dq, 'g': G, 'e': p.e}[qk](fn)
                    p.w('tensor', qk, dwd)
                    for j in range(2):
                        tl = 2 * half + j
                        p.w('tensor', 'v', m[f'hqfm_rdy_{tl}_{pc}'])
                        if ch == 0:
                            p.w('tensor', 'v', m[f'psa_slot{j}'])
                            p.w('tensor', 'v', m.get('h_done1', 0))
                            p.w('tensor', 's', m.get('silu_done1', 0))
                        for hf in range(2):
                            Tn(lambda ch=ch, j=j, tl=tl, slot=slot, hf=hf: nc.tensor.matmul(
                                PSA[:, 1024 * j + 512 * hf:1024 * j + 512 * hf + 512],
                                hqfm_v[:, ch, 128 * tl:128 * tl + 128],
                                wdnR[slot][:, 512 * hf:512 * hf + 512],
                                start=(ch == 0), stop=(ch == 21)))
                    m[f'wdn_used{slot}'] = p.cnt['t']
                t_dn = p.cnt['t']
                if half == 0:
                    for j in range(2):
                        ti = 4 * g + j
                        p.w('vector', 't', t_dn)
                        va = V(lambda ti=ti, j=j: nc.vector.tensor_tensor(
                            xres[:, KB * ti:KB * ti + KB], xres[:, KB * ti:KB * ti + KB],
                            PSA[:, 1024 * j:1024 * j + 1024], op=Op.add))
                        m[f'psa_slot{j}'] = va
                        p.w('sync', 'v', va)
                        m['y_store'] = Dq(lambda ti=ti: sec.dma_start(
                            out=y_d[128 * ti:128 * ti + 128, :],
                            in_=xres[:, KB * ti:KB * ti + KB]))
            m['hqfm_used'] = p.cnt['t']
            pendB[0] = (p.cnt['t'], [4 * g + 2, 4 * g + 3])
        emit_passB_res(*pendB[0])
        pendB[0] = None
        y_store = m['y_store']

        p.w('sync', 'd', y_store)

    with nc.Block() as block:
        @block.sync
        def _(sec):
            program(P(nc, 'sync', sec, sems), sec)

        @block.vector
        def _(sec):
            program(P(nc, 'vector', sec, sems), sec)

        @block.scalar
        def _(sec):
            program(P(nc, 'scalar', sec, sems), sec)

        @block.tensor
        def _(sec):
            program(P(nc, 'tensor', sec, sems), sec)

        @block.gpsimd
        def _(sec):
            program(P(nc, 'gpsimd', sec, sems), sec)

    return nc


_NC_CACHE = None


def _consts():
    mm = np.zeros((128, 4, 512), np.float32)
    kk = np.arange(128)[:, None]
    qq = np.arange(512)[None, :]
    for r in range(4):
        mm[:, r, :] = (qq >= kk + 128 * r)
    masks = np.ascontiguousarray(mm.reshape(128, 2048)).astype(ml_dtypes.bfloat16)
    identb = np.eye(128, dtype=ml_dtypes.bfloat16)
    cvals = np.tile(np.array([[EXP_SCALE, EXP_BIAS, LN_EPS, 0.0]], np.float32), (128, 1))
    return identb, masks, cvals


def kernel(L0_attn_x, L0_attn_W_DKV, L0_attn_W_UK, L0_attn_W_UV, L0_attn_W_UQ,
           L0_attn_W_O, L0_ffn_w_gate, L0_ffn_w_up, L0_ffn_w_down):
    global _NC_CACHE
    x = np.ascontiguousarray(np.asarray(L0_attn_x, np.float32))
    W_DKV = np.ascontiguousarray(np.asarray(L0_attn_W_DKV, np.float32))
    W_UK = np.ascontiguousarray(np.asarray(L0_attn_W_UK, np.float32))
    W_UV = np.ascontiguousarray(np.asarray(L0_attn_W_UV, np.float32))
    W_UQ = np.ascontiguousarray(np.asarray(L0_attn_W_UQ, np.float32))
    W_O = np.ascontiguousarray(np.asarray(L0_attn_W_O, np.float32))
    wg = np.ascontiguousarray(np.asarray(L0_ffn_w_gate, np.float32))
    wu = np.ascontiguousarray(np.asarray(L0_ffn_w_up, np.float32))
    wd = np.ascontiguousarray(np.asarray(L0_ffn_w_down, np.float32))

    identb, masks, cvals = _consts()
    if _NC_CACHE is None:
        _NC_CACHE = build_nc()
    nc = _NC_CACHE

    in_maps = []
    pidx = np.arange(128, dtype=np.uint32)
    for c in range(8):
        b, hq = c // 4, c % 4
        hs = slice(HCOL * hq, HCOL * hq + HCOL)
        gidx = np.zeros((128, 64), np.uint32)
        for ch in range(8):
            for tb in range(4):
                gidx[:, 4 * ch + tb] = 4096 * hq + (128 * ch + pidx) * 4 + tb
        wdsh = np.zeros((1024, 384), np.float32)
        lo = 384 * c
        hi = min(lo + 384, II)
        wdsh[:, 0:hi - lo] = wd[:, lo:hi]
        in_maps.append({
            "x": x[b],
            "xquarter": np.ascontiguousarray(x[b, 2048 * hq:2048 * hq + 2048]),
            "wdkv": W_DKV,
            "wuq": np.ascontiguousarray(W_UQ[:, hs]),
            "wuk": np.ascontiguousarray(W_UK[:, hs]),
            "wuv": np.ascontiguousarray(W_UV[:, hs]),
            "wo": W_O,
            "wgsh": np.ascontiguousarray(wg[:, 128 * c:128 * c + 128]),
            "wush": np.ascontiguousarray(wu[:, 128 * c:128 * c + 128]),
            "wdsh": wdsh,
            "identb": identb, "masks": masks, "cvals": cvals,
            "gidx": gidx,
        })

    res = run_bass_kernel_spmd(nc, in_maps, core_ids=list(range(8)))
    out = np.empty((2, T, D), np.float32)
    for c in range(8):
        b, hq = c // 4, c % 4
        out[b, 2048 * hq:2048 * hq + 2048] = np.asarray(res.results[c]["y"])
    return out


# version 23
# speedup vs baseline: 1.1852x; 1.0398x over previous
DEBUG_TAPS = False

import numpy as np
import ml_dtypes
import concourse.bass as bass
import concourse.mybir as mybir
from concourse.bass_utils import run_bass_kernel_spmd
from concourse.alu_op_type import AluOpType

F32 = mybir.dt.float32
BF16 = mybir.dt.bfloat16
I32 = mybir.dt.int32
U32 = mybir.dt.uint32
AF = mybir.ActivationFunctionType
AX = mybir.AxisListType.X
Op = AluOpType

T = 8192
D = 1024
CL = 256
HLOC = 4
HCOL = 256
II = 2816
QB = 512
NQB = 16
NKT = 64
KB = 1024
EXP_SCALE = 0.125
EXP_BIAS = -3.0
LN_EPS = 1e-5

ENGS = ('tensor', 'vector', 'scalar', 'gpsimd', 'sync')


class P:

    def __init__(self, nc, active, eng, sems):
        self.nc = nc
        self.active = active
        self.eng = eng
        self.sems = sems
        self.cnt = dict.fromkeys("tvsgdce1234567", 0)
        self.wm = {e: dict.fromkeys("tvsgdce1234567", 0) for e in ENGS}

    def w(self, tgt, key, count):
        if not count or count <= self.wm[tgt][key]:
            return
        self.wm[tgt][key] = count
        if self.active == tgt:
            self.eng.wait_ge(self.sems[key], count)

    def _do(self, key, inc, fn, act):
        self.cnt[key] += inc
        if self.active == act:
            fn().then_inc(self.sems[key], inc)
        return self.cnt[key]

    def t(self, fn):
        return self._do('t', 1, fn, 'tensor')

    def v(self, fn):
        return self._do('v', 1, fn, 'vector')

    def s(self, fn):
        return self._do('s', 1, fn, 'scalar')

    def g(self, fn):
        return self._do('g', 16, fn, 'gpsimd')

    def d(self, fn):
        return self._do('d', 16, fn, 'sync')

    def e(self, fn):
        return self._do('e', 16, fn, 'scalar')

    def c(self, fn, key='c'):
        self.cnt[key] += 1
        if self.active == 'gpsimd':
            fn().then_inc(self.sems[key])
        return self.cnt[key]


def build_nc():
    nc = bass.Bass("TRN2", target_bir_lowering=False)

    x_d = nc.dram_tensor("x", [T, D], F32, kind="ExternalInput").ap()
    xq_d = nc.dram_tensor("xquarter", [2048, D], F32, kind="ExternalInput").ap()
    wdkv_d = nc.dram_tensor("wdkv", [D, CL], F32, kind="ExternalInput").ap()
    wuq_d = nc.dram_tensor("wuq", [D, HCOL], F32, kind="ExternalInput").ap()
    wuk_d = nc.dram_tensor("wuk", [CL, HCOL], F32, kind="ExternalInput").ap()
    wuv_d = nc.dram_tensor("wuv", [CL, HCOL], F32, kind="ExternalInput").ap()
    wo_d = nc.dram_tensor("wo", [D, D], F32, kind="ExternalInput").ap()
    wgsh_d = nc.dram_tensor("wgsh", [II, 128], F32, kind="ExternalInput").ap()
    wush_d = nc.dram_tensor("wush", [II, 128], F32, kind="ExternalInput").ap()
    wdsh_d = nc.dram_tensor("wdsh", [D, 384], F32, kind="ExternalInput").ap()
    identb_d = nc.dram_tensor("identb", [128, 128], BF16, kind="ExternalInput").ap()
    masks_d = nc.dram_tensor("masks", [128, 2048], BF16, kind="ExternalInput").ap()
    cvals_d = nc.dram_tensor("cvals", [128, 4], F32, kind="ExternalInput").ap()
    gidx_d = nc.dram_tensor("gidx", [128, 64], U32, kind="ExternalInput").ap()
    y_d = nc.dram_tensor("y", [2048, D], F32, kind="ExternalOutput").ap()
    y2_d = nc.dram_tensor("y2", [2048, D], F32, kind="ExternalOutput").ap() if DEBUG_TAPS else None
    y3_d = nc.dram_tensor("y3", [2048, D], BF16, kind="ExternalOutput").ap() if DEBUG_TAPS else None
    y4_d = nc.dram_tensor("y4", [2048, II], BF16, kind="ExternalOutput").ap() if DEBUG_TAPS else None
    wgin_d = nc.dram_tensor("wgin", [128, II], BF16).ap()
    wuin_d = nc.dram_tensor("wuin", [128, II], BF16).ap()
    wdin_d = nc.dram_tensor("wdin", [384, D], BF16).ap()
    wgT_d = nc.dram_tensor("wgT", [D, II], BF16, addr_space="Shared").ap()
    wuT_d = nc.dram_tensor("wuT", [D, II], BF16, addr_space="Shared").ap()
    wdT_d = nc.dram_tensor("wdT", [3072, D], BF16, addr_space="Shared").ap()
    ccin_d = nc.dram_tensor("ccin", [4 * HCOL, 2048], BF16).ap()
    ccout_d = nc.dram_tensor("ccout", [4096, 2048], BF16).ap()
    ccout_rows = ccout_d.rearrange("r (c n) -> (r c) n", n=QB)

    def sb(name, shape, dt):
        return nc.alloc_sbuf_tensor("s_" + name, shape, dt).ap()

    xres = sb("xres", [128, 16 * KB], F32)
    xres_v = xres.rearrange("p (tt k) -> p tt k", k=KB)
    identb = sb("identb", [128, 128], BF16)
    cvals = sb("cvals", [128, 4], F32)
    nc.const_aps.register(F32, EXP_SCALE, cvals[:, 0:1])
    nc.const_aps.register(F32, EXP_BIAS, cvals[:, 1:2])
    nc.const_aps.register(F32, LN_EPS, cvals[:, 2:3])
    gidx = sb("gidx", [128, 64], U32)
    stat = sb("stat", [128, 32], F32)
    sqscr = sb("sqscr", [128, KB], F32)
    onesc = sb("onesc", [1, 64], F32)

    from concourse.bass_primitives import MemorySpace
    AB = (nc.bytes_allocated(MemorySpace.SBUF) + 511) & ~511
    _names = [0]

    def at(off, shape, dt):
        _names[0] += 1
        return nc.alloc_sbuf_tensor_at(f"a_{_names[0]}", shape, dt, offset=AB + off).ap()

    qfm = [at(16384 * u, [128, T], BF16) for u in range(2)]
    kfm = [at(32768 + 16384 * u, [128, T], BF16) for u in range(2)]
    vaug = at(65536, [128, NKT * 260], BF16)
    wdkv = at(98816, [128, 8 * CL], BF16)
    wuq = at(102912, [128, 8 * HCOL], BF16)
    wuk = at(107008, [128, 2 * HCOL], BF16)
    wuv = at(108032, [128, 2 * HCOL], BF16)
    ebuf = at(109056, [128, 4 * KB], BF16)
    masks = at(98816, [128, 2048], BF16)
    rl4 = at(117248, [1, 4 * QB], F32)
    rbuf4 = at(125440, [64, 4 * QB], F32)
    xtile = at(121344, [128, 2 * KB], F32)
    stage = at(121344, [128, KB], F32)
    stageW = at(121344, [128, 2 * KB], F32)
    wcast = at(129536, [128, KB], BF16)
    wcastW = at(129536, [128, 2 * KB], BF16)
    xntm = at(129536, [128, 2 * KB], BF16)
    ttmp = at(133632, [128, KB], BF16)
    ttmpW = at(133632, [128, 2 * KB], BF16)
    xnfm = at(133632, [128, 8 * 256], BF16)
    ckv = at(137728, [128, 2 * 256], BF16)
    ctxn = at(138752, [128, 2 * QB], BF16)
    ARENA_ATTN_END = 140800

    wo_s = at(0, [128, 8 * KB], BF16)
    ctxg = [at(16384 + 8192 * i, [128, 8 * QB], BF16) for i in range(2)]

    hbuf = at(0, [128, 4 * KB], F32)
    wdnR = [at(16384 + 2048 * i, [128, KB], BF16) for i in range(6)]
    hqtm = at(36864, [128, II], BF16)
    wgu = [at(43008 + 8192 * i, [128, 8 * QB], BF16) for i in range(4)]
    hqfm = at(75776, [128, 22 * 512], BF16)
    hqfm_v = hqfm.rearrange("p (c n) -> p c n", n=512)
    xq32 = at(98304, [128, 2 * KB], F32)
    xqtm = at(106496, [128, 2 * KB], BF16)
    xqfm = at(110592, [128, 4 * KB], BF16)
    qsP = tuple([at(4096 * i, [128, 1024], F32) for i in range(4)]
                + [at(12288, [128, 1024], I32), at(16384, [128, 192], F32),
                   at(17152, [128, 64], F32)])
    qsF = tuple([at(118784 + 4096 * i, [128, 1024], F32) for i in range(4)]
                + [at(131072, [128, 1024], I32), at(135168, [128, 192], F32),
                   at(135936, [128, 64], F32)])
    silu_s = at(136192, [128, 2 * QB], F32)
    assert AB + ARENA_ATTN_END <= 229300, (AB, ARENA_ATTN_END)
    assert AB + 140800 <= 229376, AB

    PSA = nc.alloc_psum_tensor("PSA", [128, 2048], F32).ap()
    PSB = nc.alloc_psum_tensor("PSB", [128, 1024], F32).ap()
    PSC = nc.alloc_psum_tensor("PSC", [128, 1024], F32).ap()
    PSB_b = PSB.bitcast(BF16)
    PSC_b = PSC.bitcast(BF16)

    sems = {}
    for k in "tvsgdce1234567":
        sems[k] = nc.alloc_semaphore(name="sem_" + k)

    def program(p: P, sec):
        V, S, Tn, G, Dq = p.v, p.s, p.t, p.g, p.d
        m = {}

        Dq(lambda: sec.dma_start(out=identb[:], in_=identb_d))
        Dq(lambda: sec.dma_start(out=cvals[:], in_=cvals_d))
        d_gidx = Dq(lambda: sec.dma_start(out=gidx[:], in_=gidx_d))

        def load_weight_cast(dst, src_d, rows, cols):
            last = 0
            for ch in range(rows // 128):
                p.w('sync', 'v', m.get('stage_free', 0))
                dd = Dq(lambda ch=ch: sec.dma_start(out=stage[:, 0:cols],
                                                    in_=src_d[128 * ch:128 * ch + 128, :]))
                p.w('vector', 'd', dd)
                last = V(lambda ch=ch: nc.vector.tensor_copy(
                    dst[:, cols * ch:cols * ch + cols], stage[:, 0:cols]))
                m['stage_free'] = last
            return last

        def quant(dst, src, width, bufs, twait=0):
            q0f, q1f, q2f, q3f, qsif, qsmf, qsm2f = bufs
            nb = width // 16
            p.w('vector', 'v', p.cnt['v'])
            p.w('vector', 't', twait)
            src3 = src.rearrange("p (b k) -> p b k", k=16)
            dst3 = dst.rearrange("p (b k) -> p b k", k=16)

            def vs(fn):
                r = V(fn)
                p.w('vector', 'v', r)
                return r

            q0, q1, q2, q3 = (q[:, 0:width] for q in (q0f, q1f, q2f, q3f))
            qsi_ = qsif[:, 0:width]
            q03 = q0.rearrange("p (b k) -> p b k", k=16)
            q13 = q1.rearrange("p (b k) -> p b k", k=16)
            vs(lambda: nc.vector.tensor_reduce(qsmf[:, 0:nb], src3, axis=AX, op=Op.max,
                                               apply_absolute_value=True))
            vs(lambda: nc.vector.tensor_scalar(qsmf[:, 64:64 + nb], qsmf[:, 0:nb],
                                               1.0 / 3.0, 1e-12, op0=Op.mult, op1=Op.max))
            vs(lambda: nc.vector.tensor_scalar(qsmf[:, 128:128 + nb], qsmf[:, 0:nb],
                                               1.0 / 12.0, None, op0=Op.mult))
            vs(lambda: nc.vector.reciprocal(qsm2f[:, 0:nb], qsmf[:, 64:64 + nb]))
            binvh = qsm2f[:, 0:nb].broadcast_to([128, nb, 16])
            bsc12 = qsmf[:, 128:128 + nb].broadcast_to([128, nb, 16])
            v0 = vs(lambda: nc.vector.tensor_tensor(q03, src3, binvh, op=Op.mult))
            p.w('scalar', 'v', v0)
            sq = S(lambda: nc.scalar.activation(q1, q0, AF.Square))
            p.w('vector', 's', sq)
            vs(lambda: nc.vector.tensor_scalar(q2, q1, 1.0, 1.0, op0=Op.is_lt, op1=Op.add))
            vs(lambda: nc.vector.tensor_scalar(q3, q1, 4.0, 1.0, op0=Op.is_lt, op1=Op.add))
            vs(lambda: nc.vector.tensor_tensor(q2, q2, q3, op=Op.mult))
            vs(lambda: nc.vector.tensor_tensor(qsi_, q0, q2, op=Op.mult))
            vs(lambda: nc.vector.tensor_copy(q0, qsi_))
            vs(lambda: nc.vector.tensor_scalar(q2, q1, 1.0, 1.0, op0=Op.is_ge, op1=Op.add))
            vs(lambda: nc.vector.tensor_scalar(q3, q1, 4.0, 1.0, op0=Op.is_ge, op1=Op.add))
            vs(lambda: nc.vector.tensor_tensor(q2, q2, q3, op=Op.mult))
            vs(lambda: nc.vector.tensor_tensor(q1, q0, q2, op=Op.mult))
            return vs(lambda: nc.vector.tensor_tensor(dst3, q13, bsc12, op=Op.mult))



        def prep_col(src_d, in_d, rows):
            for r0 in range(0, rows, 1024):
                rlen = min(1024, rows - r0)
                nch = rlen // 128
                p.w('sync', 'v', m.get('stage_free', 0))
                dd = Dq(lambda src_d=src_d, r0=r0, rlen=rlen: sec.dma_start(
                    out=stageW[:, 0:rlen].rearrange("p (a c) -> p a c", c=128),
                    in_=src_d[r0:r0 + rlen, :].rearrange("(a p) c -> p a c", p=128)))
                p.w('vector', 'd', dd)
                cv = quant(wcastW[:, 0:rlen], stageW[:, 0:rlen], rlen, qsP,
                           twait=m.get('wprep_t', 0))
                m['stage_free'] = cv
                p.w('tensor', 'v', cv)
                p.w('tensor', 'v', m.get('psbT_free', 0))
                p.w('tensor', 'd', 16)
                tl = 0
                for j in range(nch):
                    tl = Tn(lambda j=j: nc.tensor.transpose(
                        PSB_b[:, 128 * j:128 * j + 128],
                        wcastW[:, 128 * j:128 * j + 128], identb[:]))
                m['wprep_t'] = tl
                p.w('vector', 't', tl)
                p.w('vector', 'd', m.get('ttmp_free', 0))
                ev = V(lambda rlen=rlen: nc.vector.tensor_copy(
                    ttmpW[:, 0:rlen], PSB_b[:, 0:rlen]))
                m['psbT_free'] = ev
                p.w('sync', 'v', ev)
                m['ttmp_free'] = Dq(lambda in_d=in_d, r0=r0, rlen=rlen: sec.dma_start(
                    out=in_d[:, r0:r0 + rlen], in_=ttmpW[:, 0:rlen]))
            return m['ttmp_free']

        def prep_wd():
            for g in range(2):
                r0 = 512 * g
                p.w('sync', 'v', m.get('stage_free', 0))
                dd = Dq(lambda r0=r0: sec.dma_start(
                    out=stageW[:, 0:1536].rearrange("p (a c) -> p a c", c=384),
                    in_=wdsh_d[r0:r0 + 512, :].rearrange("(a p) c -> p a c", p=128)))
                p.w('vector', 'd', dd)
                cv = 0
                for off in range(0, 1536, 768):
                    cv = quant(wcastW[:, off:off + 768], stageW[:, off:off + 768],
                               768, qsP, twait=m.get('wprep_t', 0))
                m['stage_free'] = cv
                p.w('tensor', 'v', cv)
                p.w('tensor', 'v', m.get('psbT_free', 0))
                p.w('tensor', 'd', 16)
                tl = 0
                for cb in range(3):
                    for a in range(4):
                        tl = Tn(lambda cb=cb, a=a: nc.tensor.transpose(
                            PSB_b[:, 512 * cb + 128 * a:512 * cb + 128 * a + 128],
                            wcastW[:, 384 * a + 128 * cb:384 * a + 128 * cb + 128],
                            identb[:]))
                m['wprep_t'] = tl
                p.w('vector', 't', tl)
                p.w('vector', 'd', m.get('ttmp_free', 0))
                ev = V(lambda: nc.vector.tensor_copy(ttmpW[:, 0:1536], PSB_b[:, 0:1536]))
                m['psbT_free'] = ev
                p.w('sync', 'v', ev)
                for cb in range(3):
                    m['ttmp_free'] = Dq(lambda cb=cb, r0=r0: sec.dma_start(
                        out=wdin_d[128 * cb:128 * cb + 128, r0:r0 + 512],
                        in_=ttmpW[:, 512 * cb:512 * cb + 512]))
            return m['ttmp_free']

        G8 = [[0, 1, 2, 3, 4, 5, 6, 7]]
        d_wg = prep_col(wgsh_d, wgin_d, II)
        p.w('gpsimd', 'd', d_wg)
        p.c(lambda: nc.gpsimd.collective_compute(
            "AllGather", mybir.AluOpType.bypass, replica_groups=G8,
            ins=[wgin_d.opt()], outs=[wgT_d.opt()]), key='1')
        d_wu = prep_col(wush_d, wuin_d, II)
        p.w('gpsimd', 'd', d_wu)
        p.c(lambda: nc.gpsimd.collective_compute(
            "AllGather", mybir.AluOpType.bypass, replica_groups=G8,
            ins=[wuin_d.opt()], outs=[wuT_d.opt()]), key='2')
        d_wd = prep_wd()
        p.w('gpsimd', 'd', d_wd)
        p.c(lambda: nc.gpsimd.collective_compute(
            "AllGather", mybir.AluOpType.bypass, replica_groups=G8,
            ins=[wdin_d.opt()], outs=[wdT_d.opt()]), key='3')
        load_weight_cast(wdkv, wdkv_d, D, CL)
        load_weight_cast(wuq, wuq_d, D, HCOL)
        load_weight_cast(wuk, wuk_d, CL, HCOL)
        w_attn_v = load_weight_cast(wuv, wuv_d, CL, HCOL)
        V(lambda: nc.vector.memset(onesc[:], 1.0))
        v_init = V(lambda: nc.vector.memset(vaug[:, 64::65], 1.0))

        m['psb_free_l1'] = m.get('psbT_free', 0)

        m['xntm_free'] = p.cnt['t']
        m['xnfm_guard_d'] = m.get('ttmp_free', 0)
        p.w('tensor', 'v', w_attn_v)
        p.w('tensor', 'v', v_init)
        NB = 256
        for batch in range(32):
            tok0 = batch * NB
            dl = []
            for tt in range(2):
                p.w('sync', 'v', m.get(f'xtile_free{tt}', 0))
                p.w('sync', 's', m.get(f'xtile_free_s{tt}', 0))
                p.w('sync', 'v', m.get('stage_free', 0))
                dl.append(Dq(lambda tt=tt, tok0=tok0: sec.dma_start(
                    out=xtile[:, KB * tt:KB * tt + KB],
                    in_=x_d[tok0 + 128 * tt:tok0 + 128 * tt + 128, :])))
            p.w('vector', 'v', m.get('stat_v_free', 0))
            p.w('scalar', 'v', m.get('stat_v_free', 0))
            s_last = 0
            for tt in range(2):
                p.w('vector', 'd', dl[tt])
                p.w('scalar', 'd', dl[tt])
                V(lambda tt=tt: nc.vector.tensor_reduce(
                    stat[:, tt:tt + 1], xtile[:, KB * tt:KB * tt + KB], axis=AX, op=Op.add))
                s_last = S(lambda tt=tt: nc.scalar.activation(
                    sqscr[:, :], xtile[:, KB * tt:KB * tt + KB], AF.Square,
                    accum_out=stat[:, 8 + tt:8 + tt + 1]))
                m[f'xtile_free_s{tt}'] = s_last
            p.w('vector', 's', s_last)
            p.w('vector', 'v', p.cnt['v'])
            V(lambda: nc.vector.tensor_scalar(stat[:, 0:2], stat[:, 0:2], 1.0 / D, None, op0=Op.mult))
            p.w('vector', 'v', p.cnt['v'])
            V(lambda: nc.vector.tensor_scalar(stat[:, 8:10], stat[:, 8:10], 1.0 / D, None, op0=Op.mult))
            p.w('vector', 'v', p.cnt['v'])
            V(lambda: nc.vector.tensor_tensor(stat[:, 16:18], stat[:, 0:2], stat[:, 0:2], op=Op.mult))
            p.w('vector', 'v', p.cnt['v'])
            vvar = V(lambda: nc.vector.tensor_tensor(stat[:, 16:18], stat[:, 8:10], stat[:, 16:18], op=Op.subtract))
            p.w('scalar', 'v', vvar)
            svar = S(lambda: nc.scalar.activation(stat[:, 20:22], stat[:, 16:18], AF.Sqrt, bias=LN_EPS))
            p.w('vector', 's', svar)
            vrs = V(lambda: nc.vector.reciprocal(stat[:, 24:26], stat[:, 20:22]))
            p.w('vector', 'v', vrs)
            p.w('vector', 't', m.get('xntm_free', 0))
            for tt in range(2):
                va = V(lambda tt=tt: nc.vector.tensor_scalar(
                    xntm[:, KB * tt:KB * tt + KB], xtile[:, KB * tt:KB * tt + KB],
                    stat[:, tt:tt + 1], stat[:, 24 + tt:24 + tt + 1],
                    op0=Op.subtract, op1=Op.mult))
                m[f'xtile_free{tt}'] = va
            m['stat_v_free'] = p.cnt['v']
            v_apply = p.cnt['v']
            p.w('tensor', 'v', v_apply)
            for ch in range(8):
                p.w('tensor', 's', m.get(f'pscT_free{ch % 2}', 0))
                if ch % 2 == 1:
                    p.w('tensor', 'v', m.get('pscV_free', 0))
                tl = 0
                for tt in range(2):
                    tl = Tn(lambda ch=ch, tt=tt: nc.tensor.transpose(
                        PSC_b[:, 1024 * (ch % 2) + 128 * tt:1024 * (ch % 2) + 128 * tt + 128],
                        xntm[:, KB * tt + 128 * ch:KB * tt + 128 * ch + 128], identb[:]))
                if ch == 7:
                    m['xntm_free'] = tl
                p.w('scalar', 't', tl)
                p.w('scalar', 't', m.get('xnfm_used', 0))
                p.w('scalar', 'd', m.get('xnfm_guard_d', 0))
                ev = S(lambda ch=ch: nc.scalar.copy(
                    xnfm[:, NB * ch:NB * ch + NB],
                    PSC_b[:, 1024 * (ch % 2):1024 * (ch % 2) + NB]))
                m[f'pscT_free{ch % 2}'] = ev
            s_xnfm = p.cnt['s']
            p.w('tensor', 's', s_xnfm)
            p.w('tensor', 'v', m.get('psa_free_l1', 0))
            for mt in range(2):
                for ch in range(8):
                    Tn(lambda mt=mt, ch=ch: nc.tensor.matmul(
                        PSA[:, 512 * mt:512 * mt + NB],
                        wdkv[:, CL * ch + 128 * mt:CL * ch + 128 * mt + 128],
                        xnfm[:, NB * ch:NB * ch + NB], start=(ch == 0), stop=(ch == 7)))
            t_ckv = p.cnt['t']
            for mt in range(2):
                for ch in range(8):
                    Tn(lambda mt=mt, ch=ch: nc.tensor.matmul(
                        PSA[:, 1024 + 512 * mt:1024 + 512 * mt + NB],
                        wuq[:, HCOL * ch + 128 * mt:HCOL * ch + 128 * mt + 128],
                        xnfm[:, NB * ch:NB * ch + NB], start=(ch == 0), stop=(ch == 7)))
            m['xnfm_used'] = p.cnt['t']
            t_q = p.cnt['t']
            p.w('vector', 't', t_ckv)
            p.w('vector', 't', m.get('ckv_used', 0))
            ev_ckv = 0
            for mt in range(2):
                ev_ckv = V(lambda mt=mt: nc.vector.tensor_copy(
                    ckv[:, NB * mt:NB * mt + NB], PSA[:, 512 * mt:512 * mt + NB]))
            p.w('scalar', 't', t_q)
            for mt in range(2):
                S(lambda mt=mt, tok0=tok0: nc.scalar.copy(
                    qfm[mt][:, tok0:tok0 + NB], PSA[:, 1024 + 512 * mt:1024 + 512 * mt + NB]))
            m['psa_free_l1'] = p.cnt['v']
            m['psa_free_l1_s'] = p.cnt['s']
            p.w('tensor', 'v', ev_ckv)
            p.w('tensor', 'v', m.get('psb_free_l1', 0))
            p.w('tensor', 's', m.get('psb_free_l1_s', 0))
            for mt in range(2):
                for ch in range(2):
                    Tn(lambda mt=mt, ch=ch: nc.tensor.matmul(
                        PSB[:, 512 * mt:512 * mt + NB],
                        wuk[:, HCOL * ch + 128 * mt:HCOL * ch + 128 * mt + 128],
                        ckv[:, NB * ch:NB * ch + NB], start=(ch == 0), stop=(ch == 1)))
            t_k = p.cnt['t']
            for ktl in range(2):
                kt = 2 * batch + ktl
                p.w('tensor', 'v', m.get('pscV_free', 0))
                p.w('tensor', 'v', m.get('pscT_free1', 0))
                for ch in range(2):
                    Tn(lambda ktl=ktl, ch=ch: nc.tensor.matmul(
                        PSC[:, 512:512 + 256],
                        ckv[:, NB * ch + 128 * ktl:NB * ch + 128 * ktl + 128],
                        wuv[:, HCOL * ch:HCOL * ch + HCOL], start=(ch == 0), stop=(ch == 1)))
                t_v = p.cnt['t']
                p.w('vector', 't', t_v)
                vv2 = V(lambda kt=kt: nc.vector.tensor_copy(
                    vaug[:, 260 * kt:260 * kt + 260].rearrange("p (h j) -> p h j", j=65)[:, :, 0:64],
                    PSC[:, 512:512 + 256].rearrange("p (h j) -> p h j", j=64)))
                m['pscV_free'] = vv2
            m['ckv_used'] = p.cnt['t']
            p.w('scalar', 't', t_k)
            for mt in range(2):
                S(lambda mt=mt, tok0=tok0: nc.scalar.copy(
                    kfm[mt][:, tok0:tok0 + NB], PSB[:, 512 * mt:512 * mt + NB]))
            m['psb_free_l1_s'] = p.cnt['s']
        l1_v = p.cnt['v']
        l1_s = p.cnt['s']
        p.w('sync', 't', p.cnt['t'])
        d_masks = Dq(lambda: sec.dma_start(out=masks[:], in_=masks_d))

        p.w('tensor', 'v', l1_v)
        p.w('tensor', 's', l1_s)
        for tt in range(16):
            Dq(lambda tt=tt: sec.dma_start(out=xres[:, KB * tt:KB * tt + KB],
                                           in_=xq_d[128 * tt:128 * tt + 128, :]))
        d_xres = p.cnt['d']
        ccin_d_count = 0
        SC = [PSB[:, 0:512], PSB[:, 512:1024], PSC[:, 0:512], PSC[:, 512:1024]]

        def emit_pv(kt, nkt):
            eb0 = 2048 * (kt % 2)
            diag_r = kt - 4 * (nkt // 4 - 1)
            for h in range(HLOC):
                if diag_r >= 0:
                    p.w('tensor', 'v', m[f'mask_h{h}'])
                else:
                    p.w('tensor', 's', m[f'exp_h{h}'])
                if kt == 0:
                    p.w('tensor', 'v', m.get('norm_v_done', 0))
                Tn(lambda h=h, kt=kt, nkt=nkt, eb0=eb0: nc.tensor.matmul(
                    PSA[0:65, 512 * h:512 * h + 512],
                    vaug[:, 260 * kt + 65 * h:260 * kt + 65 * h + 65],
                    ebuf[:, eb0 + 512 * h:eb0 + 512 * h + 512],
                    start=(kt == 0), stop=(kt == nkt - 1)))
                m[f'pv_h{h}'] = p.cnt['t']
            m[f'pv_p{kt % 2}'] = p.cnt['t']

        for qb in range(NQB):
            nkt = 4 * (qb + 1)
            for kt in range(nkt):
                diag_r = kt - 4 * qb
                for h in range(HLOC):
                    u, rb0 = h // 2, 64 * (h % 2)
                    p.w('tensor', 's', m.get(f'exp_h{h}', 0))
                    if kt == 0:
                        p.w('tensor', 'v', m.get('norm_v_done', 0))
                    Tn(lambda u=u, rb0=rb0, kt=kt, qb=qb, h=h: nc.tensor.matmul(
                        SC[h],
                        kfm[u][rb0:rb0 + 64, 128 * kt:128 * kt + 128],
                        qfm[u][rb0:rb0 + 64, QB * qb:QB * qb + QB],
                        start=True, stop=True))
                    m[f'score_h{h}'] = p.cnt['t']
                eb0 = 2048 * (kt % 2)
                if kt > 0:
                    emit_pv(kt - 1, nkt)
                for grp in range(2):
                    ps_sc = PSB if grp == 0 else PSC
                    p.w('scalar', 't', m[f'score_h{2 * grp + 1}'])
                    p.w('scalar', 't', m.get(f'pv_p{kt % 2}', 0))
                    se = S(lambda grp=grp, eb0=eb0, ps_sc=ps_sc: nc.scalar.activation(
                        ebuf[:, eb0 + 1024 * grp:eb0 + 1024 * grp + 1024], ps_sc[:, 0:1024],
                        AF.Exp, bias=EXP_BIAS, scale=EXP_SCALE))
                    m[f'exp_h{2 * grp}'] = se
                    m[f'exp_h{2 * grp + 1}'] = se
                    if diag_r >= 0:
                        p.w('vector', 's', se)
                        p.w('vector', 'd', d_masks)
                        mv = V(lambda grp=grp, diag_r=diag_r, eb0=eb0: nc.vector.tensor_tensor(
                            ebuf[:, eb0 + 1024 * grp:eb0 + 1024 * grp + 1024].rearrange(
                                "p (h n) -> p h n", n=512),
                            ebuf[:, eb0 + 1024 * grp:eb0 + 1024 * grp + 1024].rearrange(
                                "p (h n) -> p h n", n=512),
                            masks[:, 512 * diag_r:512 * diag_r + 512].rearrange(
                                "p n -> p () n").broadcast_to([128, 2, 512]),
                            op=Op.mult))
                        m[f'mask_h{2 * grp}'] = mv
                        m[f'mask_h{2 * grp + 1}'] = mv
            emit_pv(nkt - 1, nkt)
            t_pv = p.cnt['t']
            p.w('vector', 'd', m.get('ctxn_free', 0))
            p.w('vector', 't', t_pv)
            p.w('vector', 'v', p.cnt['v'])
            vr2 = V(lambda: nc.vector.reciprocal(rl4[:, :], PSA[64:65, 0:2048]))
            p.w('tensor', 'v', vr2)
            tr = 0
            for h in range(HLOC):
                tr = Tn(lambda h=h: nc.tensor.matmul(
                    SC[h][0:64, :], onesc[:], rl4[:, 512 * h:512 * h + 512],
                    start=True, stop=True))
            p.w('vector', 't', tr)
            V(lambda: nc.vector.tensor_copy(rbuf4[:, 0:1024], PSB[0:64, 0:1024]))
            vR = V(lambda: nc.vector.tensor_copy(rbuf4[:, 1024:2048], PSC[0:64, 0:1024]))
            p.w('vector', 'v', vR)
            for rp in range(2):
                V(lambda rp=rp: nc.vector.tensor_tensor(
                    ctxn[64 * rp:64 * rp + 64, 0:1024].rearrange("p (a n) -> p a n", n=512),
                    PSA[0:64, 0:2048].rearrange("p (a b n) -> p a b n", b=2, n=512)[:, :, rp, :],
                    rbuf4[:, 0:2048].rearrange("p (a b n) -> p a b n", b=2, n=512)[:, :, rp, :],
                    op=Op.mult))
            v_ctxn = p.cnt['v']
            m['norm_v_done'] = v_ctxn
            m['psb_guard_v'] = v_ctxn
            p.w('sync', 'v', v_ctxn)
            for u in range(2):
                ccin_d_count = Dq(lambda u=u, qb=qb: sec.dma_start(
                    out=ccin_d[256 * (qb // 4) + 128 * u:256 * (qb // 4) + 128 * u + 128,
                               QB * (qb % 4):QB * (qb % 4) + QB],
                    in_=ctxn[:, QB * u:QB * u + QB]))
            m['ctxn_free'] = ccin_d_count
            if qb % 4 == 3:
                cq = qb // 4
                p.w('gpsimd', 'd', ccin_d_count)
                p.c(lambda cq=cq: nc.gpsimd.collective_compute(
                    "AllGather", mybir.AluOpType.bypass,
                    replica_groups=[[0, 1, 2, 3], [4, 5, 6, 7]],
                    ins=[ccin_d[256 * cq:256 * cq + 256, :].opt()],
                    outs=[ccout_d[1024 * cq:1024 * cq + 1024, :].opt()]), key=str(4 + cq))

        m['psa_free_wo'] = p.cnt['v']
        m['exp_done1'] = p.cnt['s']
        t_quad_end = p.cnt['t']

        p.w('sync', 't', t_quad_end)
        p.w('sync', 's', m.get('exp_done1', 0))
        p.w('sync', 'v', m.get('norm_v_done', 0))
        wo_v = load_weight_cast(wo_s, wo_d, D, KB)
        for k in '4567':
            p.w('gpsimd', k, 1)
        p.w('gpsimd', 'd', d_gidx)

        for tb in range(4):
            cur = tb % 2
            p.w('gpsimd', 't', t_quad_end)
            p.w('gpsimd', 't', m.get(f'ctxg_used{cur}', 0))
            gl = 0
            for ch in range(8):
                gl = G(lambda ch=ch, tb=tb, cur=cur: nc.gpsimd.indirect_dma_start(
                    out=ctxg[cur][:, QB * ch:QB * ch + QB], out_offset=None,
                    in_=ccout_rows,
                    in_offset=bass.IndirectOffsetOnAxis(ap=gidx[:, 4 * ch + tb:4 * ch + tb + 1], axis=0)))
            p.w('tensor', 'g', gl)
            p.w('tensor', 'v', wo_v)
            p.w('tensor', 'v', m.get('psa_free_wo', 0))
            for sub in range(4):
                slot = sub % 2
                p.w('tensor', 'v', m.get(f'wo_res{slot}', 0))
                for ch in range(8):
                    for hf in range(2):
                        Tn(lambda ch=ch, sub=sub, slot=slot, cur=cur, hf=hf: nc.tensor.matmul(
                            PSA[:, 1024 * slot + 512 * hf:1024 * slot + 512 * hf + 512],
                            ctxg[cur][:, QB * ch + 128 * sub:QB * ch + 128 * sub + 128],
                            wo_s[:, KB * ch + 512 * hf:KB * ch + 512 * hf + 512],
                            start=(ch == 0), stop=(ch == 7)))
                t_wo = p.cnt['t']
                m[f'ctxg_used{cur}'] = t_wo
                p.w('vector', 't', t_wo)
                p.w('vector', 'd', d_xres)
                ti = 4 * tb + sub
                va = V(lambda ti=ti, slot=slot: nc.vector.tensor_tensor(
                    xres[:, KB * ti:KB * ti + KB], xres[:, KB * ti:KB * ti + KB],
                    PSA[:, 1024 * slot:1024 * slot + 1024], op=Op.add))
                m[f'wo_res{slot}'] = va
            m[f'wo_tb_done{tb}'] = p.cnt['v']
        wo_res_v = p.cnt['v']
        t_wo_end = p.cnt['t']
        m['hqtm_used'] = t_wo_end
        m['h_guard_t'] = t_wo_end

        y_store = 0
        m['psa_slot0'] = wo_res_v
        m['psa_slot1'] = wo_res_v
        pendB = [None]

        def emit_passB_res(t_dnB, tiles):
            for j, ti in enumerate(tiles):
                p.w('vector', 't', t_dnB)
                va = V(lambda ti=ti, j=j: nc.vector.tensor_tensor(
                    xres[:, KB * ti:KB * ti + KB], xres[:, KB * ti:KB * ti + KB],
                    PSA[:, 1024 * j:1024 * j + 1024], op=Op.add))
                m[f'psa_slot{j}'] = va
                p.w('sync', 'v', va)
                m['y_store'] = Dq(lambda ti=ti: sec.dma_start(
                    out=y_d[128 * ti:128 * ti + 128, :],
                    in_=xres[:, KB * ti:KB * ti + KB]))

        for g in range(4):
            for pr in range(2):
                p.w('vector', 'v', m[f'wo_tb_done{g}'])
                p.w('scalar', 'v', m[f'wo_tb_done{g}'])
                p.w('vector', 'v', m.get('stat_v_free', 0))
                p.w('scalar', 'v', m.get('stat_v_free', 0))
                s_last = 0
                for t2 in range(2):
                    xsl = xres[:, KB * (4 * g + 2 * pr + t2):KB * (4 * g + 2 * pr + t2) + KB]
                    V(lambda xsl=xsl, t2=t2: nc.vector.tensor_reduce(
                        stat[:, t2:t2 + 1], xsl, axis=AX, op=Op.add))
                    s_last = S(lambda xsl=xsl, t2=t2: nc.scalar.activation(
                        sqscr[:, :], xsl, AF.Square, accum_out=stat[:, 8 + t2:8 + t2 + 1]))
                p.w('vector', 's', s_last)
                p.w('vector', 'v', p.cnt['v'])
                V(lambda: nc.vector.tensor_scalar(stat[:, 0:2], stat[:, 0:2], 1.0 / D, None, op0=Op.mult))
                p.w('vector', 'v', p.cnt['v'])
                V(lambda: nc.vector.tensor_scalar(stat[:, 8:10], stat[:, 8:10], 1.0 / D, None, op0=Op.mult))
                p.w('vector', 'v', p.cnt['v'])
                V(lambda: nc.vector.tensor_tensor(stat[:, 16:18], stat[:, 0:2], stat[:, 0:2], op=Op.mult))
                p.w('vector', 'v', p.cnt['v'])
                vv3 = V(lambda: nc.vector.tensor_tensor(stat[:, 16:18], stat[:, 8:10], stat[:, 16:18], op=Op.subtract))
                p.w('scalar', 'v', vv3)
                sv3 = S(lambda: nc.scalar.activation(stat[:, 20:22], stat[:, 16:18], AF.Sqrt, bias=LN_EPS))
                p.w('vector', 's', sv3)
                vr3 = V(lambda: nc.vector.reciprocal(stat[:, 24:26], stat[:, 20:22]))
                p.w('vector', 'v', vr3)
                for t2 in range(2):
                    xsl = xres[:, KB * (4 * g + 2 * pr + t2):KB * (4 * g + 2 * pr + t2) + KB]
                    V(lambda xsl=xsl, t2=t2: nc.vector.tensor_scalar(
                        xq32[:, KB * t2:KB * t2 + KB], xsl,
                        stat[:, t2:t2 + 1], stat[:, 24 + t2:24 + t2 + 1],
                        op0=Op.subtract, op1=Op.mult))
                m['stat_v_free'] = p.cnt['v']
                if pr == 0 and pendB[0] is not None:
                    emit_passB_res(*pendB[0])
                    pendB[0] = None
                for t2 in range(2):
                    tl = 2 * pr + t2
                    qv = quant(xqtm[:, KB * t2:KB * t2 + KB], xq32[:, KB * t2:KB * t2 + KB],
                               KB, qsF, twait=m.get('xqtm_used', 0))
                    par = m.get('pscpar', 0)
                    p.w('tensor', 'v', qv)
                    p.w('tensor', 'v', m.get(f'pscT{par}', 0))
                    tl_t = 0
                    for ch in range(8):
                        tl_t = Tn(lambda t2=t2, ch=ch, par=par: nc.tensor.transpose(
                            PSC_b[:, 1024 * par + 128 * ch:1024 * par + 128 * ch + 128],
                            xqtm[:, KB * t2 + 128 * ch:KB * t2 + 128 * ch + 128], identb[:]))
                    m['xqtm_used'] = tl_t
                    p.w('vector', 't', tl_t)
                    p.w('vector', 't', m.get('xqfm_used', 0))
                    vxf = V(lambda tl=tl, par=par: nc.vector.tensor_copy(
                        xqfm[:, KB * tl:KB * tl + KB], PSC_b[:, 1024 * par:1024 * par + KB]))
                    m[f'pscT{par}'] = vxf
                    m[f'xqfm_rdy{tl}'] = vxf
                    m['pscpar'] = par ^ 1
            for si in range(6):
                wslice = min(512, II - 512 * si)
                cur = si % 2
                pcw = si // 2
                p.w('sync', '1', 1)
                p.w('sync', 't', m.get(f'wgu_used{cur}', 0))
                p.w('gpsimd', '2', 1)
                p.w('gpsimd', 't', m.get(f'wgu_used{cur}', 0))
                dgu = 0
                ggu = 0
                for wch in range(8):
                    srcg = wgT_d[:, 512 * si:512 * si + wslice]
                    dgu = Dq(lambda srcg=srcg, wch=wch, cur=cur, wslice=wslice: sec.dma_start(
                        out=wgu[2 * cur][:, QB * wch:QB * wch + wslice],
                        in_=srcg[128 * wch:128 * wch + 128, :]))
                    srcu = wuT_d[:, 512 * si:512 * si + wslice]
                    ggu = G(lambda srcu=srcu, wch=wch, cur=cur, wslice=wslice: sec.dma_start(
                        out=wgu[2 * cur + 1][:, QB * wch:QB * wch + wslice],
                        in_=srcu[128 * wch:128 * wch + 128, :]))
                p.w('tensor', 'd', dgu)
                p.w('tensor', 'g', ggu)
                p.w('tensor', 'v', m.get('psb_guard_v', 0))
                for tl in range(4):
                    pp = tl % 2
                    gb = PSB[:, 0:wslice] if pp == 0 else PSA[:, 0:wslice]
                    ub = PSB[:, 512:512 + wslice] if pp == 0 else PSA[:, 512:512 + wslice]
                    p.w('tensor', 'v', m[f'xqfm_rdy{tl}'])
                    if pp == 1:
                        p.w('tensor', 'v', m.get('psa_slot0', 0))
                    p.w('tensor', 'v', m.get(f'h_done{pp}', 0))
                    p.w('tensor', 's', m.get(f'silu_done{pp}', 0))
                    for ch in range(8):
                        Tn(lambda tl=tl, ch=ch, cur=cur, wslice=wslice, gb=gb: nc.tensor.matmul(
                            gb,
                            xqfm[:, 1024 * tl + 128 * ch:1024 * tl + 128 * ch + 128],
                            wgu[2 * cur][:, QB * ch:QB * ch + wslice],
                            start=(ch == 0), stop=(ch == 7)))
                    t_g = p.cnt['t']
                    for ch in range(8):
                        Tn(lambda tl=tl, ch=ch, cur=cur, wslice=wslice, ub=ub: nc.tensor.matmul(
                            ub,
                            xqfm[:, 1024 * tl + 128 * ch:1024 * tl + 128 * ch + 128],
                            wgu[2 * cur + 1][:, QB * ch:QB * ch + wslice],
                            start=(ch == 0), stop=(ch == 7)))
                    t_u = p.cnt['t']
                    m[f'wgu_used{cur}'] = t_u
                    m['xqfm_used'] = t_u
                    p.w('scalar', 't', t_g)
                    p.w('scalar', 'v', m.get(f'h_done{pp}', 0))
                    ssl = S(lambda wslice=wslice, pp=pp, gb=gb: nc.scalar.activation(
                        silu_s[:, 512 * pp:512 * pp + wslice], gb, AF.Silu))
                    m[f'silu_done{pp}'] = ssl
                    p.w('vector', 't', t_u)
                    p.w('vector', 't', m.get('h_guard_t', 0))
                    p.w('vector', 's', ssl)
                    hv = V(lambda tl=tl, si=si, wslice=wslice, pp=pp, ub=ub:
                           nc.vector.tensor_tensor(
                        hbuf[:, 1024 * tl + 512 * (si % 2):
                             1024 * tl + 512 * (si % 2) + wslice],
                        ub, silu_s[:, 512 * pp:512 * pp + wslice], op=Op.mult))
                    m[f'h_done{pp}'] = hv
                if si % 2 == 1 or si == 5:
                    pc = si // 2
                    w = 1024 if pc < 2 else 768
                    nch = w // 128
                    c0 = 1024 * pc
                    for tl in range(4):
                        qv = quant(hqtm[:, c0:c0 + w],
                                   hbuf[:, 1024 * tl:1024 * tl + w],
                                   w, qsF, twait=m.get('hqtm_used', 0))
                        par = m.get('pscpar', 0)
                        p.w('tensor', 'v', qv)
                        p.w('tensor', 'v', m.get(f'pscT{par}', 0))
                        tl_t = 0
                        for j in range(nch):
                            tl_t = Tn(lambda c0=c0, j=j, par=par: nc.tensor.transpose(
                                PSC_b[:, 1024 * par + 128 * j:1024 * par + 128 * j + 128],
                                hqtm[:, c0 + 128 * j:c0 + 128 * j + 128], identb[:]))
                        m['hqtm_used'] = tl_t
                        p.w('vector', 't', tl_t)
                        p.w('vector', 't', m.get('hqfm_used', 0))
                        hv2 = V(lambda tl=tl, pc=pc, nch=nch, par=par: nc.vector.tensor_copy(
                            hqfm_v[:, 8 * pc:8 * pc + nch, 128 * tl:128 * tl + 128],
                            PSC_b[:, 1024 * par:1024 * par + 128 * nch].rearrange("p (c i) -> p c i", i=128)))
                        m[f'pscT{par}'] = hv2
                        m[f'hqfm_rdy_{tl}_{pc}'] = hv2
                        m['pscpar'] = par ^ 1
            for half in range(2):
                for ch in range(22):
                    pc = min(ch // 8, 2)
                    slot = m.get('wdnslot', 0)
                    m['wdnslot'] = (slot + 1) % 6
                    qi = slot % 3
                    qk = ('d', 'g', 'e')[qi]
                    qeng = ('sync', 'gpsimd', 'scalar')[qi]
                    p.w(qeng, '3', 1)
                    p.w(qeng, 't', t_wo_end)
                    p.w(qeng, 't', m.get(f'wdn_used{slot}', 0))
                    fn = (lambda ch=ch, slot=slot: sec.dma_start(
                        out=wdnR[slot][:, :], in_=wdT_d[128 * ch:128 * ch + 128, :]))
                    dwd = {'d': Dq, 'g': G, 'e': p.e}[qk](fn)
                    p.w('tensor', qk, dwd)
                    for j in range(2):
                        tl = 2 * half + j
                        p.w('tensor', 'v', m[f'hqfm_rdy_{tl}_{pc}'])
                        if ch == 0:
                            p.w('tensor', 'v', m[f'psa_slot{j}'])
                            p.w('tensor', 'v', m.get('h_done1', 0))
                            p.w('tensor', 's', m.get('silu_done1', 0))
                        for hf in range(2):
                            Tn(lambda ch=ch, j=j, tl=tl, slot=slot, hf=hf: nc.tensor.matmul(
                                PSA[:, 1024 * j + 512 * hf:1024 * j + 512 * hf + 512],
                                hqfm_v[:, ch, 128 * tl:128 * tl + 128],
                                wdnR[slot][:, 512 * hf:512 * hf + 512],
                                start=(ch == 0), stop=(ch == 21)))
                    m[f'wdn_used{slot}'] = p.cnt['t']
                t_dn = p.cnt['t']
                if half == 0:
                    for j in range(2):
                        ti = 4 * g + j
                        p.w('vector', 't', t_dn)
                        va = V(lambda ti=ti, j=j: nc.vector.tensor_tensor(
                            xres[:, KB * ti:KB * ti + KB], xres[:, KB * ti:KB * ti + KB],
                            PSA[:, 1024 * j:1024 * j + 1024], op=Op.add))
                        m[f'psa_slot{j}'] = va
                        p.w('sync', 'v', va)
                        m['y_store'] = Dq(lambda ti=ti: sec.dma_start(
                            out=y_d[128 * ti:128 * ti + 128, :],
                            in_=xres[:, KB * ti:KB * ti + KB]))
            m['hqfm_used'] = p.cnt['t']
            pendB[0] = (p.cnt['t'], [4 * g + 2, 4 * g + 3])
        emit_passB_res(*pendB[0])
        pendB[0] = None
        y_store = m['y_store']

        p.w('sync', 'd', y_store)

    with nc.Block() as block:
        @block.sync
        def _(sec):
            program(P(nc, 'sync', sec, sems), sec)

        @block.vector
        def _(sec):
            program(P(nc, 'vector', sec, sems), sec)

        @block.scalar
        def _(sec):
            program(P(nc, 'scalar', sec, sems), sec)

        @block.tensor
        def _(sec):
            program(P(nc, 'tensor', sec, sems), sec)

        @block.gpsimd
        def _(sec):
            program(P(nc, 'gpsimd', sec, sems), sec)

    return nc


_NC_CACHE = None


def _consts():
    mm = np.zeros((128, 4, 512), np.float32)
    kk = np.arange(128)[:, None]
    qq = np.arange(512)[None, :]
    for r in range(4):
        mm[:, r, :] = (qq >= kk + 128 * r)
    masks = np.ascontiguousarray(mm.reshape(128, 2048)).astype(ml_dtypes.bfloat16)
    identb = np.eye(128, dtype=ml_dtypes.bfloat16)
    cvals = np.tile(np.array([[EXP_SCALE, EXP_BIAS, LN_EPS, 0.0]], np.float32), (128, 1))
    return identb, masks, cvals


def kernel(L0_attn_x, L0_attn_W_DKV, L0_attn_W_UK, L0_attn_W_UV, L0_attn_W_UQ,
           L0_attn_W_O, L0_ffn_w_gate, L0_ffn_w_up, L0_ffn_w_down):
    global _NC_CACHE
    x = np.ascontiguousarray(np.asarray(L0_attn_x, np.float32))
    W_DKV = np.ascontiguousarray(np.asarray(L0_attn_W_DKV, np.float32))
    W_UK = np.ascontiguousarray(np.asarray(L0_attn_W_UK, np.float32))
    W_UV = np.ascontiguousarray(np.asarray(L0_attn_W_UV, np.float32))
    W_UQ = np.ascontiguousarray(np.asarray(L0_attn_W_UQ, np.float32))
    W_O = np.ascontiguousarray(np.asarray(L0_attn_W_O, np.float32))
    wg = np.ascontiguousarray(np.asarray(L0_ffn_w_gate, np.float32))
    wu = np.ascontiguousarray(np.asarray(L0_ffn_w_up, np.float32))
    wd = np.ascontiguousarray(np.asarray(L0_ffn_w_down, np.float32))

    identb, masks, cvals = _consts()
    if _NC_CACHE is None:
        _NC_CACHE = build_nc()
    nc = _NC_CACHE

    in_maps = []
    pidx = np.arange(128, dtype=np.uint32)
    for c in range(8):
        b, hq = c // 4, c % 4
        hs = slice(HCOL * hq, HCOL * hq + HCOL)
        gidx = np.zeros((128, 64), np.uint32)
        for ch in range(8):
            for tb in range(4):
                gidx[:, 4 * ch + tb] = 4096 * hq + (128 * ch + pidx) * 4 + tb
        wdsh = np.zeros((1024, 384), np.float32)
        lo = 384 * c
        hi = min(lo + 384, II)
        wdsh[:, 0:hi - lo] = wd[:, lo:hi]
        in_maps.append({
            "x": x[b],
            "xquarter": np.ascontiguousarray(x[b, 2048 * hq:2048 * hq + 2048]),
            "wdkv": W_DKV,
            "wuq": np.ascontiguousarray(W_UQ[:, hs]),
            "wuk": np.ascontiguousarray(W_UK[:, hs]),
            "wuv": np.ascontiguousarray(W_UV[:, hs]),
            "wo": W_O,
            "wgsh": np.ascontiguousarray(wg[:, 128 * c:128 * c + 128]),
            "wush": np.ascontiguousarray(wu[:, 128 * c:128 * c + 128]),
            "wdsh": wdsh,
            "identb": identb, "masks": masks, "cvals": cvals,
            "gidx": gidx,
        })

    res = run_bass_kernel_spmd(nc, in_maps, core_ids=list(range(8)))
    out = np.empty((2, T, D), np.float32)
    for c in range(8):
        b, hq = c // 4, c % 4
        out[b, 2048 * hq:2048 * hq + 2048] = np.asarray(res.results[c]["y"])
    return out


# version 24 (final)
# speedup vs baseline: 1.1873x; 1.0018x over previous
DEBUG_TAPS = False

import numpy as np
import ml_dtypes
import concourse.bass as bass
import concourse.mybir as mybir
from concourse.bass_utils import run_bass_kernel_spmd
from concourse.alu_op_type import AluOpType

F32 = mybir.dt.float32
BF16 = mybir.dt.bfloat16
I32 = mybir.dt.int32
U32 = mybir.dt.uint32
AF = mybir.ActivationFunctionType
AX = mybir.AxisListType.X
Op = AluOpType

T = 8192
D = 1024
CL = 256
HLOC = 4
HCOL = 256
II = 2816
QB = 512
NQB = 16
NKT = 64
KB = 1024
EXP_SCALE = 0.125
EXP_BIAS = -3.0
LN_EPS = 1e-5

ENGS = ('tensor', 'vector', 'scalar', 'gpsimd', 'sync')


class P:

    def __init__(self, nc, active, eng, sems):
        self.nc = nc
        self.active = active
        self.eng = eng
        self.sems = sems
        self.cnt = dict.fromkeys("tvsgdce1234567", 0)
        self.wm = {e: dict.fromkeys("tvsgdce1234567", 0) for e in ENGS}

    def w(self, tgt, key, count):
        if not count or count <= self.wm[tgt][key]:
            return
        self.wm[tgt][key] = count
        if self.active == tgt:
            self.eng.wait_ge(self.sems[key], count)

    def _do(self, key, inc, fn, act):
        self.cnt[key] += inc
        if self.active == act:
            fn().then_inc(self.sems[key], inc)
        return self.cnt[key]

    def t(self, fn):
        return self._do('t', 1, fn, 'tensor')

    def v(self, fn):
        return self._do('v', 1, fn, 'vector')

    def s(self, fn):
        return self._do('s', 1, fn, 'scalar')

    def g(self, fn):
        return self._do('g', 16, fn, 'gpsimd')

    def d(self, fn):
        return self._do('d', 16, fn, 'sync')

    def e(self, fn):
        return self._do('e', 16, fn, 'scalar')

    def c(self, fn, key='c'):
        self.cnt[key] += 1
        if self.active == 'gpsimd':
            fn().then_inc(self.sems[key])
        return self.cnt[key]


def build_nc():
    nc = bass.Bass("TRN2", target_bir_lowering=False)

    x_d = nc.dram_tensor("x", [T, D], F32, kind="ExternalInput").ap()
    xq_d = nc.dram_tensor("xquarter", [2048, D], F32, kind="ExternalInput").ap()
    wdkv_d = nc.dram_tensor("wdkv", [D, CL], F32, kind="ExternalInput").ap()
    wuq_d = nc.dram_tensor("wuq", [D, HCOL], F32, kind="ExternalInput").ap()
    wuk_d = nc.dram_tensor("wuk", [CL, HCOL], F32, kind="ExternalInput").ap()
    wuv_d = nc.dram_tensor("wuv", [CL, HCOL], F32, kind="ExternalInput").ap()
    wo_d = nc.dram_tensor("wo", [D, D], F32, kind="ExternalInput").ap()
    wgsh_d = nc.dram_tensor("wgsh", [II, 128], F32, kind="ExternalInput").ap()
    wush_d = nc.dram_tensor("wush", [II, 128], F32, kind="ExternalInput").ap()
    wdsh_d = nc.dram_tensor("wdsh", [D, 384], F32, kind="ExternalInput").ap()
    identb_d = nc.dram_tensor("identb", [128, 128], BF16, kind="ExternalInput").ap()
    masks_d = nc.dram_tensor("masks", [128, 2048], BF16, kind="ExternalInput").ap()
    cvals_d = nc.dram_tensor("cvals", [128, 4], F32, kind="ExternalInput").ap()
    gidx_d = nc.dram_tensor("gidx", [128, 64], U32, kind="ExternalInput").ap()
    y_d = nc.dram_tensor("y", [2048, D], F32, kind="ExternalOutput").ap()
    y2_d = nc.dram_tensor("y2", [2048, D], F32, kind="ExternalOutput").ap() if DEBUG_TAPS else None
    y3_d = nc.dram_tensor("y3", [2048, D], BF16, kind="ExternalOutput").ap() if DEBUG_TAPS else None
    y4_d = nc.dram_tensor("y4", [2048, II], BF16, kind="ExternalOutput").ap() if DEBUG_TAPS else None
    wgin_d = nc.dram_tensor("wgin", [128, II], BF16).ap()
    wuin_d = nc.dram_tensor("wuin", [128, II], BF16).ap()
    wdin_d = nc.dram_tensor("wdin", [384, D], BF16).ap()
    wgT_d = nc.dram_tensor("wgT", [D, II], BF16, addr_space="Shared").ap()
    wuT_d = nc.dram_tensor("wuT", [D, II], BF16, addr_space="Shared").ap()
    wdT_d = nc.dram_tensor("wdT", [3072, D], BF16, addr_space="Shared").ap()
    ccin_d = nc.dram_tensor("ccin", [4 * HCOL, 2048], BF16).ap()
    ccout_d = nc.dram_tensor("ccout", [4096, 2048], BF16).ap()
    ccout_rows = ccout_d.rearrange("r (c n) -> (r c) n", n=QB)

    def sb(name, shape, dt):
        return nc.alloc_sbuf_tensor("s_" + name, shape, dt).ap()

    xres = sb("xres", [128, 16 * KB], F32)
    xres_v = xres.rearrange("p (tt k) -> p tt k", k=KB)
    identb = sb("identb", [128, 128], BF16)
    cvals = sb("cvals", [128, 4], F32)
    nc.const_aps.register(F32, EXP_SCALE, cvals[:, 0:1])
    nc.const_aps.register(F32, EXP_BIAS, cvals[:, 1:2])
    nc.const_aps.register(F32, LN_EPS, cvals[:, 2:3])
    gidx = sb("gidx", [128, 64], U32)
    stat = sb("stat", [128, 32], F32)
    sqscr = sb("sqscr", [128, KB], F32)
    onesc = sb("onesc", [1, 64], F32)

    from concourse.bass_primitives import MemorySpace
    AB = (nc.bytes_allocated(MemorySpace.SBUF) + 511) & ~511
    _names = [0]

    def at(off, shape, dt):
        _names[0] += 1
        return nc.alloc_sbuf_tensor_at(f"a_{_names[0]}", shape, dt, offset=AB + off).ap()

    qfm = [at(16384 * u, [128, T], BF16) for u in range(2)]
    kfm = [at(32768 + 16384 * u, [128, T], BF16) for u in range(2)]
    vaug = at(65536, [128, NKT * 260], BF16)
    wdkv = at(98816, [128, 8 * CL], BF16)
    wuq = at(102912, [128, 8 * HCOL], BF16)
    wuk = at(107008, [128, 2 * HCOL], BF16)
    wuv = at(108032, [128, 2 * HCOL], BF16)
    ebuf = at(109056, [128, 4 * KB], BF16)
    masks = at(98816, [128, 2048], BF16)
    rl4 = at(117248, [1, 4 * QB], F32)
    rbuf4 = at(125440, [64, 4 * QB], F32)
    xtile = at(121344, [128, 2 * KB], F32)
    stage = at(121344, [128, KB], F32)
    stageW = at(121344, [128, 2 * KB], F32)
    wcast = at(129536, [128, KB], BF16)
    wcastW = at(129536, [128, 2 * KB], BF16)
    xntm = at(129536, [128, 2 * KB], BF16)
    ttmp = at(133632, [128, KB], BF16)
    ttmpW = at(133632, [128, 2 * KB], BF16)
    xnfm = at(133632, [128, 8 * 256], BF16)
    ckv = at(137728, [128, 2 * 256], BF16)
    ctxn = at(138752, [128, 2 * QB], BF16)
    ARENA_ATTN_END = 140800

    wo_s = at(0, [128, 8 * KB], BF16)
    ctxg = [at(16384 + 8192 * i, [128, 8 * QB], BF16) for i in range(2)]

    hbuf = at(0, [128, 4 * KB], F32)
    wdnR = [at(16384 + 2048 * i, [128, KB], BF16) for i in range(6)]
    hqtm = at(36864, [128, II], BF16)
    wgu = [at(43008 + 8192 * i, [128, 8 * QB], BF16) for i in range(4)]
    hqfm = at(75776, [128, 22 * 512], BF16)
    hqfm_v = hqfm.rearrange("p (c n) -> p c n", n=512)
    xq32 = at(98304, [128, 2 * KB], F32)
    xqtm = at(106496, [128, 2 * KB], BF16)
    xqfm = at(110592, [128, 4 * KB], BF16)
    qsP = tuple([at(4096 * i, [128, 1024], F32) for i in range(4)]
                + [at(12288, [128, 1024], I32), at(16384, [128, 192], F32),
                   at(17152, [128, 64], F32)])
    qsF = tuple([at(118784 + 4096 * i, [128, 1024], F32) for i in range(4)]
                + [at(131072, [128, 1024], I32), at(135168, [128, 192], F32),
                   at(135936, [128, 64], F32)])
    silu_s = at(136192, [128, 2 * QB], F32)
    assert AB + ARENA_ATTN_END <= 229300, (AB, ARENA_ATTN_END)
    assert AB + 140800 <= 229376, AB

    PSA = nc.alloc_psum_tensor("PSA", [128, 2048], F32).ap()
    PSB = nc.alloc_psum_tensor("PSB", [128, 1024], F32).ap()
    PSC = nc.alloc_psum_tensor("PSC", [128, 1024], F32).ap()
    PSB_b = PSB.bitcast(BF16)
    PSC_b = PSC.bitcast(BF16)

    sems = {}
    for k in "tvsgdce1234567":
        sems[k] = nc.alloc_semaphore(name="sem_" + k)

    def program(p: P, sec):
        V, S, Tn, G, Dq = p.v, p.s, p.t, p.g, p.d
        m = {}

        Dq(lambda: sec.dma_start(out=identb[:], in_=identb_d))
        Dq(lambda: sec.dma_start(out=cvals[:], in_=cvals_d))
        d_gidx = Dq(lambda: sec.dma_start(out=gidx[:], in_=gidx_d))

        def load_weight_cast(dst, src_d, rows, cols):
            last = 0
            for ch in range(rows // 128):
                p.w('sync', 'v', m.get('stage_free', 0))
                dd = Dq(lambda ch=ch: sec.dma_start(out=stage[:, 0:cols],
                                                    in_=src_d[128 * ch:128 * ch + 128, :]))
                p.w('vector', 'd', dd)
                last = V(lambda ch=ch: nc.vector.tensor_copy(
                    dst[:, cols * ch:cols * ch + cols], stage[:, 0:cols]))
                m['stage_free'] = last
            return last

        def quant(dst, src, width, bufs, twait=0):
            q0f, q1f, q2f, q3f, qsif, qsmf, qsm2f = bufs
            nb = width // 16
            p.w('vector', 'v', p.cnt['v'])
            p.w('vector', 't', twait)
            src3 = src.rearrange("p (b k) -> p b k", k=16)
            dst3 = dst.rearrange("p (b k) -> p b k", k=16)

            def vs(fn):
                r = V(fn)
                p.w('vector', 'v', r)
                return r

            q0, q1, q2, q3 = (q[:, 0:width] for q in (q0f, q1f, q2f, q3f))
            qsi_ = qsif[:, 0:width]
            q03 = q0.rearrange("p (b k) -> p b k", k=16)
            q13 = q1.rearrange("p (b k) -> p b k", k=16)
            vs(lambda: nc.vector.tensor_reduce(qsmf[:, 0:nb], src3, axis=AX, op=Op.max,
                                               apply_absolute_value=True))
            vs(lambda: nc.vector.tensor_scalar(qsmf[:, 64:64 + nb], qsmf[:, 0:nb],
                                               1.0 / 3.0, 1e-12, op0=Op.mult, op1=Op.max))
            vs(lambda: nc.vector.tensor_scalar(qsmf[:, 128:128 + nb], qsmf[:, 0:nb],
                                               1.0 / 12.0, None, op0=Op.mult))
            vs(lambda: nc.vector.reciprocal(qsm2f[:, 0:nb], qsmf[:, 64:64 + nb]))
            binvh = qsm2f[:, 0:nb].broadcast_to([128, nb, 16])
            bsc12 = qsmf[:, 128:128 + nb].broadcast_to([128, nb, 16])
            v0 = vs(lambda: nc.vector.tensor_tensor(q03, src3, binvh, op=Op.mult))
            p.w('scalar', 'v', v0)
            sq = S(lambda: nc.scalar.activation(q1, q0, AF.Square))
            p.w('vector', 's', sq)
            vs(lambda: nc.vector.tensor_scalar(q2, q1, 1.0, 1.0, op0=Op.is_lt, op1=Op.add))
            vs(lambda: nc.vector.tensor_scalar(q3, q1, 4.0, 1.0, op0=Op.is_lt, op1=Op.add))
            vs(lambda: nc.vector.tensor_tensor(q2, q2, q3, op=Op.mult))
            vs(lambda: nc.vector.tensor_tensor(qsi_, q0, q2, op=Op.mult))
            vs(lambda: nc.vector.tensor_copy(q0, qsi_))
            vs(lambda: nc.vector.tensor_scalar(q2, q1, 1.0, 1.0, op0=Op.is_ge, op1=Op.add))
            vs(lambda: nc.vector.tensor_scalar(q3, q1, 4.0, 1.0, op0=Op.is_ge, op1=Op.add))
            vs(lambda: nc.vector.tensor_tensor(q2, q2, q3, op=Op.mult))
            vs(lambda: nc.vector.tensor_tensor(q1, q0, q2, op=Op.mult))
            return vs(lambda: nc.vector.tensor_tensor(dst3, q13, bsc12, op=Op.mult))



        def prep_col(src_d, in_d, rows):
            for r0 in range(0, rows, 1024):
                rlen = min(1024, rows - r0)
                nch = rlen // 128
                p.w('sync', 'v', m.get('stage_free', 0))
                dd = Dq(lambda src_d=src_d, r0=r0, rlen=rlen: sec.dma_start(
                    out=stageW[:, 0:rlen].rearrange("p (a c) -> p a c", c=128),
                    in_=src_d[r0:r0 + rlen, :].rearrange("(a p) c -> p a c", p=128)))
                p.w('vector', 'd', dd)
                cv = quant(wcastW[:, 0:rlen], stageW[:, 0:rlen], rlen, qsP,
                           twait=m.get('wprep_t', 0))
                m['stage_free'] = cv
                p.w('tensor', 'v', cv)
                p.w('tensor', 'v', m.get('psbT_free', 0))
                p.w('tensor', 'd', 16)
                tl = 0
                for j in range(nch):
                    tl = Tn(lambda j=j: nc.tensor.transpose(
                        PSB_b[:, 128 * j:128 * j + 128],
                        wcastW[:, 128 * j:128 * j + 128], identb[:]))
                m['wprep_t'] = tl
                p.w('vector', 't', tl)
                p.w('vector', 'd', m.get('ttmp_free', 0))
                ev = V(lambda rlen=rlen: nc.vector.tensor_copy(
                    ttmpW[:, 0:rlen], PSB_b[:, 0:rlen]))
                m['psbT_free'] = ev
                p.w('sync', 'v', ev)
                m['ttmp_free'] = Dq(lambda in_d=in_d, r0=r0, rlen=rlen: sec.dma_start(
                    out=in_d[:, r0:r0 + rlen], in_=ttmpW[:, 0:rlen]))
            return m['ttmp_free']

        def prep_wd():
            for g in range(2):
                r0 = 512 * g
                p.w('sync', 'v', m.get('stage_free', 0))
                dd = Dq(lambda r0=r0: sec.dma_start(
                    out=stageW[:, 0:1536].rearrange("p (a c) -> p a c", c=384),
                    in_=wdsh_d[r0:r0 + 512, :].rearrange("(a p) c -> p a c", p=128)))
                p.w('vector', 'd', dd)
                cv = 0
                for off in range(0, 1536, 768):
                    cv = quant(wcastW[:, off:off + 768], stageW[:, off:off + 768],
                               768, qsP, twait=m.get('wprep_t', 0))
                m['stage_free'] = cv
                p.w('tensor', 'v', cv)
                p.w('tensor', 'v', m.get('psbT_free', 0))
                p.w('tensor', 'd', 16)
                tl = 0
                for cb in range(3):
                    for a in range(4):
                        tl = Tn(lambda cb=cb, a=a: nc.tensor.transpose(
                            PSB_b[:, 512 * cb + 128 * a:512 * cb + 128 * a + 128],
                            wcastW[:, 384 * a + 128 * cb:384 * a + 128 * cb + 128],
                            identb[:]))
                m['wprep_t'] = tl
                p.w('vector', 't', tl)
                p.w('vector', 'd', m.get('ttmp_free', 0))
                ev = V(lambda: nc.vector.tensor_copy(ttmpW[:, 0:1536], PSB_b[:, 0:1536]))
                m['psbT_free'] = ev
                p.w('sync', 'v', ev)
                for cb in range(3):
                    m['ttmp_free'] = Dq(lambda cb=cb, r0=r0: sec.dma_start(
                        out=wdin_d[128 * cb:128 * cb + 128, r0:r0 + 512],
                        in_=ttmpW[:, 512 * cb:512 * cb + 512]))
            return m['ttmp_free']

        G8 = [[0, 1, 2, 3, 4, 5, 6, 7]]
        d_wg = prep_col(wgsh_d, wgin_d, II)
        p.w('gpsimd', 'd', d_wg)
        p.c(lambda: nc.gpsimd.collective_compute(
            "AllGather", mybir.AluOpType.bypass, replica_groups=G8,
            ins=[wgin_d.opt()], outs=[wgT_d.opt()]), key='1')
        d_wu = prep_col(wush_d, wuin_d, II)
        p.w('gpsimd', 'd', d_wu)
        p.c(lambda: nc.gpsimd.collective_compute(
            "AllGather", mybir.AluOpType.bypass, replica_groups=G8,
            ins=[wuin_d.opt()], outs=[wuT_d.opt()]), key='2')
        d_wd = prep_wd()
        p.w('gpsimd', 'd', d_wd)
        p.c(lambda: nc.gpsimd.collective_compute(
            "AllGather", mybir.AluOpType.bypass, replica_groups=G8,
            ins=[wdin_d.opt()], outs=[wdT_d.opt()]), key='3')
        load_weight_cast(wdkv, wdkv_d, D, CL)
        load_weight_cast(wuq, wuq_d, D, HCOL)
        load_weight_cast(wuk, wuk_d, CL, HCOL)
        w_attn_v = load_weight_cast(wuv, wuv_d, CL, HCOL)
        V(lambda: nc.vector.memset(onesc[:], 1.0))
        v_init = V(lambda: nc.vector.memset(vaug[:, 64::65], 1.0))

        m['psb_free_l1'] = m.get('psbT_free', 0)

        m['xntm_free'] = p.cnt['t']
        m['xnfm_guard_d'] = m.get('ttmp_free', 0)
        p.w('tensor', 'v', w_attn_v)
        p.w('tensor', 'v', v_init)
        NB = 256
        for batch in range(32):
            tok0 = batch * NB
            dl = []
            for tt in range(2):
                p.w('sync', 'v', m.get(f'xtile_free{tt}', 0))
                p.w('sync', 's', m.get(f'xtile_free_s{tt}', 0))
                p.w('sync', 'v', m.get('stage_free', 0))
                dl.append(Dq(lambda tt=tt, tok0=tok0: sec.dma_start(
                    out=xtile[:, KB * tt:KB * tt + KB],
                    in_=x_d[tok0 + 128 * tt:tok0 + 128 * tt + 128, :])))
            p.w('vector', 'v', m.get('stat_v_free', 0))
            p.w('scalar', 'v', m.get('stat_v_free', 0))
            s_last = 0
            for tt in range(2):
                p.w('vector', 'd', dl[tt])
                p.w('scalar', 'd', dl[tt])
                V(lambda tt=tt: nc.vector.tensor_reduce(
                    stat[:, tt:tt + 1], xtile[:, KB * tt:KB * tt + KB], axis=AX, op=Op.add))
                s_last = S(lambda tt=tt: nc.scalar.activation(
                    sqscr[:, :], xtile[:, KB * tt:KB * tt + KB], AF.Square,
                    accum_out=stat[:, 8 + tt:8 + tt + 1]))
                m[f'xtile_free_s{tt}'] = s_last
            p.w('vector', 's', s_last)
            p.w('vector', 'v', p.cnt['v'])
            V(lambda: nc.vector.tensor_scalar(stat[:, 0:2], stat[:, 0:2], 1.0 / D, None, op0=Op.mult))
            p.w('vector', 'v', p.cnt['v'])
            V(lambda: nc.vector.tensor_scalar(stat[:, 8:10], stat[:, 8:10], 1.0 / D, None, op0=Op.mult))
            p.w('vector', 'v', p.cnt['v'])
            V(lambda: nc.vector.tensor_tensor(stat[:, 16:18], stat[:, 0:2], stat[:, 0:2], op=Op.mult))
            p.w('vector', 'v', p.cnt['v'])
            vvar = V(lambda: nc.vector.tensor_tensor(stat[:, 16:18], stat[:, 8:10], stat[:, 16:18], op=Op.subtract))
            p.w('scalar', 'v', vvar)
            svar = S(lambda: nc.scalar.activation(stat[:, 20:22], stat[:, 16:18], AF.Sqrt, bias=LN_EPS))
            p.w('vector', 's', svar)
            vrs = V(lambda: nc.vector.reciprocal(stat[:, 24:26], stat[:, 20:22]))
            p.w('vector', 'v', vrs)
            p.w('vector', 't', m.get('xntm_free', 0))
            for tt in range(2):
                va = V(lambda tt=tt: nc.vector.tensor_scalar(
                    xntm[:, KB * tt:KB * tt + KB], xtile[:, KB * tt:KB * tt + KB],
                    stat[:, tt:tt + 1], stat[:, 24 + tt:24 + tt + 1],
                    op0=Op.subtract, op1=Op.mult))
                m[f'xtile_free{tt}'] = va
            m['stat_v_free'] = p.cnt['v']
            v_apply = p.cnt['v']
            p.w('tensor', 'v', v_apply)
            for ch in range(8):
                p.w('tensor', 's', m.get(f'pscT_free{ch % 2}', 0))
                if ch % 2 == 1:
                    p.w('tensor', 'v', m.get('pscV_free', 0))
                tl = 0
                for tt in range(2):
                    tl = Tn(lambda ch=ch, tt=tt: nc.tensor.transpose(
                        PSC_b[:, 1024 * (ch % 2) + 128 * tt:1024 * (ch % 2) + 128 * tt + 128],
                        xntm[:, KB * tt + 128 * ch:KB * tt + 128 * ch + 128], identb[:]))
                if ch == 7:
                    m['xntm_free'] = tl
                p.w('scalar', 't', tl)
                p.w('scalar', 't', m.get('xnfm_used', 0))
                p.w('scalar', 'd', m.get('xnfm_guard_d', 0))
                ev = S(lambda ch=ch: nc.scalar.copy(
                    xnfm[:, NB * ch:NB * ch + NB],
                    PSC_b[:, 1024 * (ch % 2):1024 * (ch % 2) + NB]))
                m[f'pscT_free{ch % 2}'] = ev
            s_xnfm = p.cnt['s']
            p.w('tensor', 's', s_xnfm)
            p.w('tensor', 'v', m.get('psa_free_l1', 0))
            for mt in range(2):
                for ch in range(8):
                    Tn(lambda mt=mt, ch=ch: nc.tensor.matmul(
                        PSA[:, 512 * mt:512 * mt + NB],
                        wdkv[:, CL * ch + 128 * mt:CL * ch + 128 * mt + 128],
                        xnfm[:, NB * ch:NB * ch + NB], start=(ch == 0), stop=(ch == 7)))
            t_ckv = p.cnt['t']
            for mt in range(2):
                for ch in range(8):
                    Tn(lambda mt=mt, ch=ch: nc.tensor.matmul(
                        PSA[:, 1024 + 512 * mt:1024 + 512 * mt + NB],
                        wuq[:, HCOL * ch + 128 * mt:HCOL * ch + 128 * mt + 128],
                        xnfm[:, NB * ch:NB * ch + NB], start=(ch == 0), stop=(ch == 7)))
            m['xnfm_used'] = p.cnt['t']
            t_q = p.cnt['t']
            p.w('vector', 't', t_ckv)
            p.w('vector', 't', m.get('ckv_used', 0))
            ev_ckv = 0
            for mt in range(2):
                ev_ckv = V(lambda mt=mt: nc.vector.tensor_copy(
                    ckv[:, NB * mt:NB * mt + NB], PSA[:, 512 * mt:512 * mt + NB]))
            p.w('scalar', 't', t_q)
            for mt in range(2):
                S(lambda mt=mt, tok0=tok0: nc.scalar.copy(
                    qfm[mt][:, tok0:tok0 + NB], PSA[:, 1024 + 512 * mt:1024 + 512 * mt + NB]))
            m['psa_free_l1'] = p.cnt['v']
            m['psa_free_l1_s'] = p.cnt['s']
            p.w('tensor', 'v', ev_ckv)
            p.w('tensor', 'v', m.get('psb_free_l1', 0))
            p.w('tensor', 's', m.get('psb_free_l1_s', 0))
            for mt in range(2):
                for ch in range(2):
                    Tn(lambda mt=mt, ch=ch: nc.tensor.matmul(
                        PSB[:, 512 * mt:512 * mt + NB],
                        wuk[:, HCOL * ch + 128 * mt:HCOL * ch + 128 * mt + 128],
                        ckv[:, NB * ch:NB * ch + NB], start=(ch == 0), stop=(ch == 1)))
            t_k = p.cnt['t']
            for ktl in range(2):
                kt = 2 * batch + ktl
                p.w('tensor', 'v', m.get('pscV_free', 0))
                p.w('tensor', 'v', m.get('pscT_free1', 0))
                for ch in range(2):
                    Tn(lambda ktl=ktl, ch=ch: nc.tensor.matmul(
                        PSC[:, 512:512 + 256],
                        ckv[:, NB * ch + 128 * ktl:NB * ch + 128 * ktl + 128],
                        wuv[:, HCOL * ch:HCOL * ch + HCOL], start=(ch == 0), stop=(ch == 1)))
                t_v = p.cnt['t']
                p.w('vector', 't', t_v)
                vv2 = V(lambda kt=kt: nc.vector.tensor_copy(
                    vaug[:, 260 * kt:260 * kt + 260].rearrange("p (h j) -> p h j", j=65)[:, :, 0:64],
                    PSC[:, 512:512 + 256].rearrange("p (h j) -> p h j", j=64)))
                m['pscV_free'] = vv2
            m['ckv_used'] = p.cnt['t']
            p.w('scalar', 't', t_k)
            for mt in range(2):
                S(lambda mt=mt, tok0=tok0: nc.scalar.copy(
                    kfm[mt][:, tok0:tok0 + NB], PSB[:, 512 * mt:512 * mt + NB]))
            m['psb_free_l1_s'] = p.cnt['s']
        l1_v = p.cnt['v']
        l1_s = p.cnt['s']
        p.w('sync', 't', p.cnt['t'])
        d_masks = Dq(lambda: sec.dma_start(out=masks[:], in_=masks_d))

        p.w('tensor', 'v', l1_v)
        p.w('tensor', 's', l1_s)
        for tt in range(16):
            Dq(lambda tt=tt: sec.dma_start(out=xres[:, KB * tt:KB * tt + KB],
                                           in_=xq_d[128 * tt:128 * tt + 128, :]))
        d_xres = p.cnt['d']
        ccin_d_count = 0
        SC = [PSB[:, 0:512], PSB[:, 512:1024], PSC[:, 0:512], PSC[:, 512:1024]]

        def emit_pv(kt, nkt):
            eb0 = 2048 * (kt % 2)
            diag_r = kt - 4 * (nkt // 4 - 1)
            for h in range(HLOC):
                if diag_r >= 0:
                    p.w('tensor', 'v', m[f'mask_h{h}'])
                else:
                    p.w('tensor', 's', m[f'exp_h{h}'])
                if kt == 0:
                    p.w('tensor', 'v', m.get('norm_v_done', 0))
                Tn(lambda h=h, kt=kt, nkt=nkt, eb0=eb0: nc.tensor.matmul(
                    PSA[0:65, 512 * h:512 * h + 512],
                    vaug[:, 260 * kt + 65 * h:260 * kt + 65 * h + 65],
                    ebuf[:, eb0 + 512 * h:eb0 + 512 * h + 512],
                    start=(kt == 0), stop=(kt == nkt - 1)))
                m[f'pv_h{h}'] = p.cnt['t']
            m[f'pv_p{kt % 2}'] = p.cnt['t']

        for qb in range(NQB):
            nkt = 4 * (qb + 1)
            for kt in range(nkt):
                diag_r = kt - 4 * qb
                for h in range(HLOC):
                    u, rb0 = h // 2, 64 * (h % 2)
                    p.w('tensor', 's', m.get(f'exp_h{h}', 0))
                    if kt == 0:
                        p.w('tensor', 'v', m.get('norm_v_done', 0))
                    Tn(lambda u=u, rb0=rb0, kt=kt, qb=qb, h=h: nc.tensor.matmul(
                        SC[h],
                        kfm[u][rb0:rb0 + 64, 128 * kt:128 * kt + 128],
                        qfm[u][rb0:rb0 + 64, QB * qb:QB * qb + QB],
                        start=True, stop=True))
                    m[f'score_h{h}'] = p.cnt['t']
                eb0 = 2048 * (kt % 2)
                if kt > 0:
                    emit_pv(kt - 1, nkt)
                for grp in range(2):
                    ps_sc = PSB if grp == 0 else PSC
                    p.w('scalar', 't', m[f'score_h{2 * grp + 1}'])
                    p.w('scalar', 't', m.get(f'pv_p{kt % 2}', 0))
                    se = S(lambda grp=grp, eb0=eb0, ps_sc=ps_sc: nc.scalar.activation(
                        ebuf[:, eb0 + 1024 * grp:eb0 + 1024 * grp + 1024], ps_sc[:, 0:1024],
                        AF.Exp, bias=EXP_BIAS, scale=EXP_SCALE))
                    m[f'exp_h{2 * grp}'] = se
                    m[f'exp_h{2 * grp + 1}'] = se
                    if diag_r >= 0:
                        p.w('vector', 's', se)
                        p.w('vector', 'd', d_masks)
                        mv = V(lambda grp=grp, diag_r=diag_r, eb0=eb0: nc.vector.tensor_tensor(
                            ebuf[:, eb0 + 1024 * grp:eb0 + 1024 * grp + 1024].rearrange(
                                "p (h n) -> p h n", n=512),
                            ebuf[:, eb0 + 1024 * grp:eb0 + 1024 * grp + 1024].rearrange(
                                "p (h n) -> p h n", n=512),
                            masks[:, 512 * diag_r:512 * diag_r + 512].rearrange(
                                "p n -> p () n").broadcast_to([128, 2, 512]),
                            op=Op.mult))
                        m[f'mask_h{2 * grp}'] = mv
                        m[f'mask_h{2 * grp + 1}'] = mv
            emit_pv(nkt - 1, nkt)
            t_pv = p.cnt['t']
            p.w('vector', 'd', m.get('ctxn_free', 0))
            p.w('vector', 't', t_pv)
            p.w('vector', 'v', p.cnt['v'])
            vr2 = V(lambda: nc.vector.reciprocal(rl4[:, :], PSA[64:65, 0:2048]))
            p.w('tensor', 'v', vr2)
            tr = 0
            for h in range(HLOC):
                tr = Tn(lambda h=h: nc.tensor.matmul(
                    SC[h][0:64, :], onesc[:], rl4[:, 512 * h:512 * h + 512],
                    start=True, stop=True))
            p.w('vector', 't', tr)
            V(lambda: nc.vector.tensor_copy(rbuf4[:, 0:1024], PSB[0:64, 0:1024]))
            vR = V(lambda: nc.vector.tensor_copy(rbuf4[:, 1024:2048], PSC[0:64, 0:1024]))
            p.w('vector', 'v', vR)
            for rp in range(2):
                V(lambda rp=rp: nc.vector.tensor_tensor(
                    ctxn[64 * rp:64 * rp + 64, 0:1024].rearrange("p (a n) -> p a n", n=512),
                    PSA[0:64, 0:2048].rearrange("p (a b n) -> p a b n", b=2, n=512)[:, :, rp, :],
                    rbuf4[:, 0:2048].rearrange("p (a b n) -> p a b n", b=2, n=512)[:, :, rp, :],
                    op=Op.mult))
            v_ctxn = p.cnt['v']
            m['norm_v_done'] = v_ctxn
            m['psb_guard_v'] = v_ctxn
            p.w('sync', 'v', v_ctxn)
            for u in range(2):
                ccin_d_count = Dq(lambda u=u, qb=qb: sec.dma_start(
                    out=ccin_d[256 * (qb // 4) + 128 * u:256 * (qb // 4) + 128 * u + 128,
                               QB * (qb % 4):QB * (qb % 4) + QB],
                    in_=ctxn[:, QB * u:QB * u + QB]))
            m['ctxn_free'] = ccin_d_count
            if qb % 4 == 3:
                cq = qb // 4
                p.w('gpsimd', 'd', ccin_d_count)
                p.c(lambda cq=cq: nc.gpsimd.collective_compute(
                    "AllGather", mybir.AluOpType.bypass,
                    replica_groups=[[0, 1, 2, 3], [4, 5, 6, 7]],
                    ins=[ccin_d[256 * cq:256 * cq + 256, :].opt()],
                    outs=[ccout_d[1024 * cq:1024 * cq + 1024, :].opt()]), key=str(4 + cq))

        m['psa_free_wo'] = p.cnt['v']
        m['exp_done1'] = p.cnt['s']
        t_quad_end = p.cnt['t']

        p.w('sync', 't', t_quad_end)
        p.w('sync', 's', m.get('exp_done1', 0))
        p.w('sync', 'v', m.get('norm_v_done', 0))
        wo_v = load_weight_cast(wo_s, wo_d, D, KB)
        for k in '4567':
            p.w('gpsimd', k, 1)
        p.w('gpsimd', 'd', d_gidx)

        for tb in range(4):
            cur = tb % 2
            p.w('gpsimd', 't', t_quad_end)
            p.w('gpsimd', 't', m.get(f'ctxg_used{cur}', 0))
            gl = 0
            for ch in range(8):
                gl = G(lambda ch=ch, tb=tb, cur=cur: nc.gpsimd.indirect_dma_start(
                    out=ctxg[cur][:, QB * ch:QB * ch + QB], out_offset=None,
                    in_=ccout_rows,
                    in_offset=bass.IndirectOffsetOnAxis(ap=gidx[:, 4 * ch + tb:4 * ch + tb + 1], axis=0)))
            p.w('tensor', 'g', gl)
            p.w('tensor', 'v', wo_v)
            p.w('tensor', 'v', m.get('psa_free_wo', 0))
            for sub in range(4):
                slot = sub % 2
                p.w('tensor', 'v', m.get(f'wo_res{slot}', 0))
                for ch in range(8):
                    for hf in range(2):
                        Tn(lambda ch=ch, sub=sub, slot=slot, cur=cur, hf=hf: nc.tensor.matmul(
                            PSA[:, 1024 * slot + 512 * hf:1024 * slot + 512 * hf + 512],
                            ctxg[cur][:, QB * ch + 128 * sub:QB * ch + 128 * sub + 128],
                            wo_s[:, KB * ch + 512 * hf:KB * ch + 512 * hf + 512],
                            start=(ch == 0), stop=(ch == 7)))
                t_wo = p.cnt['t']
                m[f'ctxg_used{cur}'] = t_wo
                p.w('vector', 't', t_wo)
                p.w('vector', 'd', d_xres)
                ti = 4 * tb + sub
                va = V(lambda ti=ti, slot=slot: nc.vector.tensor_tensor(
                    xres[:, KB * ti:KB * ti + KB], xres[:, KB * ti:KB * ti + KB],
                    PSA[:, 1024 * slot:1024 * slot + 1024], op=Op.add))
                m[f'wo_res{slot}'] = va
            m[f'wo_tb_done{tb}'] = p.cnt['v']
        wo_res_v = p.cnt['v']
        t_wo_end = p.cnt['t']
        m['hqtm_used'] = t_wo_end
        m['h_guard_t'] = t_wo_end

        y_store = 0
        m['psa_slot0'] = wo_res_v
        m['psa_slot1'] = wo_res_v
        pendB = [None]

        def emit_passB_res(t_dnB, tiles):
            for j, ti in enumerate(tiles):
                p.w('vector', 't', t_dnB)
                va = V(lambda ti=ti, j=j: nc.vector.tensor_tensor(
                    xres[:, KB * ti:KB * ti + KB], xres[:, KB * ti:KB * ti + KB],
                    PSA[:, 1024 * j:1024 * j + 1024], op=Op.add))
                m[f'psa_slot{j}'] = va
                p.w('sync', 'v', va)
                m['y_store'] = Dq(lambda ti=ti: sec.dma_start(
                    out=y_d[128 * ti:128 * ti + 128, :],
                    in_=xres[:, KB * ti:KB * ti + KB]))

        for g in range(4):
            for pr in range(2):
                p.w('vector', 'v', m[f'wo_tb_done{g}'])
                p.w('scalar', 'v', m[f'wo_tb_done{g}'])
                p.w('vector', 'v', m.get('stat_v_free', 0))
                p.w('scalar', 'v', m.get('stat_v_free', 0))
                s_last = 0
                for t2 in range(2):
                    xsl = xres[:, KB * (4 * g + 2 * pr + t2):KB * (4 * g + 2 * pr + t2) + KB]
                    V(lambda xsl=xsl, t2=t2: nc.vector.tensor_reduce(
                        stat[:, t2:t2 + 1], xsl, axis=AX, op=Op.add))
                    s_last = S(lambda xsl=xsl, t2=t2: nc.scalar.activation(
                        sqscr[:, :], xsl, AF.Square, accum_out=stat[:, 8 + t2:8 + t2 + 1]))
                p.w('vector', 's', s_last)
                p.w('vector', 'v', p.cnt['v'])
                V(lambda: nc.vector.tensor_scalar(stat[:, 0:2], stat[:, 0:2], 1.0 / D, None, op0=Op.mult))
                p.w('vector', 'v', p.cnt['v'])
                V(lambda: nc.vector.tensor_scalar(stat[:, 8:10], stat[:, 8:10], 1.0 / D, None, op0=Op.mult))
                p.w('vector', 'v', p.cnt['v'])
                V(lambda: nc.vector.tensor_tensor(stat[:, 16:18], stat[:, 0:2], stat[:, 0:2], op=Op.mult))
                p.w('vector', 'v', p.cnt['v'])
                vv3 = V(lambda: nc.vector.tensor_tensor(stat[:, 16:18], stat[:, 8:10], stat[:, 16:18], op=Op.subtract))
                p.w('scalar', 'v', vv3)
                sv3 = S(lambda: nc.scalar.activation(stat[:, 20:22], stat[:, 16:18], AF.Sqrt, bias=LN_EPS))
                p.w('vector', 's', sv3)
                vr3 = V(lambda: nc.vector.reciprocal(stat[:, 24:26], stat[:, 20:22]))
                p.w('vector', 'v', vr3)
                for t2 in range(2):
                    xsl = xres[:, KB * (4 * g + 2 * pr + t2):KB * (4 * g + 2 * pr + t2) + KB]
                    V(lambda xsl=xsl, t2=t2: nc.vector.tensor_scalar(
                        xq32[:, KB * t2:KB * t2 + KB], xsl,
                        stat[:, t2:t2 + 1], stat[:, 24 + t2:24 + t2 + 1],
                        op0=Op.subtract, op1=Op.mult))
                m['stat_v_free'] = p.cnt['v']
                if pr == 0 and pendB[0] is not None:
                    emit_passB_res(*pendB[0])
                    pendB[0] = None
                for t2 in range(2):
                    tl = 2 * pr + t2
                    qv = quant(xqtm[:, KB * t2:KB * t2 + KB], xq32[:, KB * t2:KB * t2 + KB],
                               KB, qsF, twait=m.get('xqtm_used', 0))
                    par = m.get('pscpar', 0)
                    p.w('tensor', 'v', qv)
                    p.w('tensor', 's', m.get(f'pscT{par}', 0))
                    tl_t = 0
                    for ch in range(8):
                        tl_t = Tn(lambda t2=t2, ch=ch, par=par: nc.tensor.transpose(
                            PSC_b[:, 1024 * par + 128 * ch:1024 * par + 128 * ch + 128],
                            xqtm[:, KB * t2 + 128 * ch:KB * t2 + 128 * ch + 128], identb[:]))
                    m['xqtm_used'] = tl_t
                    p.w('scalar', 't', tl_t)
                    p.w('scalar', 't', m.get('xqfm_used', 0))
                    vxf = S(lambda tl=tl, par=par: nc.scalar.copy(
                        xqfm[:, KB * tl:KB * tl + KB], PSC_b[:, 1024 * par:1024 * par + KB]))
                    m[f'pscT{par}'] = vxf
                    m[f'xqfm_rdy{tl}'] = vxf
                    m['pscpar'] = par ^ 1
            for si in range(6):
                wslice = min(512, II - 512 * si)
                cur = si % 2
                pcw = si // 2
                p.w('sync', '1', 1)
                p.w('sync', 't', m.get(f'wgu_used{cur}', 0))
                p.w('gpsimd', '2', 1)
                p.w('gpsimd', 't', m.get(f'wgu_used{cur}', 0))
                dgu = 0
                ggu = 0
                for wch in range(8):
                    srcg = wgT_d[:, 512 * si:512 * si + wslice]
                    dgu = Dq(lambda srcg=srcg, wch=wch, cur=cur, wslice=wslice: sec.dma_start(
                        out=wgu[2 * cur][:, QB * wch:QB * wch + wslice],
                        in_=srcg[128 * wch:128 * wch + 128, :]))
                    srcu = wuT_d[:, 512 * si:512 * si + wslice]
                    ggu = G(lambda srcu=srcu, wch=wch, cur=cur, wslice=wslice: sec.dma_start(
                        out=wgu[2 * cur + 1][:, QB * wch:QB * wch + wslice],
                        in_=srcu[128 * wch:128 * wch + 128, :]))
                p.w('tensor', 'd', dgu)
                p.w('tensor', 'g', ggu)
                p.w('tensor', 'v', m.get('psb_guard_v', 0))
                for tl in range(4):
                    pp = tl % 2
                    gb = PSB[:, 0:wslice] if pp == 0 else PSA[:, 0:wslice]
                    ub = PSB[:, 512:512 + wslice] if pp == 0 else PSA[:, 512:512 + wslice]
                    p.w('tensor', 's', m[f'xqfm_rdy{tl}'])
                    if pp == 1:
                        p.w('tensor', 'v', m.get('psa_slot0', 0))
                    p.w('tensor', 'v', m.get(f'h_done{pp}', 0))
                    p.w('tensor', 's', m.get(f'silu_done{pp}', 0))
                    for ch in range(8):
                        Tn(lambda tl=tl, ch=ch, cur=cur, wslice=wslice, gb=gb: nc.tensor.matmul(
                            gb,
                            xqfm[:, 1024 * tl + 128 * ch:1024 * tl + 128 * ch + 128],
                            wgu[2 * cur][:, QB * ch:QB * ch + wslice],
                            start=(ch == 0), stop=(ch == 7)))
                    t_g = p.cnt['t']
                    for ch in range(8):
                        Tn(lambda tl=tl, ch=ch, cur=cur, wslice=wslice, ub=ub: nc.tensor.matmul(
                            ub,
                            xqfm[:, 1024 * tl + 128 * ch:1024 * tl + 128 * ch + 128],
                            wgu[2 * cur + 1][:, QB * ch:QB * ch + wslice],
                            start=(ch == 0), stop=(ch == 7)))
                    t_u = p.cnt['t']
                    m[f'wgu_used{cur}'] = t_u
                    m['xqfm_used'] = t_u
                    p.w('scalar', 't', t_g)
                    p.w('scalar', 'v', m.get(f'h_done{pp}', 0))
                    ssl = S(lambda wslice=wslice, pp=pp, gb=gb: nc.scalar.activation(
                        silu_s[:, 512 * pp:512 * pp + wslice], gb, AF.Silu))
                    m[f'silu_done{pp}'] = ssl
                    p.w('vector', 't', t_u)
                    p.w('vector', 't', m.get('h_guard_t', 0))
                    p.w('vector', 's', ssl)
                    hv = V(lambda tl=tl, si=si, wslice=wslice, pp=pp, ub=ub:
                           nc.vector.tensor_tensor(
                        hbuf[:, 1024 * tl + 512 * (si % 2):
                             1024 * tl + 512 * (si % 2) + wslice],
                        ub, silu_s[:, 512 * pp:512 * pp + wslice], op=Op.mult))
                    m[f'h_done{pp}'] = hv
                if si % 2 == 1 or si == 5:
                    pc = si // 2
                    w = 1024 if pc < 2 else 768
                    nch = w // 128
                    c0 = 1024 * pc
                    for tl in range(4):
                        qv = quant(hqtm[:, c0:c0 + w],
                                   hbuf[:, 1024 * tl:1024 * tl + w],
                                   w, qsF, twait=m.get('hqtm_used', 0))
                        par = m.get('pscpar', 0)
                        p.w('tensor', 'v', qv)
                        p.w('tensor', 's', m.get(f'pscT{par}', 0))
                        tl_t = 0
                        for j in range(nch):
                            tl_t = Tn(lambda c0=c0, j=j, par=par: nc.tensor.transpose(
                                PSC_b[:, 1024 * par + 128 * j:1024 * par + 128 * j + 128],
                                hqtm[:, c0 + 128 * j:c0 + 128 * j + 128], identb[:]))
                        m['hqtm_used'] = tl_t
                        p.w('scalar', 't', tl_t)
                        p.w('scalar', 't', m.get('hqfm_used', 0))
                        hv2 = S(lambda tl=tl, pc=pc, nch=nch, par=par: nc.scalar.copy(
                            hqfm_v[:, 8 * pc:8 * pc + nch, 128 * tl:128 * tl + 128],
                            PSC_b[:, 1024 * par:1024 * par + 128 * nch].rearrange("p (c i) -> p c i", i=128)))
                        m[f'pscT{par}'] = hv2
                        m[f'hqfm_rdy_{tl}_{pc}'] = hv2
                        m['pscpar'] = par ^ 1
            for half in range(2):
                for ch in range(22):
                    pc = min(ch // 8, 2)
                    slot = m.get('wdnslot', 0)
                    m['wdnslot'] = (slot + 1) % 6
                    qi = slot % 3
                    qk = ('d', 'g', 'e')[qi]
                    qeng = ('sync', 'gpsimd', 'scalar')[qi]
                    p.w(qeng, '3', 1)
                    p.w(qeng, 't', t_wo_end)
                    p.w(qeng, 't', m.get(f'wdn_used{slot}', 0))
                    fn = (lambda ch=ch, slot=slot: sec.dma_start(
                        out=wdnR[slot][:, :], in_=wdT_d[128 * ch:128 * ch + 128, :]))
                    dwd = {'d': Dq, 'g': G, 'e': p.e}[qk](fn)
                    p.w('tensor', qk, dwd)
                    for j in range(2):
                        tl = 2 * half + j
                        p.w('tensor', 's', m[f'hqfm_rdy_{tl}_{pc}'])
                        if ch == 0:
                            p.w('tensor', 'v', m[f'psa_slot{j}'])
                            p.w('tensor', 'v', m.get('h_done1', 0))
                            p.w('tensor', 's', m.get('silu_done1', 0))
                        for hf in range(2):
                            Tn(lambda ch=ch, j=j, tl=tl, slot=slot, hf=hf: nc.tensor.matmul(
                                PSA[:, 1024 * j + 512 * hf:1024 * j + 512 * hf + 512],
                                hqfm_v[:, ch, 128 * tl:128 * tl + 128],
                                wdnR[slot][:, 512 * hf:512 * hf + 512],
                                start=(ch == 0), stop=(ch == 21)))
                    m[f'wdn_used{slot}'] = p.cnt['t']
                t_dn = p.cnt['t']
                if half == 0:
                    for j in range(2):
                        ti = 4 * g + j
                        p.w('vector', 't', t_dn)
                        va = V(lambda ti=ti, j=j: nc.vector.tensor_tensor(
                            xres[:, KB * ti:KB * ti + KB], xres[:, KB * ti:KB * ti + KB],
                            PSA[:, 1024 * j:1024 * j + 1024], op=Op.add))
                        m[f'psa_slot{j}'] = va
                        p.w('sync', 'v', va)
                        m['y_store'] = Dq(lambda ti=ti: sec.dma_start(
                            out=y_d[128 * ti:128 * ti + 128, :],
                            in_=xres[:, KB * ti:KB * ti + KB]))
            m['hqfm_used'] = p.cnt['t']
            pendB[0] = (p.cnt['t'], [4 * g + 2, 4 * g + 3])
        emit_passB_res(*pendB[0])
        pendB[0] = None
        y_store = m['y_store']

        p.w('sync', 'd', y_store)

    with nc.Block() as block:
        @block.sync
        def _(sec):
            program(P(nc, 'sync', sec, sems), sec)

        @block.vector
        def _(sec):
            program(P(nc, 'vector', sec, sems), sec)

        @block.scalar
        def _(sec):
            program(P(nc, 'scalar', sec, sems), sec)

        @block.tensor
        def _(sec):
            program(P(nc, 'tensor', sec, sems), sec)

        @block.gpsimd
        def _(sec):
            program(P(nc, 'gpsimd', sec, sems), sec)

    return nc


_NC_CACHE = None


def _consts():
    mm = np.zeros((128, 4, 512), np.float32)
    kk = np.arange(128)[:, None]
    qq = np.arange(512)[None, :]
    for r in range(4):
        mm[:, r, :] = (qq >= kk + 128 * r)
    masks = np.ascontiguousarray(mm.reshape(128, 2048)).astype(ml_dtypes.bfloat16)
    identb = np.eye(128, dtype=ml_dtypes.bfloat16)
    cvals = np.tile(np.array([[EXP_SCALE, EXP_BIAS, LN_EPS, 0.0]], np.float32), (128, 1))
    return identb, masks, cvals


def kernel(L0_attn_x, L0_attn_W_DKV, L0_attn_W_UK, L0_attn_W_UV, L0_attn_W_UQ,
           L0_attn_W_O, L0_ffn_w_gate, L0_ffn_w_up, L0_ffn_w_down):
    global _NC_CACHE
    x = np.ascontiguousarray(np.asarray(L0_attn_x, np.float32))
    W_DKV = np.ascontiguousarray(np.asarray(L0_attn_W_DKV, np.float32))
    W_UK = np.ascontiguousarray(np.asarray(L0_attn_W_UK, np.float32))
    W_UV = np.ascontiguousarray(np.asarray(L0_attn_W_UV, np.float32))
    W_UQ = np.ascontiguousarray(np.asarray(L0_attn_W_UQ, np.float32))
    W_O = np.ascontiguousarray(np.asarray(L0_attn_W_O, np.float32))
    wg = np.ascontiguousarray(np.asarray(L0_ffn_w_gate, np.float32))
    wu = np.ascontiguousarray(np.asarray(L0_ffn_w_up, np.float32))
    wd = np.ascontiguousarray(np.asarray(L0_ffn_w_down, np.float32))

    identb, masks, cvals = _consts()
    if _NC_CACHE is None:
        _NC_CACHE = build_nc()
    nc = _NC_CACHE

    in_maps = []
    pidx = np.arange(128, dtype=np.uint32)
    for c in range(8):
        b, hq = c // 4, c % 4
        hs = slice(HCOL * hq, HCOL * hq + HCOL)
        gidx = np.zeros((128, 64), np.uint32)
        for ch in range(8):
            for tb in range(4):
                gidx[:, 4 * ch + tb] = 4096 * hq + (128 * ch + pidx) * 4 + tb
        wdsh = np.zeros((1024, 384), np.float32)
        lo = 384 * c
        hi = min(lo + 384, II)
        wdsh[:, 0:hi - lo] = wd[:, lo:hi]
        in_maps.append({
            "x": x[b],
            "xquarter": np.ascontiguousarray(x[b, 2048 * hq:2048 * hq + 2048]),
            "wdkv": W_DKV,
            "wuq": np.ascontiguousarray(W_UQ[:, hs]),
            "wuk": np.ascontiguousarray(W_UK[:, hs]),
            "wuv": np.ascontiguousarray(W_UV[:, hs]),
            "wo": W_O,
            "wgsh": np.ascontiguousarray(wg[:, 128 * c:128 * c + 128]),
            "wush": np.ascontiguousarray(wu[:, 128 * c:128 * c + 128]),
            "wdsh": wdsh,
            "identb": identb, "masks": masks, "cvals": cvals,
            "gidx": gidx,
        })

    res = run_bass_kernel_spmd(nc, in_maps, core_ids=list(range(8)))
    out = np.empty((2, T, D), np.float32)
    for c in range(8):
        b, hq = c // 4, c % 4
        out[b, 2048 * hq:2048 * hq + 2048] = np.asarray(res.results[c]["y"])
    return out
